# version 19
# baseline (speedup 1.0000x reference)
import contextlib
import numpy as np
import ml_dtypes
import concourse.bass as bass
import concourse.mybir as mybir
from concourse.bass_utils import run_bass_kernel_spmd

F32 = mybir.dt.float32
F32R = mybir.dt.float32r
BF16 = mybir.dt.bfloat16
F16 = mybir.dt.float16
FP8 = mybir.dt.float8e4
AF = mybir.ActivationFunctionType
ALU = mybir.AluOpType

T = 2048
TC = 512
NT = T // TC
C = 1024
NC = C // 128
DI = 2048
NDI = DI // 128
H = 4096
NH = H // 128
NST = 16

CVALS = [1.0, -1.0, -0.5, 1e-5, 1e-6, 1e-8, 20.0, -20.0, 0.0]
CIX = {v: i for i, v in enumerate(CVALS)}

SEM_ENG = {"T": "tensor", "V": "vector", "S": "scalar", "G": "gpsimd"}
ENG_SEM = {"tensor": "T", "vector": "V", "scalar": "S", "gpsimd": "G"}


class Buf:
    __slots__ = ("h", "w", "r")

    def __init__(self, h):
        self.h = h
        self.w = None
        self.r = []


class Ctx:
    def __init__(self, nc):
        self.nc = nc
        self.es = contextlib.ExitStack()
        self.block = None
        self.sem = {}
        self.cnt = {}
        for s in ("T", "V", "S", "G", "MS", "dW", "dL", "dS"):
            self.sem[s] = self.es.enter_context(nc.semaphore(s))
            self.cnt[s] = 0
        self.waited = {e: {} for e in ("tensor", "vector", "scalar", "gpsimd", "sync")}
        self.pend_by_eng = {}

    def sb(self, name, shape, dt, scope=None):
        cm = self.nc.sbuf_tensor(name, shape, dt)
        h = scope.enter_context(cm) if scope is not None else self.es.enter_context(cm)
        return Buf(h)

    def _flush(self):
        for eng in ("sync", "tensor", "vector", "scalar", "gpsimd"):
            items = self.pend_by_eng.get(eng)
            if not items:
                continue
            self.pend_by_eng[eng] = []

            def body(p, items=items):
                for it in items:
                    it(p)
            getattr(self.block, eng)(body)

    def _q(self, eng, fn):
        self.pend_by_eng.setdefault(eng, []).append(fn)

    def _wait_emit(self, eng, p, toks, skip_same=True):
        best = {}
        for t in toks:
            if t is None:
                continue
            s, v, short = t
            if skip_same and not short and SEM_ENG.get(s) in (eng,):
                continue
            if v > best.get(s, 0):
                best[s] = v
        for s, v in best.items():
            if v > self.waited[eng].get(s, 0):
                p.wait_ge(self.sem[s], v)
                self.waited[eng][s] = v

    def op(self, eng, fn, reads=(), writes=(), extra=(), dma=None, memset=False,
           short=False):
        toks = []
        for b in reads:
            toks.append(b.w)
        for b in writes:
            toks.append(b.w)
            toks.extend(b.r)
        toks.extend(extra)
        toks = [(s, self.cnt[s] if s in ("MS", "dW", "dL", "dS") else v, sh)
                for (s, v, sh) in toks if toks is not None
                for (s, v, sh) in ((s, v, sh),)] if False else [
            ((t[0], self.cnt[t[0]], t[2]) if t is not None and t[0] in ("MS", "dW", "dL", "dS")
             else t) for t in toks]
        sem = dma if dma else ("MS" if memset else ENG_SEM[eng])
        self.cnt[sem] += 16 if dma else 1
        tok = (sem, self.cnt[sem], bool(short))
        skip_same = not (dma or memset)

        def run(p, fn=fn, toks=tuple(toks), sem=sem, tok=tok, eng=eng, sk=skip_same):
            self._wait_emit(eng, p, toks, skip_same=sk)
            ins = fn(p)
            ins.then_inc(self.sem[sem], 16 if sem in ("dW", "dL", "dS") else 1)
        self._q(eng, run)
        for b in reads:
            b.r.append(tok)
        for b in writes:
            b.w = tok
            b.r = []
        return tok

    def raw(self, eng, fn):
        self._q(eng, fn)

    def tok(self, s):
        return (s, self.cnt[s], False)

    def barrier(self):
        self._flush()
        snap = tuple((s, self.cnt[s], False)
                     for s in ("T", "V", "S", "G", "MS", "dW", "dL", "dS"))
        for e in ("tensor", "vector", "scalar", "gpsimd"):
            def f(p, e=e):
                self._wait_emit(e, p, snap)
            getattr(self.block, e)(f)


class PsumPool:
    def __init__(self, ctx):
        self.banks = []
        for i in range(8):
            h = ctx.es.enter_context(ctx.nc.psum_tensor(f"pb{i}", [128, TC], F32))
            self.banks.append(Buf(h))
        self.idx = 0

    def get(self, n=1):
        out = []
        for _ in range(n):
            out.append(self.banks[self.idx])
            self.idx = (self.idx + 1) % 8
        return out if n > 1 else out[0]


class _StopBuild(Exception):
    pass


def build(debug=False, stop=None):
    nc = bass.Bass("TRN2", target_bir_lowering=False)
    X = Ctx(nc)
    pool = PsumPool(X)

    def dram_in(name, shape, dt=F32):
        return nc.dram_tensor(name, shape, dt, kind="ExternalInput").ap()

    def scratch(name, shape, dt):
        kind = "ExternalOutput" if debug else "Internal"
        return nc.dram_tensor(name, shape, dt, kind=kind).ap()

    xT = dram_in("xT", [C, T])
    Wk0 = dram_in("Wk0", [C, C]); Wv0 = dram_in("Wv0", [C, C]); Wr0 = dram_in("Wr0", [C, C])
    Wo0p = dram_in("Wo0p", [NC, 128, C])
    Wk2p = dram_in("Wk2p", [NH, 128, C])
    Wr2p = dram_in("Wr2p", [NC, 128, C])
    Wv2p = dram_in("Wv2p", [NC, 128, H])
    w1p = dram_in("w1p", [NH, 128, C])
    w2p = dram_in("w2p", [NC, 128, H])
    Winp = dram_in("Winp", [2 * NDI, 128, C])
    WxTi = dram_in("WxTi", [128, NDI * 32])
    Wdtp = dram_in("Wdtp", [NDI, 128, DI])
    Woutp = dram_in("Woutp", [NC, 128, DI])
    f1p = dram_in("f1p", [NH, 128, C])
    f2p = dram_in("f2p", [NH, 128, C])
    h1p = dram_in("h1p", [NH, 128, H])
    h2p = dram_in("h2p", [NH, 128, H])
    w3p = dram_in("w3p", [NC, 128, H])
    VSPEC = [("ln1w", NC), ("ln2w", NC), ("td", NC), ("tf", NC), ("tmk", NC),
             ("tmv", NC), ("tmr", NC), ("tmk2", NC), ("tmr2", NC), ("b1", NH),
             ("b2", NC), ("cb", NDI), ("bdt", NDI), ("dskip", NDI), ("lng", NC),
             ("lnb", NC), ("alog", NDI * NST), ("cw", NDI * 4), ("f1s", NH),
             ("f2s", NH), ("h1s", NH), ("h2s", NH), ("w3s", NC), ("cvec", len(CVALS))]
    vin = {nm: dram_in("v_" + nm, [128, w]) for nm, w in VSPEC}
    ones_col = dram_in("ones_col", [128, 1], BF16)
    ones_row = dram_in("ones_row", [1, 128])
    yT = nc.dram_tensor("yT", [C, T], F32, kind="ExternalOutput").ap()

    d_xn = scratch("d_xn", [C, T], F32)
    d_astr = scratch("d_astr", [C, T], F32)
    d_xnbf = scratch("d_xnbf", [C, T], BF16)
    d_x1 = scratch("d_x1", [C, T], F32)
    d_xn1 = scratch("d_xn1", [C, T], BF16)
    d_k2 = scratch("d_k2", [H, T], BF16)
    d_sres = scratch("d_sres", [DI, T], BF16)
    d_xi = scratch("d_xi", [DI, T], BF16)
    d_delta = scratch("d_delta", [DI, T], F32)
    d_ybf = scratch("d_ybf", [DI, T], BF16)
    d_mamba = scratch("d_mamba", [C, T], F32)
    d_bmat = scratch("d_bmat", [32, T], F32)
    d_hq8 = scratch("d_hq8", [H, T], FP8) if debug else None
    d_wq8 = scratch("d_wq8", [C, H], FP8) if debug else None
    d_ff16 = scratch("d_ff16", [C, T], F16) if debug else None
    d_cbq = scratch("d_cbq", [128, NC], F32) if debug else None
    d_cbq16 = scratch("d_cbq16", [128, NC], F16) if debug else None
    d_ffn0 = scratch("d_ffn0", [C, T], F32) if debug else None
    d_h = scratch("d_h", [H, T], BF16)
    d_g1 = scratch("d_g1", [H, T], BF16)
    d_g2 = scratch("d_g2", [H, T], BF16)

    cons = {nm: X.sb("c_" + nm, [128, w], F32) for nm, w in VSPEC}
    c_ones = X.sb("c_ones", [128, 1], BF16)
    c_onesrow = X.sb("c_onesrow", [1, 128], F32)
    c_onesrow_r = X.sb("c_onesrow_r", [1, 128], F32R)
    c_ew = X.sb("c_ew", [128, NC], F32)
    c_A = X.sb("c_A", [128, NDI * NST], F32)
    c_bq = X.sb("c_bq", [128, NC], F32)
    c_bq16 = X.sb("c_bq16", [128, NC], F16)
    ring = [X.sb(f"ring{i}", [128, 2048], F32) for i in range(2)]
    ring_ix = [0]
    cast_rr = [0]
    st_s1 = X.sb("st_s1", [1, TC], F32)
    st_s2 = X.sb("st_s2", [1, TC], F32)
    st_m = X.sb("st_m", [1, TC], F32)
    st_inv = X.sb("st_inv", [1, TC], F32)
    st_r1 = X.sb("st_r1", [1, TC], F32R)
    st_r2 = X.sb("st_r2", [1, TC], F32R)
    bc_m = X.sb("bc_m", [128, TC], F32)
    bc_inv = X.sb("bc_inv", [128, TC], F32)
    c_Ast = X.sb("c_Ast", [128, NC], F32)
    c_Bst = X.sb("c_Bst", [128, NC], F32)

    def cv(val, np_=128):
        return cons["cvec"].h[0:np_, CIX[val]:CIX[val] + 1]

    RW = 2048

    CAST_ENGS = ("gpsimd",)

    def load_parcel(src_ap, dst_buf, K, cast_eng=None):
        for c0 in range(0, K, RW):
            c1 = min(c0 + RW, K)
            slot = ring[ring_ix[0]]
            ring_ix[0] = (ring_ix[0] + 1) % len(ring)
            X.op("sync", lambda p, s=slot, a=src_ap, c0=c0, c1=c1: p.dma_start(
                 out=s.h[:, 0:c1 - c0], in_=a[:, c0:c1]),
                 reads=(), writes=(slot,), dma="dW")
            eng = cast_eng or CAST_ENGS[cast_rr[0] % len(CAST_ENGS)]
            cast_rr[0] += 1
            if eng == "scalar":
                X.op(eng, lambda p, s=slot, d=dst_buf, c0=c0, c1=c1: p.copy(
                     out=d.h[:, c0:c1], in_=s.h[:, 0:c1 - c0]),
                     reads=(slot,), writes=(dst_buf,))
            else:
                X.op(eng, lambda p, s=slot, d=dst_buf, c0=c0, c1=c1: p.tensor_copy(
                     out=d.h[:, c0:c1], in_=s.h[:, 0:c1 - c0]),
                     reads=(slot,), writes=(dst_buf,))

    def stats_tc(sq_tiles, val_tiles, nchunks, eps, want_mean, tcol):
        n = 128 * nchunks
        for which, tiles, dst in ((0, val_tiles, st_s1), (1, sq_tiles, st_s2)):
            if which == 0 and not want_mean:
                continue
            bank = pool.get()
            def mm(p, tiles=tiles, bank=bank):
                for k in range(nchunks):
                    ins = p.matmul(out=bank.h[0:1, :], lhsT=c_ones.h[:], rhs=tiles[k].h[:],
                                   start=(k == 0), stop=(k == nchunks - 1))
                return ins
            X.op("tensor", mm, reads=[c_ones] + list(tiles), writes=(bank,))
            X.op("scalar", lambda p, bank=bank, dst=dst: p.copy(out=dst.h[0:1, :],
                                                                in_=bank.h[0:1, :]),
                 reads=(bank,), writes=(dst,), short=True)
        if want_mean:
            X.op("vector", lambda p: p.tensor_scalar(out=st_m.h[0:1, :], in0=st_s1.h[0:1, :],
                                                     scalar1=1.0 / n, scalar2=None,
                                                     op0=ALU.mult),
                 reads=(st_s1,), writes=(st_m,), short=True)
            X.op("vector", lambda p: p.tensor_tensor(out=st_inv.h[0:1, :], in0=st_m.h[0:1, :],
                                                     in1=st_m.h[0:1, :], op=ALU.mult),
                 reads=(st_m,), writes=(st_inv,), short=True)
            X.op("vector", lambda p: p.tensor_scalar(out=st_s2.h[0:1, :], in0=st_s2.h[0:1, :],
                                                     scalar1=1.0 / n, scalar2=None,
                                                     op0=ALU.mult),
                 reads=(st_s2,), writes=(st_s2,), short=True)
            X.op("vector", lambda p: p.tensor_tensor(out=st_s2.h[0:1, :], in0=st_s2.h[0:1, :],
                                                     in1=st_inv.h[0:1, :], op=ALU.subtract),
                 reads=(st_s2, st_inv), writes=(st_s2,), short=True)
        else:
            X.op("vector", lambda p: p.tensor_scalar(out=st_s2.h[0:1, :], in0=st_s2.h[0:1, :],
                                                     scalar1=1.0 / n, scalar2=None,
                                                     op0=ALU.mult),
                 reads=(st_s2,), writes=(st_s2,), short=True)
        X.op("vector", lambda p, eps=eps: p.tensor_scalar(out=st_s2.h[0:1, :],
                                                          in0=st_s2.h[0:1, :], scalar1=eps,
                                                          scalar2=None, op0=ALU.add),
             reads=(st_s2,), writes=(st_s2,), short=True)
        X.op("scalar", lambda p: p.activation(out=st_inv.h[0:1, :], in_=st_s2.h[0:1, :],
                                              func=AF.Ln),
             reads=(st_s2,), writes=(st_inv,), short=True)
        X.op("scalar", lambda p: p.activation(out=st_inv.h[0:1, :], in_=st_inv.h[0:1, :],
                                              func=AF.Exp, scale=cv(-0.5, 1)),
             reads=(st_inv, cons["cvec"]), writes=(st_inv,), short=True)
        X.op("vector", lambda p: p.tensor_copy(out=st_r2.h[0:1, :], in_=st_inv.h[0:1, :]),
             reads=(st_inv,), writes=(st_r2,), short=True)
        if want_mean:
            X.op("vector", lambda p: p.tensor_copy(out=st_r1.h[0:1, :], in_=st_m.h[0:1, :]),
                 reads=(st_m,), writes=(st_r1,), short=True)
        for row, dstbc, en in ((st_r2, bc_inv, True), (st_r1, bc_m, want_mean)):
            if not en:
                continue
            bank = pool.get()
            X.op("tensor", lambda p, row=row, bank=bank: p.matmul(out=bank.h[:],
                 lhsT=c_onesrow_r.h[:], rhs=row.h[0:1, :], start=True, stop=True),
                 reads=(row, c_onesrow_r), writes=(bank,))
            X.op("scalar", lambda p, bank=bank, d=dstbc: p.copy(out=d.h[:], in_=bank.h[:]),
                 reads=(bank,), writes=(dstbc,))

    with X.es, nc.Block() as block:
        X.block = block
        def ckpt(n):
            if stop is not None and n >= stop:
                raise _StopBuild()
        try:
            for nm, w in VSPEC:
                X.op("sync", lambda p, nm=nm, w=w: p.dma_start(out=cons[nm].h[:, 0:w],
                                                               in_=vin[nm]),
                     writes=(cons[nm],), dma="dW")
            X.op("sync", lambda p: p.dma_start(out=c_ones.h[:], in_=ones_col),
                 writes=(c_ones,), dma="dW")
            X.op("sync", lambda p: p.dma_start(out=c_onesrow.h[:], in_=ones_row),
                 writes=(c_onesrow,), dma="dW")
            X.op("vector", lambda p: p.tensor_copy(out=c_onesrow_r.h[:], in_=c_onesrow.h[:]),
                 reads=(c_onesrow,), writes=(c_onesrow_r,), short=True)
            X.op("vector", lambda p: p.tensor_copy(out=c_bq16.h[:], in_=cons["b2"].h[:, 0:NC]),
                 reads=(cons["b2"],), writes=(c_bq16,), short=True)
            X.op("vector", lambda p: p.tensor_copy(out=c_bq.h[:], in_=c_bq16.h[:]),
                 reads=(c_bq16,), writes=(c_bq,), short=True)
            X.op("vector", lambda p: p.memset(c_Ast.h[:], 0.0), writes=(c_Ast,), memset=True)
            X.op("vector", lambda p: p.memset(c_Bst.h[:], 0.0), writes=(c_Bst,), memset=True)
            if debug:
                X.op("scalar", lambda p: p.dma_start(out=d_cbq, in_=c_bq.h[:]),
                     reads=(c_bq,), dma="dS")
                X.op("scalar", lambda p: p.dma_start(out=d_cbq16, in_=c_bq16.h[:]),
                     reads=(c_bq16,), dma="dS")
            X.op("scalar", lambda p: p.activation(out=c_ew.h[:], in_=cons["td"].h[:, 0:NC],
                                                  func=AF.Exp),
                 reads=(cons["td"],), writes=(c_ew,), short=True)
            X.op("scalar", lambda p: p.activation(out=c_A.h[:], in_=cons["alog"].h[:],
                                                  func=AF.Exp),
                 reads=(cons["alog"],), writes=(c_A,), short=True)
            X.op("vector", lambda p: p.tensor_scalar(out=c_ew.h[:], in0=c_ew.h[:], scalar1=-1.0,
                                                     scalar2=-20.0, op0=ALU.mult, op1=ALU.max),
                 reads=(c_ew,), writes=(c_ew,), short=True)
            X.op("vector", lambda p: p.tensor_scalar(out=c_A.h[:], in0=c_A.h[:], scalar1=-1.0,
                                                     scalar2=None, op0=ALU.mult),
                 reads=(c_A,), writes=(c_A,), short=True)
            X.op("scalar", lambda p: p.activation(out=c_ew.h[:], in_=c_ew.h[:], func=AF.Exp),
                 reads=(c_ew,), writes=(c_ew,), short=True)
            X.barrier()

            ckpt(0)
            ph0 = contextlib.ExitStack()
            with ph0:
                x0t = [X.sb(f"p0x{k}", [128, T], F32, ph0) for k in range(NC)]
                vbf = [X.sb(f"p0vb{k}", [128, TC], BF16, ph0) for k in range(NC)]
                sbf = [X.sb(f"p0sb{k}", [128, TC], BF16, ph0) for k in range(NC)]
                xnc = [X.sb(f"p0c{i}", [128, TC], F32, ph0) for i in range(2)]
                xnb = [X.sb(f"p0b{i}", [128, TC], BF16, ph0) for i in range(2)]
                for k in range(NC):
                    X.op("sync", lambda p, k=k: p.dma_start(out=x0t[k].h[:],
                                                            in_=xT[k * 128:(k + 1) * 128, :]),
                         writes=(x0t[k],), dma="dW")
                for tau in range(NT):
                    t0 = tau * TC
                    for k in range(NC):
                        X.op("vector", lambda p, k=k, t0=t0: p.tensor_copy(out=vbf[k].h[:],
                             in_=x0t[k].h[:, t0:t0 + TC]), reads=(x0t[k],), writes=(vbf[k],))
                        X.op("scalar", lambda p, k=k, t0=t0: p.activation(out=sbf[k].h[:],
                             in_=x0t[k].h[:, t0:t0 + TC], func=AF.Square),
                             reads=(x0t[k],), writes=(sbf[k],))
                    stats_tc(sbf, vbf, NC, 1e-5, True, t0)
                    for k in range(NC):
                        stg = xnc[k % 2]; stgb = xnb[k % 2]
                        X.op("vector", lambda p, k=k, t0=t0, stg=stg: p.tensor_tensor(
                             out=stg.h[:], in0=x0t[k].h[:, t0:t0 + TC],
                             in1=bc_m.h[:], op=ALU.subtract),
                             reads=(x0t[k], bc_m), writes=(stg,))
                        X.op("vector", lambda p, stg=stg: p.tensor_tensor(
                             out=stg.h[:], in0=stg.h[:], in1=bc_inv.h[:], op=ALU.mult),
                             reads=(stg, bc_inv), writes=(stg,))
                        X.op("vector", lambda p, stg=stg, stgb=stgb: p.tensor_copy(
                             out=stgb.h[:], in_=stg.h[:]), reads=(stg,), writes=(stgb,))
                        X.op("scalar", lambda p, k=k, t0=t0, stg=stg: p.dma_start(
                             out=d_xn[k * 128:(k + 1) * 128, t0:t0 + TC], in_=stg.h[:]),
                             reads=(stg,), dma="dS")
                        X.op("scalar", lambda p, k=k, t0=t0, stgb=stgb: p.dma_start(
                             out=d_xnbf[k * 128:(k + 1) * 128, t0:t0 + TC], in_=stgb.h[:]),
                             reads=(stgb,), dma="dS")
            X.barrier()

            ckpt(1)
            ph1 = contextlib.ExitStack()
            with ph1:
                wk = [X.sb(f"p1wk{k}", [128, C], BF16, ph1) for k in range(NC)]
                wv = [X.sb(f"p1wv{k}", [128, C], BF16, ph1) for k in range(NC)]
                wr = [X.sb(f"p1wr{k}", [128, C], BF16, ph1) for k in range(NC)]
                wo_t = [X.sb(f"p1wo{i}", [128, C], BF16, ph1) for i in range(2)]
                xk = [X.sb(f"p1xk{k}", [128, TC], BF16, ph1) for k in range(NC)]
                xv = [X.sb(f"p1xv{k}", [128, TC], BF16, ph1) for k in range(NC)]
                xr = [X.sb(f"p1xr{k}", [128, TC], BF16, ph1) for k in range(NC)]
                tdif = X.sb("p1td", [128, TC], F32, ph1)
                kc = X.sb("p1kc", [128, TC], F32, ph1)
                vc = X.sb("p1vc", [128, TC], F32, ph1)
                tek = X.sb("p1tek", [128, TC], F32, ph1)
                tekt = X.sb("p1tekt", [128, TC], F32, ph1)
                ekv = X.sb("p1ekv", [128, TC], F32, ph1)
                Ab = X.sb("p1Ab", [128, TC + 1], F32, ph1)
                Bb = X.sb("p1Bb", [128, TC + 1], F32, ph1)
                tnum = X.sb("p1num", [128, TC], F32, ph1)
                tden = X.sb("p1den", [128, TC], F32, ph1)
                rc_t = X.sb("p1rc", [128, TC], F32, ph1)
                rw = [X.sb(f"p1rw{k}", [128, TC], BF16, ph1) for k in range(NC)]
                a1st = [X.sb(f"p1a1{i}", [128, TC], F32, ph1) for i in range(2)]
                xnw = [X.sb(f"p1xw{k}", [128, TC + 1], F32, ph1) for k in range(NC)]

                for src, dsts in ((Wk0, wk), (Wv0, wv), (Wr0, wr)):
                    for k in range(NC):
                        load_parcel(src[k * 128:(k + 1) * 128, :], dsts[k], C)

                for tau in range(NT):
                    t0 = tau * TC
                    for k in range(NC):
                        if tau == 0:
                            X.op("vector", lambda p, k=k: p.memset(xnw[k].h[:, 0:1], 0.0),
                                 writes=(xnw[k],), memset=True)
                            X.op("scalar", lambda p, k=k: p.dma_start(out=xnw[k].h[:, 1:TC + 1],
                                 in_=d_xn[k * 128:(k + 1) * 128, 0:TC]),
                                 writes=(xnw[k],), dma="dL")
                        else:
                            X.op("scalar", lambda p, k=k, t0=t0: p.dma_start(out=xnw[k].h[:],
                                 in_=d_xn[k * 128:(k + 1) * 128, t0 - 1:t0 + TC]),
                                 writes=(xnw[k],), dma="dL")
                    for k in range(NC):
                        X.op("vector", lambda p, k=k: p.tensor_tensor(out=tdif.h[:],
                             in0=xnw[k].h[:, 1:TC + 1],
                             in1=xnw[k].h[:, 0:TC], op=ALU.subtract),
                             reads=(xnw[k],), writes=(tdif,))
                        for tmv_, dst in (("tmk", xk), ("tmv", xv), ("tmr", xr)):
                            X.op("vector", lambda p, k=k, tmv_=tmv_, dst=dst:
                                 p.scalar_tensor_tensor(out=dst[k].h[:], in0=tdif.h[:],
                                                        scalar=cons[tmv_].h[:, k:k + 1],
                                                        in1=xnw[k].h[:, 0:TC],
                                                        op0=ALU.mult, op1=ALU.add),
                                 reads=(tdif, cons[tmv_], xnw[k]), writes=(dst[k],))
                    for m in range(NC):
                        bks = {}
                        for nm, wt, xt in (("k", wk, xk), ("v", wv, xv), ("r", wr, xr)):
                            bank = pool.get()
                            def mmf(p, wt=wt, xt=xt, bank=bank, m=m):
                                for kk in range(NC):
                                    ins = p.matmul(out=bank.h[:],
                                                   lhsT=wt[kk].h[:, m * 128:(m + 1) * 128],
                                                   rhs=xt[kk].h[:],
                                                   start=(kk == 0), stop=(kk == NC - 1))
                                return ins
                            X.op("tensor", mmf, reads=list(wt) + list(xt), writes=(bank,))
                            bks[nm] = bank
                        X.op("scalar", lambda p, b=bks["k"]: p.copy(out=kc.h[:], in_=b.h[:]),
                             reads=(bks["k"],), writes=(kc,))
                        X.op("scalar", lambda p, b=bks["v"]: p.copy(out=vc.h[:], in_=b.h[:]),
                             reads=(bks["v"],), writes=(vc,))
                        X.op("vector", lambda p: p.tensor_scalar(out=tek.h[:], in0=kc.h[:],
                             scalar1=-20.0, scalar2=20.0, op0=ALU.max, op1=ALU.min),
                             reads=(kc,), writes=(tek,))
                        X.op("vector", lambda p, m=m: p.tensor_scalar(out=tekt.h[:], in0=kc.h[:],
                             scalar1=cons["tf"].h[:, m:m + 1], scalar2=-20.0,
                             op0=ALU.add, op1=ALU.max),
                             reads=(kc, cons["tf"]), writes=(tekt,))
                        X.op("vector", lambda p: p.tensor_scalar(out=tekt.h[:], in0=tekt.h[:],
                             scalar1=20.0, scalar2=None, op0=ALU.min),
                             reads=(tekt,), writes=(tekt,))
                        X.op("scalar", lambda p: p.activation(out=tek.h[:], in_=tek.h[:],
                             func=AF.Exp), reads=(tek,), writes=(tek,))
                        X.op("scalar", lambda p: p.activation(out=tekt.h[:], in_=tekt.h[:],
                             func=AF.Exp), reads=(tekt,), writes=(tekt,))
                        X.op("vector", lambda p: p.tensor_tensor(out=ekv.h[:], in0=tek.h[:],
                             in1=vc.h[:], op=ALU.mult), reads=(tek, vc), writes=(ekv,))
                        X.op("vector", lambda p, m=m: p.tensor_copy(out=Ab.h[:, 0:1],
                             in_=c_Ast.h[:, m:m + 1]), reads=(c_Ast,), writes=(Ab,), short=True)
                        X.op("vector", lambda p, m=m: p.tensor_copy(out=Bb.h[:, 0:1],
                             in_=c_Bst.h[:, m:m + 1]), reads=(c_Bst,), writes=(Bb,), short=True)
                        X.op("vector", lambda p, m=m: p.tensor_tensor_scan(
                             out=Ab.h[:, 1:TC + 1],
                             data0=c_ew.h[:, m:m + 1].broadcast_to((128, TC)),
                             data1=ekv.h[:], initial=c_Ast.h[:, m:m + 1],
                             op0=ALU.mult, op1=ALU.add),
                             reads=(c_ew, ekv, c_Ast), writes=(Ab,))
                        X.op("vector", lambda p, m=m: p.tensor_tensor_scan(
                             out=Bb.h[:, 1:TC + 1],
                             data0=c_ew.h[:, m:m + 1].broadcast_to((128, TC)),
                             data1=tek.h[:], initial=c_Bst.h[:, m:m + 1],
                             op0=ALU.mult, op1=ALU.add),
                             reads=(c_ew, tek, c_Bst), writes=(Bb,))
                        X.op("vector", lambda p, m=m: p.tensor_copy(out=c_Ast.h[:, m:m + 1],
                             in_=Ab.h[:, TC:TC + 1]), reads=(Ab,), writes=(c_Ast,), short=True)
                        X.op("vector", lambda p, m=m: p.tensor_copy(out=c_Bst.h[:, m:m + 1],
                             in_=Bb.h[:, TC:TC + 1]), reads=(Bb,), writes=(c_Bst,), short=True)
                        X.op("vector", lambda p: p.tensor_tensor(out=tnum.h[:], in0=tekt.h[:],
                             in1=vc.h[:], op=ALU.mult), reads=(tekt, vc), writes=(tnum,))
                        X.op("vector", lambda p: p.tensor_tensor(out=tnum.h[:], in0=tnum.h[:],
                             in1=Ab.h[:, 0:TC], op=ALU.add), reads=(tnum, Ab), writes=(tnum,))
                        X.op("vector", lambda p: p.tensor_tensor(out=tden.h[:], in0=tekt.h[:],
                             in1=Bb.h[:, 0:TC], op=ALU.add), reads=(tekt, Bb), writes=(tden,))
                        X.op("vector", lambda p: p.tensor_scalar(out=tden.h[:], in0=tden.h[:],
                             scalar1=1e-8, scalar2=None, op0=ALU.add),
                             reads=(tden,), writes=(tden,))
                        X.op("vector", lambda p: p.reciprocal(out=tden.h[:], in_=tden.h[:]),
                             reads=(tden,), writes=(tden,))
                        X.op("vector", lambda p: p.tensor_tensor(out=tnum.h[:], in0=tnum.h[:],
                             in1=tden.h[:], op=ALU.mult), reads=(tnum, tden), writes=(tnum,))
                        X.op("scalar", lambda p, b=bks["r"]: p.activation(out=rc_t.h[:],
                             in_=b.h[:], func=AF.Exp, scale=cv(-1.0)),
                             reads=(bks["r"], cons["cvec"]), writes=(rc_t,))
                        X.op("vector", lambda p: p.tensor_scalar(out=rc_t.h[:], in0=rc_t.h[:],
                             scalar1=1.0, scalar2=None, op0=ALU.add),
                             reads=(rc_t,), writes=(rc_t,))
                        X.op("vector", lambda p: p.reciprocal(out=rc_t.h[:], in_=rc_t.h[:]),
                             reads=(rc_t,), writes=(rc_t,))
                        X.op("vector", lambda p, m=m: p.tensor_tensor(out=rw[m].h[:],
                             in0=rc_t.h[:], in1=tnum.h[:], op=ALU.mult),
                             reads=(rc_t, tnum), writes=(rw[m],))
                    for m in range(NC):
                        wob = wo_t[m % 2]
                        load_parcel(Wo0p[m], wob, C)
                        bank = pool.get()
                        def mmo(p, wob=wob, bank=bank):
                            for kk in range(NC):
                                ins = p.matmul(out=bank.h[:],
                                               lhsT=wob.h[:, kk * 128:(kk + 1) * 128],
                                               rhs=rw[kk].h[:], start=(kk == 0), stop=(kk == NC - 1))
                            return ins
                        X.op("tensor", mmo, reads=[wob] + list(rw), writes=(bank,))
                        stg = a1st[m % 2]
                        X.op("vector", lambda p, bank=bank, m=m, stg=stg: p.tensor_tensor(
                             out=stg.h[:], in0=bank.h[:],
                             in1=xnw[m].h[:, 1:TC + 1], op=ALU.add),
                             reads=(bank, xnw[m]), writes=(stg,))
                        X.op("scalar", lambda p, m=m, t0=t0, stg=stg: p.dma_start(
                             out=d_astr[m * 128:(m + 1) * 128, t0:t0 + TC], in_=stg.h[:]),
                             reads=(stg,), dma="dS")
            X.barrier()

            ckpt(2)
            ph2 = contextlib.ExitStack()
            with ph2:
                a1t = [X.sb(f"p2a1{k}", [128, TC], F32, ph2) for k in range(NC)]
                a1sq = [X.sb(f"p2sq{k}", [128, TC], BF16, ph2) for k in range(NC)]
                tdif2 = X.sb("p2td", [128, TC], F32, ph2)
                r2f = [X.sb(f"p2r2{k}", [128, T], F16, ph2) for k in range(NC)]
                tmpf = X.sb("p2tmp", [128, TC], F32, ph2)
                k2st = X.sb("p2k2st", [128, T], BF16, ph2)
                ph2m = contextlib.ExitStack()
                xk2 = [X.sb(f"p2xk{k}", [128, T], BF16, ph2m) for k in range(NC)]
                xr2 = [X.sb(f"p2xr{k}", [128, T], BF16, ph2m) for k in range(NC)]
                h2b = [X.sb(f"p2h2{k}", [128, TC + 1], F32, ph2m) for k in range(NC)]
                for k in range(NC):
                    X.op("vector", lambda p, k=k: p.memset(h2b[k].h[:, 0:1], 0.0),
                         writes=(h2b[k],), memset=True)
                for tau in range(NT):
                    t0 = tau * TC
                    for k in range(NC):
                        X.op("scalar", lambda p, k=k, t0=t0: p.dma_start(out=a1t[k].h[:],
                             in_=d_astr[k * 128:(k + 1) * 128, t0:t0 + TC]),
                             writes=(a1t[k],), dma="dL")
                    for k in range(NC):
                        X.op("scalar", lambda p, k=k: p.activation(out=a1sq[k].h[:],
                             in_=a1t[k].h[:], func=AF.Square), reads=(a1t[k],), writes=(a1sq[k],))
                    stats_tc(a1sq, None, NC, 1e-6, False, t0)
                    for k in range(NC):
                        X.op("vector", lambda p, k=k: p.scalar_tensor_tensor(
                             out=h2b[k].h[:, 1:TC + 1], in0=a1t[k].h[:],
                             scalar=cons["ln2w"].h[:, k:k + 1], in1=bc_inv.h[:],
                             op0=ALU.mult, op1=ALU.mult),
                             reads=(a1t[k], cons["ln2w"], bc_inv), writes=(h2b[k],))
                    for k in range(NC):
                        X.op("vector", lambda p, k=k: p.tensor_tensor(out=tdif2.h[:],
                             in0=h2b[k].h[:, 1:TC + 1], in1=h2b[k].h[:, 0:TC], op=ALU.subtract),
                             reads=(h2b[k],), writes=(tdif2,))
                        for tm, dst in (("tmk2", xk2), ("tmr2", xr2)):
                            X.op("vector", lambda p, k=k, t0=t0, tm=tm, dst=dst:
                                 p.scalar_tensor_tensor(out=dst[k].h[:, t0:t0 + TC],
                                                        in0=tdif2.h[:],
                                                        scalar=cons[tm].h[:, k:k + 1],
                                                        in1=h2b[k].h[:, 0:TC],
                                                        op0=ALU.mult, op1=ALU.add),
                                 reads=(tdif2, cons[tm], h2b[k]), writes=(dst[k],))
                        X.op("vector", lambda p, k=k: p.tensor_copy(out=h2b[k].h[:, 0:1],
                             in_=h2b[k].h[:, TC:TC + 1]), reads=(h2b[k],), writes=(h2b[k],), short=True)
                wr2t = [X.sb(f"p2wr2{i}", [128, C], BF16, ph2m) for i in range(2)]
                for m in range(NC):
                    wrt = wr2t[m % 2]
                    load_parcel(Wr2p[m], wrt, C)
                    for tau in range(NT):
                        t0 = tau * TC
                        bank = pool.get()
                        def mm(p, m=m, t0=t0, bank=bank, wrt=wrt):
                            for kk in range(NC):
                                ins = p.matmul(out=bank.h[:],
                                               lhsT=wrt.h[:, kk * 128:(kk + 1) * 128],
                                               rhs=xr2[kk].h[:, t0:t0 + TC],
                                               start=(kk == 0), stop=(kk == NC - 1))
                            return ins
                        X.op("tensor", mm, reads=[wrt] + list(xr2), writes=(bank,))
                        X.op("scalar", lambda p, bank=bank: p.activation(out=tmpf.h[:],
                             in_=bank.h[:], func=AF.Exp, scale=cv(-1.0)),
                             reads=(bank, cons["cvec"]), writes=(tmpf,))
                        X.op("vector", lambda p: p.tensor_scalar(out=tmpf.h[:], in0=tmpf.h[:],
                             scalar1=1.0, scalar2=None, op0=ALU.add),
                             reads=(tmpf,), writes=(tmpf,))
                        X.op("vector", lambda p: p.reciprocal(out=tmpf.h[:], in_=tmpf.h[:]),
                             reads=(tmpf,), writes=(tmpf,))
                        X.op("vector", lambda p, m=m, t0=t0: p.tensor_copy(
                             out=r2f[m].h[:, t0:t0 + TC], in_=tmpf.h[:]),
                             reads=(tmpf,), writes=(r2f[m],))
                wk2t = [X.sb(f"p2wk2{i}", [128, C], BF16, ph2m) for i in range(2)]
                for m in range(NH):
                    wt = wk2t[m % 2]
                    load_parcel(Wk2p[m], wt, C)
                    for tau in range(NT):
                        t0 = tau * TC
                        bank = pool.get()
                        def mm(p, wt=wt, t0=t0, bank=bank):
                            for kk in range(NC):
                                ins = p.matmul(out=bank.h[:],
                                               lhsT=wt.h[:, kk * 128:(kk + 1) * 128],
                                               rhs=xk2[kk].h[:, t0:t0 + TC],
                                               start=(kk == 0), stop=(kk == NC - 1))
                            return ins
                        X.op("tensor", mm, reads=[wt] + list(xk2), writes=(bank,))
                        X.op("scalar", lambda p, bank=bank: p.copy(out=tmpf.h[:], in_=bank.h[:]),
                             reads=(bank,), writes=(tmpf,))
                        X.op("vector", lambda p, t0=t0: p.scalar_tensor_tensor(
                             out=k2st.h[:, t0:t0 + TC], in0=tmpf.h[:], scalar=0.0,
                             in1=tmpf.h[:], op0=ALU.max, op1=ALU.mult),
                             reads=(tmpf,), writes=(k2st,))
                    X.op("scalar", lambda p, m=m: p.dma_start(out=d_k2[m * 128:(m + 1) * 128, :],
                         in_=k2st.h[:]), reads=(k2st,), dma="dS")
                X.barrier()
                ph2m.close()
                wv2t = [X.sb(f"p2wv2{k}", [128, H], BF16, ph2) for k in range(NC)]
                k2c = [X.sb(f"p2k2c{i}", [128, TC], BF16, ph2) for i in range(2)]
                outst = [X.sb(f"p2o{i}", [128, TC], F32, ph2) for i in range(2)]
                for m in range(NC):
                    load_parcel(Wv2p[m], wv2t[m], H)
                for tau in range(NT):
                    t0 = tau * TC
                    banks = pool.get(8)
                    for kk in range(NH):
                        kb = k2c[kk % 2]
                        X.op("scalar", lambda p, kk=kk, t0=t0, kb=kb: p.dma_start(out=kb.h[:],
                             in_=d_k2[kk * 128:(kk + 1) * 128, t0:t0 + TC]),
                             writes=(kb,), dma="dL")
                        def mm(p, kk=kk, kb=kb, banks=banks):
                            for m in range(NC):
                                ins = p.matmul(out=banks[m].h[:],
                                               lhsT=wv2t[m].h[:, kk * 128:(kk + 1) * 128],
                                               rhs=kb.h[:], start=(kk == 0), stop=(kk == NH - 1))
                            return ins
                        X.op("tensor", mm, reads=[kb] + wv2t, writes=tuple(banks))
                    for m in range(NC):
                        X.op("scalar", lambda p, m=m, t0=t0: p.dma_start(out=a1t[m].h[:],
                             in_=d_astr[m * 128:(m + 1) * 128, t0:t0 + TC]),
                             writes=(a1t[m],), dma="dL")
                        stg = outst[m % 2]
                        X.op("vector", lambda p, m=m, t0=t0, banks=banks, stg=stg:
                             p.tensor_tensor(out=stg.h[:], in0=banks[m].h[:],
                                             in1=r2f[m].h[:, t0:t0 + TC], op=ALU.mult),
                             reads=(banks[m], r2f[m]), writes=(stg,))
                        X.op("vector", lambda p, m=m, stg=stg: p.tensor_tensor(out=stg.h[:],
                             in0=stg.h[:], in1=a1t[m].h[:], op=ALU.add),
                             reads=(stg, a1t[m]), writes=(stg,))
                        X.op("scalar", lambda p, m=m, t0=t0, stg=stg: p.dma_start(
                             out=d_astr[m * 128:(m + 1) * 128, t0:t0 + TC], in_=stg.h[:]),
                             reads=(stg,), dma="dS")
            X.barrier()

            ckpt(3)
            ph3 = contextlib.ExitStack()
            with ph3:
                hq8 = [X.sb(f"p3h8{m}", [128, T], FP8, ph3) for m in range(NH)]
                ffst = X.sb("p3ff", [128, TC], F16, ph3)
                ffs32 = X.sb("p3ff32", [128, TC], F32, ph3)
                x1t = [X.sb(f"p3x1{k}", [128, TC], F32, ph3) for k in range(NC)]
                x1sq = [X.sb(f"p3sq{k}", [128, TC], BF16, ph3) for k in range(NC)]
                x1v = [X.sb(f"p3vv{k}", [128, TC], BF16, ph3) for k in range(NC)]
                a1c = X.sb("p3a1", [128, TC], F32, ph3)
                x0c = X.sb("p3x0", [128, TC], F32, ph3)
                xn1st = [X.sb(f"p3xo{i}", [128, TC], BF16, ph3) for i in range(2)]
                dbgf = X.sb("p3dbg", [128, TC], F32, ph3)
                ph3a = contextlib.ExitStack()
                xnbf = [X.sb(f"p3xn{k}", [128, T], BF16, ph3a) for k in range(NC)]
                w1t = [X.sb(f"p3w1{i}", [128, C], BF16, ph3a) for i in range(2)]
                for k in range(NC):
                    X.op("scalar", lambda p, k=k: p.dma_start(out=xnbf[k].h[:],
                         in_=d_xnbf[k * 128:(k + 1) * 128, :]), writes=(xnbf[k],), dma="dL")
                for m in range(NH):
                    wt = w1t[m % 2]
                    load_parcel(w1p[m], wt, C)
                    for tau in range(NT):
                        t0 = tau * TC
                        bank = pool.get()
                        def mm(p, wt=wt, t0=t0, bank=bank):
                            for kk in range(NC):
                                ins = p.matmul(out=bank.h[:],
                                               lhsT=wt.h[:, kk * 128:(kk + 1) * 128],
                                               rhs=xnbf[kk].h[:, t0:t0 + TC],
                                               start=(kk == 0), stop=(kk == NC - 1))
                            return ins
                        X.op("tensor", mm, reads=[wt] + xnbf, writes=(bank,))
                        X.op("scalar", lambda p, m=m, t0=t0, bank=bank: p.activation(
                             out=hq8[m].h[:, t0:t0 + TC], in_=bank.h[:], func=AF.Relu,
                             bias=cons["b1"].h[:, m:m + 1]),
                             reads=(bank, cons["b1"]), writes=(hq8[m],))
                    if debug:
                        X.op("scalar", lambda p, m=m: p.dma_start(
                             out=d_hq8[m * 128:(m + 1) * 128, :], in_=hq8[m].h[:]),
                             reads=(hq8[m],), dma="dS")
                X.barrier()
                ph3a.close()
                wq8 = [X.sb(f"p3w2{m}", [128, H], FP8, ph3) for m in range(NC)]
                for m in range(NC):
                    load_parcel(w2p[m], wq8[m], H)
                    if debug:
                        X.op("scalar", lambda p, m=m: p.dma_start(
                             out=d_wq8[m * 128:(m + 1) * 128, :], in_=wq8[m].h[:]),
                             reads=(wq8[m],), dma="dS")
                for tau in range(NT):
                    t0 = tau * TC
                    banks = pool.get(8)
                    for kk in range(NH):
                        def mm(p, kk=kk, t0=t0, banks=banks):
                            for m in range(NC):
                                ins = p.matmul(out=banks[m].h[:],
                                               lhsT=wq8[m].h[:, kk * 128:(kk + 1) * 128],
                                               rhs=hq8[kk].h[:, t0:t0 + TC],
                                               start=(kk == 0), stop=(kk == NH - 1))
                            return ins
                        X.op("tensor", mm, reads=[hq8[kk]] + wq8, writes=tuple(banks))
                    for m in range(NC):
                        X.op("vector", lambda p, m=m, banks=banks: p.tensor_scalar(
                             out=ffs32.h[:], in0=banks[m].h[:], scalar1=c_bq.h[:, m:m + 1],
                             scalar2=None, op0=ALU.add),
                             reads=(banks[m], c_bq), writes=(ffs32,))
                        X.op("vector", lambda p: p.tensor_copy(out=ffst.h[:], in_=ffs32.h[:]),
                             reads=(ffs32,), writes=(ffst,))
                        if debug:
                            X.op("scalar", lambda p, m=m, t0=t0: p.dma_start(
                                 out=d_ff16[m * 128:(m + 1) * 128, t0:t0 + TC], in_=ffst.h[:]),
                                 reads=(ffst,), dma="dS")
                        if debug:
                            X.op("scalar", lambda p, m=m, t0=t0, banks=banks: p.copy(
                                 out=dbgf.h[:], in_=banks[m].h[:]),
                                 reads=(banks[m],), writes=(dbgf,))
                            X.op("scalar", lambda p, m=m, t0=t0: p.dma_start(
                                 out=d_ffn0[m * 128:(m + 1) * 128, t0:t0 + TC], in_=dbgf.h[:]),
                                 reads=(dbgf,), dma="dS")
                        X.op("scalar", lambda p, m=m, t0=t0: p.dma_start(out=a1c.h[:],
                             in_=d_astr[m * 128:(m + 1) * 128, t0:t0 + TC]),
                             writes=(a1c,), dma="dL")
                        X.op("scalar", lambda p, m=m, t0=t0: p.dma_start(out=x0c.h[:],
                             in_=xT[m * 128:(m + 1) * 128, t0:t0 + TC]),
                             writes=(x0c,), dma="dL")
                        X.op("vector", lambda p, m=m: p.tensor_tensor(out=x1t[m].h[:],
                             in0=a1c.h[:], in1=x0c.h[:], op=ALU.add),
                             reads=(a1c, x0c), writes=(x1t[m],))
                        X.op("vector", lambda p, m=m: p.tensor_tensor(out=x1t[m].h[:],
                             in0=x1t[m].h[:], in1=ffst.h[:], op=ALU.add),
                             reads=(x1t[m], ffst), writes=(x1t[m],))
                        X.op("scalar", lambda p, m=m, t0=t0: p.dma_start(
                             out=d_x1[m * 128:(m + 1) * 128, t0:t0 + TC], in_=x1t[m].h[:]),
                             reads=(x1t[m],), dma="dS")
                    for k in range(NC):
                        X.op("vector", lambda p, k=k: p.tensor_copy(out=x1v[k].h[:],
                             in_=x1t[k].h[:]), reads=(x1t[k],), writes=(x1v[k],))
                        X.op("scalar", lambda p, k=k: p.activation(out=x1sq[k].h[:],
                             in_=x1t[k].h[:], func=AF.Square), reads=(x1t[k],), writes=(x1sq[k],))
                    stats_tc(x1sq, x1v, NC, 1e-5, True, t0)
                    for k in range(NC):
                        stg = xn1st[k % 2]
                        X.op("vector", lambda p, k=k: p.tensor_tensor(out=x1t[k].h[:],
                             in0=x1t[k].h[:], in1=bc_m.h[:], op=ALU.subtract),
                             reads=(x1t[k], bc_m), writes=(x1t[k],))
                        X.op("vector", lambda p, k=k, stg=stg: p.tensor_tensor(out=stg.h[:],
                             in0=x1t[k].h[:], in1=bc_inv.h[:], op=ALU.mult),
                             reads=(x1t[k], bc_inv), writes=(stg,))
                        X.op("scalar", lambda p, k=k, t0=t0, stg=stg: p.dma_start(
                             out=d_xn1[k * 128:(k + 1) * 128, t0:t0 + TC], in_=stg.h[:]),
                             reads=(stg,), dma="dS")
            X.barrier()

            ckpt(4)
            ph45 = contextlib.ExitStack()
            if True:
                xn1 = [X.sb(f"p4xn{k}", [128, T], BF16, ph45) for k in range(NC)]
                xi_bf = [X.sb(f"p4xi{m}", [128, T], BF16, ph45) for m in range(NDI)]
                xipre = X.sb("p4xp", [128, T + 3], F32, ph45)
                xc_b = X.sb("p4xc", [128, T], F32, ph45)
                wt2 = [X.sb(f"p4w{i}", [128, C], BF16, ph45) for i in range(2)]
                srst = X.sb("p4sr", [128, T], BF16, ph45)
                wxt = X.sb("p4wx", [128, NDI * 32], BF16, ph45)
                bmat = X.sb("p4bm", [32, T], F32, ph45)
                for k in range(NC):
                    X.op("scalar", lambda p, k=k: p.dma_start(out=xn1[k].h[:],
                         in_=d_xn1[k * 128:(k + 1) * 128, :]), writes=(xn1[k],), dma="dL")
                X.op("vector", lambda p: p.memset(xipre.h[:, 0:3], 0.0), writes=(xipre,),
                     memset=True)
                for m in range(2 * NDI):
                    wt = wt2[m % 2]
                    load_parcel(Winp[m], wt, C)
                    bks = []
                    for tau in range(NT):
                        t0 = tau * TC
                        bank = pool.get()
                        def mm(p, wt=wt, t0=t0, bank=bank):
                            for kk in range(NC):
                                ins = p.matmul(out=bank.h[:],
                                               lhsT=wt.h[:, kk * 128:(kk + 1) * 128],
                                               rhs=xn1[kk].h[:, t0:t0 + TC],
                                               start=(kk == 0), stop=(kk == NC - 1))
                            return ins
                        X.op("tensor", mm, reads=[wt] + xn1, writes=(bank,))
                        bks.append(bank)
                    if m < NDI:
                        for tau in range(NT):
                            X.op("scalar", lambda p, tau=tau, bks=bks: p.copy(
                                 out=xipre.h[:, 3 + tau * TC:3 + (tau + 1) * TC],
                                 in_=bks[tau].h[:]), reads=(bks[tau],), writes=(xipre,))
                        X.op("vector", lambda p, m=m: p.tensor_scalar(out=xc_b.h[:],
                             in0=xipre.h[:, 3:3 + T], scalar1=cons["cw"].h[:, 4 * m + 3:4 * m + 4],
                             scalar2=None, op0=ALU.mult),
                             reads=(xipre, cons["cw"]), writes=(xc_b,))
                        for j in range(3):
                            X.op("vector", lambda p, m=m, j=j: p.scalar_tensor_tensor(
                                 out=xc_b.h[:], in0=xipre.h[:, j:j + T],
                                 scalar=cons["cw"].h[:, 4 * m + j:4 * m + j + 1],
                                 in1=xc_b.h[:], op0=ALU.mult, op1=ALU.add),
                                 reads=(xipre, cons["cw"], xc_b), writes=(xc_b,))
                        X.op("scalar", lambda p, m=m: p.activation(out=xi_bf[m].h[:],
                             in_=xc_b.h[:], func=AF.Silu, bias=cons["cb"].h[:, m:m + 1]),
                             reads=(xc_b, cons["cb"]), writes=(xi_bf[m],))
                    else:
                        mm_ = m - NDI
                        for tau in range(NT):
                            X.op("scalar", lambda p, tau=tau, bks=bks: p.activation(
                                 out=srst.h[:, tau * TC:(tau + 1) * TC], in_=bks[tau].h[:],
                                 func=AF.Silu), reads=(bks[tau],), writes=(srst,))
                        X.op("scalar", lambda p, mm_=mm_: p.dma_start(
                             out=d_sres[mm_ * 128:(mm_ + 1) * 128, :], in_=srst.h[:]),
                             reads=(srst,), dma="dS")
                load_parcel(WxTi, wxt, NDI * 32)
                for tau in range(NT):
                    t0 = tau * TC
                    bank = pool.get()
                    def mmx(p, t0=t0, bank=bank):
                        for kk in range(NDI):
                            ins = p.matmul(out=bank.h[0:32, :],
                                           lhsT=wxt.h[:, kk * 32:(kk + 1) * 32],
                                           rhs=xi_bf[kk].h[:, t0:t0 + TC],
                                           start=(kk == 0), stop=(kk == NDI - 1))
                        return ins
                    X.op("tensor", mmx, reads=[wxt] + xi_bf, writes=(bank,))
                    X.op("scalar", lambda p, t0=t0, bank=bank: p.copy(
                         out=bmat.h[0:32, t0:t0 + TC], in_=bank.h[0:32, :]),
                         reads=(bank,), writes=(bmat,))
                X.op("scalar", lambda p: p.dma_start(out=d_bmat, in_=bmat.h[0:32, :]),
                     reads=(bmat,), dma="dS")
            X.barrier()

            ckpt(5)
            if True:
                wdt = [X.sb(f"p5w{i}", [128, DI], BF16, ph45) for i in range(2)]
                dst_ = X.sb("p5d", [128, T], F32, ph45)
                for m in range(NDI):
                    wt = wdt[m % 2]
                    load_parcel(Wdtp[m], wt, DI)
                    for tau in range(NT):
                        t0 = tau * TC
                        bank = pool.get()
                        def mm(p, wt=wt, t0=t0, bank=bank):
                            for kk in range(NDI):
                                ins = p.matmul(out=bank.h[:],
                                               lhsT=wt.h[:, kk * 128:(kk + 1) * 128],
                                               rhs=xi_bf[kk].h[:, t0:t0 + TC],
                                               start=(kk == 0), stop=(kk == NDI - 1))
                            return ins
                        X.op("tensor", mm, reads=[wt] + xi_bf, writes=(bank,))
                        X.op("scalar", lambda p, m=m, t0=t0, bank=bank: p.activation(
                             out=dst_.h[:, t0:t0 + TC], in_=bank.h[:], func=AF.Exp,
                             bias=cons["bdt"].h[:, m:m + 1]),
                             reads=(bank, cons["bdt"]), writes=(dst_,))
                        X.op("scalar", lambda p, t0=t0: p.activation(
                             out=dst_.h[:, t0:t0 + TC], in_=dst_.h[:, t0:t0 + TC],
                             func=AF.Ln, bias=1.0), reads=(dst_,), writes=(dst_,))
                    X.op("scalar", lambda p, m=m: p.dma_start(
                         out=d_delta[m * 128:(m + 1) * 128, :], in_=dst_.h[:]),
                         reads=(dst_,), dma="dS")
                for m in range(NDI):
                    X.op("scalar", lambda p, m=m: p.dma_start(
                         out=d_xi[m * 128:(m + 1) * 128, :], in_=xi_bf[m].h[:]),
                         reads=(xi_bf[m],), dma="dS")
            X.barrier()
            ph45.close()

            ckpt(6)
            ph6 = contextlib.ExitStack()
            with ph6:
                brep = [X.sb(f"p6b{n}", [128, T], BF16, ph6) for n in range(NST)]
                dl = [X.sb("p6d0", [128, T], F32, ph6)] * 2
                xic = [X.sb(f"p6x{i}", [128, T], BF16, ph6) for i in range(2)]
                src = [X.sb(f"p6s{i}", [128, T], BF16, ph6) for i in range(2)]
                ub = X.sb("p6u", [128, T], BF16, ph6)
                ab = [X.sb(f"p6a{i}", [128, T], F32, ph6) for i in range(2)]
                bb = [X.sb("p6bb0", [128, T], F32, ph6)] * 2
                hb = X.sb("p6h", [128, T], F32, ph6)
                hc = X.sb("p6hc", [128, T], BF16, ph6)
                tb = X.sb("p6t", [128, T], BF16, ph6)
                yb = X.sb("p6y", [128, T], BF16, ph6)
                ybfst = ub
                fxn = [X.sb(f"p8xn{k}", [128, TC], BF16, ph6) for k in range(NC)]
                fw1 = [X.sb(f"p8w1{i}", [128, C], BF16, ph6) for i in range(2)]
                fw2 = [X.sb(f"p8w2{i}", [128, C], BF16, ph6) for i in range(2)]
                s1t = X.sb("p8s1", [128, TC], F32, ph6)
                t2t = X.sb("p8t2", [128, TC], F32, ph6)
                hst = [X.sb(f"p8h{i}", [128, TC], BF16, ph6) for i in range(2)]
                ffn1_state = {"u": 0}

                def emit_ffn1_units(nu):
                    for _ in range(nu):
                        u_ = ffn1_state["u"]
                        if u_ >= NT * NH:
                            return
                        ffn1_state["u"] += 1
                        tau, m = divmod(u_, NH)
                        t0 = tau * TC
                        if m == 0:
                            for k in range(NC):
                                X.op("scalar", lambda p, k=k, t0=t0: p.dma_start(
                                     out=fxn[k].h[:],
                                     in_=d_xn1[k * 128:(k + 1) * 128, t0:t0 + TC]),
                                     writes=(fxn[k],), dma="dL")
                        w1b = fw1[m % 2]; w2b = fw2[m % 2]
                        load_parcel(f1p[m], w1b, C)
                        load_parcel(f2p[m], w2b, C)
                        hs = hst[m % 2]
                        bank1 = pool.get(); bank2 = pool.get()
                        def mm1(p, w1b=w1b, bank1=bank1):
                            for kk in range(NC):
                                ins = p.matmul(out=bank1.h[:],
                                               lhsT=w1b.h[:, kk * 128:(kk + 1) * 128],
                                               rhs=fxn[kk].h[:],
                                               start=(kk == 0), stop=(kk == NC - 1))
                            return ins
                        X.op("tensor", mm1, reads=[w1b] + fxn, writes=(bank1,))
                        def mm2(p, w2b=w2b, bank2=bank2):
                            for kk in range(NC):
                                ins = p.matmul(out=bank2.h[:],
                                               lhsT=w2b.h[:, kk * 128:(kk + 1) * 128],
                                               rhs=fxn[kk].h[:],
                                               start=(kk == 0), stop=(kk == NC - 1))
                            return ins
                        X.op("tensor", mm2, reads=[w2b] + fxn, writes=(bank2,))
                        X.op("scalar", lambda p, m=m, bank1=bank1: p.activation(
                             out=s1t.h[:], in_=bank1.h[:], func=AF.Silu,
                             scale=cons["f1s"].h[:, m:m + 1]),
                             reads=(bank1, cons["f1s"]), writes=(s1t,))
                        X.op("scalar", lambda p, m=m, bank2=bank2: p.mul(
                             out=t2t.h[:], in_=bank2.h[:],
                             mul=cons["f2s"].h[:, m:m + 1]),
                             reads=(bank2, cons["f2s"]), writes=(t2t,))
                        X.op("vector", lambda p, hs=hs: p.tensor_tensor(
                             out=hs.h[:], in0=s1t.h[:], in1=t2t.h[:], op=ALU.mult),
                             reads=(s1t, t2t), writes=(hs,))
                        X.op("scalar", lambda p, m=m, t0=t0, hs=hs: p.dma_start(
                             out=d_h[m * 128:(m + 1) * 128, t0:t0 + TC], in_=hs.h[:]),
                             reads=(hs,), dma="dS")
                for n in range(NST):
                    for tau in range(NT):
                        t0 = tau * TC
                        X.op("scalar", lambda p, n=n, t0=t0: p.dma_start(
                             out=st_s1.h[0:1, :],
                             in_=d_bmat[16 + n:17 + n, t0:t0 + TC]),
                             writes=(st_s1,), dma="dL")
                        X.op("vector", lambda p: p.tensor_copy(out=st_r1.h[0:1, :],
                             in_=st_s1.h[0:1, :]), reads=(st_s1,), writes=(st_r1,),
                             short=True)
                        bank = pool.get()
                        X.op("tensor", lambda p, t0=t0, bank=bank: p.matmul(
                             out=bank.h[:], lhsT=c_onesrow_r.h[:],
                             rhs=st_r1.h[0:1, :], start=True, stop=True),
                             reads=(st_r1, c_onesrow_r), writes=(bank,))
                        X.op("scalar", lambda p, n=n, t0=t0, bank=bank: p.copy(
                             out=brep[n].h[:, t0:t0 + TC], in_=bank.h[:]),
                             reads=(bank,), writes=(brep[n],))
                for c in range(NDI):
                    dlc = dl[c % 2]; xcc = xic[c % 2]; srr = src[c % 2]
                    X.op("scalar", lambda p, c=c, dlc=dlc: p.dma_start(out=dlc.h[:],
                         in_=d_delta[c * 128:(c + 1) * 128, :]), writes=(dlc,), dma="dL")
                    X.op("scalar", lambda p, c=c, xcc=xcc: p.dma_start(out=xcc.h[:],
                         in_=d_xi[c * 128:(c + 1) * 128, :]), writes=(xcc,), dma="dL")
                    X.op("scalar", lambda p, c=c, s=srr: p.dma_start(out=s.h[:],
                         in_=d_sres[c * 128:(c + 1) * 128, :]), writes=(srr,), dma="dL")
                    X.op("vector", lambda p, dlc=dlc, xcc=xcc: p.tensor_tensor(out=ub.h[:],
                         in0=dlc.h[:], in1=xcc.h[:], op=ALU.mult),
                         reads=(dlc, xcc), writes=(ub,))
                    for n in range(NST):
                        av = ab[n % 2]
                        bv = bb[n % 2]
                        X.op("scalar", lambda p, c=c, n=n, dlc=dlc, av=av: p.activation(
                             out=av.h[:], in_=dlc.h[:], func=AF.Exp,
                             scale=c_A.h[:, c * NST + n:c * NST + n + 1]),
                             reads=(dlc, c_A), writes=(av,))
                        X.op("vector", lambda p, n=n, bv=bv: p.tensor_tensor(out=bv.h[:],
                             in0=ub.h[:], in1=brep[n].h[:], op=ALU.mult),
                             reads=(ub, brep[n]), writes=(bv,))
                        X.op("vector", lambda p, av=av, bv=bv: p.tensor_tensor_scan(
                             out=hb.h[:], data0=av.h[:], data1=bv.h[:], initial=0.0,
                             op0=ALU.mult, op1=ALU.add), reads=(av, bv), writes=(hb,))
                        X.op("scalar", lambda p: p.copy(out=hc.h[:], in_=hb.h[:]),
                             reads=(hb,), writes=(hc,))
                        X.op("vector", lambda p, n=n: p.tensor_tensor(out=tb.h[:],
                             in0=hc.h[:], in1=brep[n].h[:], op=ALU.mult),
                             reads=(hc, brep[n]), writes=(tb,))
                        if n == 0:
                            X.op("vector", lambda p: p.tensor_copy(out=yb.h[:],
                                 in_=tb.h[:]), reads=(tb,), writes=(yb,))
                        else:
                            X.op("vector", lambda p: p.tensor_tensor(out=yb.h[:],
                                 in0=yb.h[:], in1=tb.h[:], op=ALU.add),
                                 reads=(yb, tb), writes=(yb,))
                    X.op("vector", lambda p, c=c, xcc=xcc: p.scalar_tensor_tensor(out=tb.h[:],
                         in0=xcc.h[:], scalar=cons["dskip"].h[:, c:c + 1], in1=yb.h[:],
                         op0=ALU.mult, op1=ALU.add),
                         reads=(xcc, cons["dskip"], yb), writes=(tb,))
                    X.op("vector", lambda p, s=srr: p.tensor_tensor(out=ybfst.h[:],
                         in0=tb.h[:], in1=s.h[:], op=ALU.mult),
                         reads=(tb, srr), writes=(ybfst,))
                    X.op("scalar", lambda p, c=c: p.dma_start(
                         out=d_ybf[c * 128:(c + 1) * 128, :], in_=ybfst.h[:]),
                         reads=(ybfst,), dma="dS")
                    emit_ffn1_units(8)
                emit_ffn1_units(NT * NH)
            X.barrier()

            ckpt(7)
            ph7 = contextlib.ExitStack()
            with ph7:
                wot = [X.sb(f"p7w{m}", [128, DI], BF16, ph7) for m in range(NC)]
                ybc = [X.sb(f"p7y{i}", [128, TC], BF16, ph7) for i in range(2)]
                zt = [X.sb(f"p7z{k}", [128, TC], F32, ph7) for k in range(NC)]
                zsq = [X.sb(f"p7sq{k}", [128, TC], BF16, ph7) for k in range(NC)]
                zv = [X.sb(f"p7vv{k}", [128, TC], BF16, ph7) for k in range(NC)]
                x1c = X.sb("p7x1", [128, TC], F32, ph7)
                mst = [X.sb(f"p7m{i}", [128, TC], F32, ph7) for i in range(2)]
                for m in range(NC):
                    load_parcel(Woutp[m], wot[m], DI)
                for tau in range(NT):
                    t0 = tau * TC
                    banks = pool.get(8)
                    for kk in range(NDI):
                        kb = ybc[kk % 2]
                        X.op("scalar", lambda p, kk=kk, t0=t0, kb=kb: p.dma_start(out=kb.h[:],
                             in_=d_ybf[kk * 128:(kk + 1) * 128, t0:t0 + TC]),
                             writes=(kb,), dma="dL")
                        def mm(p, kk=kk, kb=kb, banks=banks):
                            for m in range(NC):
                                ins = p.matmul(out=banks[m].h[:],
                                               lhsT=wot[m].h[:, kk * 128:(kk + 1) * 128],
                                               rhs=kb.h[:], start=(kk == 0), stop=(kk == NDI - 1))
                            return ins
                        X.op("tensor", mm, reads=[kb] + wot, writes=tuple(banks))
                    for m in range(NC):
                        X.op("scalar", lambda p, m=m, t0=t0: p.dma_start(out=x1c.h[:],
                             in_=d_x1[m * 128:(m + 1) * 128, t0:t0 + TC]),
                             writes=(x1c,), dma="dL")
                        X.op("vector", lambda p, m=m, banks=banks: p.tensor_tensor(
                             out=zt[m].h[:], in0=banks[m].h[:], in1=x1c.h[:], op=ALU.add),
                             reads=(banks[m], x1c), writes=(zt[m],))
                    for k in range(NC):
                        X.op("vector", lambda p, k=k: p.tensor_copy(out=zv[k].h[:],
                             in_=zt[k].h[:]), reads=(zt[k],), writes=(zv[k],))
                        X.op("scalar", lambda p, k=k: p.activation(out=zsq[k].h[:],
                             in_=zt[k].h[:], func=AF.Square), reads=(zt[k],), writes=(zsq[k],))
                    stats_tc(zsq, zv, NC, 1e-5, True, t0)
                    for k in range(NC):
                        stg = mst[k % 2]
                        X.op("vector", lambda p, k=k: p.tensor_tensor(out=zt[k].h[:],
                             in0=zt[k].h[:], in1=bc_m.h[:], op=ALU.subtract),
                             reads=(zt[k], bc_m), writes=(zt[k],))
                        X.op("vector", lambda p, k=k: p.tensor_tensor(out=zt[k].h[:],
                             in0=zt[k].h[:], in1=bc_inv.h[:], op=ALU.mult),
                             reads=(zt[k], bc_inv), writes=(zt[k],))
                        X.op("vector", lambda p, k=k, stg=stg: p.tensor_scalar(out=stg.h[:],
                             in0=zt[k].h[:], scalar1=cons["lng"].h[:, k:k + 1],
                             scalar2=cons["lnb"].h[:, k:k + 1], op0=ALU.mult, op1=ALU.add),
                             reads=(zt[k], cons["lng"], cons["lnb"]), writes=(stg,))
                        X.op("scalar", lambda p, k=k, t0=t0, stg=stg: p.dma_start(
                             out=d_mamba[k * 128:(k + 1) * 128, t0:t0 + TC], in_=stg.h[:]),
                             reads=(stg,), dma="dS")
            X.barrier()

            ckpt(8)
            ckpt(9)
            def gelu_stage(ph_idx, src_d, parcels, scale_nm, dst_d):
                ph = contextlib.ExitStack()
                with ph:
                    int_ = [X.sb(f"p{ph_idx}i{k}", [128, T], BF16, ph) for k in range(NH)]
                    wt2_ = [X.sb(f"p{ph_idx}w{i}", [128, H], BF16, ph) for i in range(2)]
                    gst = [X.sb(f"p{ph_idx}g{i}", [128, T], BF16, ph) for i in range(2)]
                    for k in range(NH):
                        X.op("scalar", lambda p, k=k: p.dma_start(out=int_[k].h[:],
                             in_=src_d[k * 128:(k + 1) * 128, :]), writes=(int_[k],), dma="dL")
                    for m in range(NH):
                        wt = wt2_[m % 2]
                        load_parcel(parcels[m], wt, H)
                        gs = gst[m % 2]
                        for tau in range(NT):
                            t0 = tau * TC
                            bank = pool.get()
                            def mm(p, wt=wt, t0=t0, bank=bank):
                                for kk in range(NH):
                                    ins = p.matmul(out=bank.h[:],
                                                   lhsT=wt.h[:, kk * 128:(kk + 1) * 128],
                                                   rhs=int_[kk].h[:, t0:t0 + TC],
                                                   start=(kk == 0), stop=(kk == NH - 1))
                                return ins
                            X.op("tensor", mm, reads=[wt] + int_, writes=(bank,))
                            X.op("scalar", lambda p, m=m, t0=t0, bank=bank, gs=gs: p.activation(
                                 out=gs.h[:, t0:t0 + TC], in_=bank.h[:], func=AF.Gelu,
                                 scale=cons[scale_nm].h[:, m:m + 1]),
                                 reads=(bank, cons[scale_nm]), writes=(gs,))
                        X.op("scalar", lambda p, m=m, gs=gs: p.dma_start(
                             out=dst_d[m * 128:(m + 1) * 128, :], in_=gs.h[:]),
                             reads=(gs,), dma="dS")
                X.barrier()
            gelu_stage(9, d_h, h1p, "h1s", d_g1)
            gelu_stage(10, d_g1, h2p, "h2s", d_g2)

            ckpt(11)
            ph11 = contextlib.ExitStack()
            with ph11:
                g2t = [X.sb(f"p11g{k}", [128, T], BF16, ph11) for k in range(NH)]
                w3t = [X.sb(f"p11w{i}", [128, H], BF16, ph11) for i in range(2)]
                ffc = X.sb("p11f", [128, TC], F32, ph11)
                x1c2 = X.sb("p11x", [128, TC], F32, ph11)
                mbc = X.sb("p11m", [128, TC], F32, ph11)
                ost = [X.sb(f"p11o{i}", [128, TC], F32, ph11) for i in range(2)]
                for k in range(NH):
                    X.op("scalar", lambda p, k=k: p.dma_start(out=g2t[k].h[:],
                         in_=d_g2[k * 128:(k + 1) * 128, :]), writes=(g2t[k],), dma="dL")
                for m in range(NC):
                    wt = w3t[m % 2]
                    load_parcel(w3p[m], wt, H)
                    for tau in range(NT):
                        t0 = tau * TC
                        bank = pool.get()
                        def mm(p, wt=wt, t0=t0, bank=bank):
                            for kk in range(NH):
                                ins = p.matmul(out=bank.h[:],
                                               lhsT=wt.h[:, kk * 128:(kk + 1) * 128],
                                               rhs=g2t[kk].h[:, t0:t0 + TC],
                                               start=(kk == 0), stop=(kk == NH - 1))
                            return ins
                        X.op("tensor", mm, reads=[wt] + g2t, writes=(bank,))
                        X.op("vector", lambda p, m=m, bank=bank: p.tensor_scalar(out=ffc.h[:],
                             in0=bank.h[:], scalar1=cons["w3s"].h[:, m:m + 1], scalar2=None,
                             op0=ALU.mult), reads=(bank, cons["w3s"]), writes=(ffc,))
                        X.op("scalar", lambda p, m=m, t0=t0: p.dma_start(out=x1c2.h[:],
                             in_=d_x1[m * 128:(m + 1) * 128, t0:t0 + TC]),
                             writes=(x1c2,), dma="dL")
                        X.op("scalar", lambda p, m=m, t0=t0: p.dma_start(out=mbc.h[:],
                             in_=d_mamba[m * 128:(m + 1) * 128, t0:t0 + TC]),
                             writes=(mbc,), dma="dL")
                        stg = ost[tau % 2]
                        X.op("vector", lambda p, stg=stg: p.tensor_tensor(out=stg.h[:],
                             in0=ffc.h[:], in1=x1c2.h[:], op=ALU.add),
                             reads=(ffc, x1c2), writes=(stg,))
                        X.op("vector", lambda p, stg=stg: p.tensor_tensor(out=stg.h[:],
                             in0=stg.h[:], in1=mbc.h[:], op=ALU.add),
                             reads=(stg, mbc), writes=(stg,))
                        X.op("scalar", lambda p, m=m, t0=t0, stg=stg: p.dma_start(
                             out=yT[m * 128:(m + 1) * 128, t0:t0 + TC], in_=stg.h[:]),
                             reads=(stg,), dma="dS")

        except _StopBuild:
            pass
        X._flush()
        fin = tuple((s, X.cnt[s], False) for s in ("dS", "dL", "dW"))
        def fin_f(p):
            X._wait_emit("scalar", p, fin)
        block.scalar(fin_f)
        def fin_g(p):
            X._wait_emit("gpsimd", p, fin)
        block.gpsimd(fin_g)

    return nc


def _parcel(WT, n_m):
    K, M = WT.shape
    assert M == n_m * 128
    out = np.empty((n_m, 128, K), np.float32)
    for m in range(n_m):
        blk = WT[:, m * 128:(m + 1) * 128]
        out[m] = blk.reshape(K // 128, 128, 128).transpose(1, 0, 2).reshape(128, K)
    return np.ascontiguousarray(out)


def _vimg(v, nch):
    return np.ascontiguousarray(np.asarray(v, np.float32).reshape(nch, 128).T)


_NC_CACHE = {}


def kernel(**inputs):
    f32 = np.float32
    g = {k: np.asarray(v, f32) for k, v in inputs.items()}
    x = g["L0_attn_x"]

    shared = {
        "Wk0": g["L0_attn_W_key"], "Wv0": g["L0_attn_W_value"], "Wr0": g["L0_attn_W_recept"],
        "Wo0p": _parcel(g["L0_attn_W_out"], NC),
        "Wk2p": _parcel(g["L0_attn_W_key2"], NH),
        "Wr2p": _parcel(g["L0_attn_W_recept2"], NC),
        "Wv2p": _parcel(g["L0_attn_W_value2"], NC),
        "w1p": _parcel(np.ascontiguousarray(g["L0_ffn_w1"].T), NH),
        "w2p": _parcel(np.ascontiguousarray(g["L0_ffn_w2"].T), NC),
        "Winp": _parcel(np.ascontiguousarray(g["L1_attn_W_in"].T), 2 * NDI),
        "WxTi": np.ascontiguousarray(
            g["L1_attn_W_x"].T.reshape(NDI, 128, 32).transpose(1, 0, 2).reshape(128, NDI * 32)),
        "Wdtp": _parcel(np.ascontiguousarray(g["L1_attn_W_dt"].T), NDI),
        "Woutp": _parcel(np.ascontiguousarray(g["L1_attn_W_out"].T), NC),
        "f1p": _parcel(np.ascontiguousarray(g["L1_ffn_w1_w"].T), NH),
        "f2p": _parcel(np.ascontiguousarray(g["L1_ffn_w2_w"].T), NH),
        "h1p": _parcel(np.ascontiguousarray(g["L1_ffn_h1_w"].T), NH),
        "h2p": _parcel(np.ascontiguousarray(g["L1_ffn_h2_w"].T), NH),
        "w3p": _parcel(np.ascontiguousarray(g["L1_ffn_w3_w"].T), NC),
        "v_ln1w": _vimg(g["L0_attn_ln1_w"], NC), "v_ln2w": _vimg(g["L0_attn_ln2_w"], NC),
        "v_td": _vimg(g["L0_attn_time_decay"], NC), "v_tf": _vimg(g["L0_attn_time_first"], NC),
        "v_tmk": _vimg(g["L0_attn_time_mix_k"], NC), "v_tmv": _vimg(g["L0_attn_time_mix_v"], NC),
        "v_tmr": _vimg(g["L0_attn_time_mix_r"], NC),
        "v_tmk2": _vimg(g["L0_attn_time_mix_k2"], NC),
        "v_tmr2": _vimg(g["L0_attn_time_mix_r2"], NC),
        "v_b1": _vimg(g["L0_ffn_b1"], NH), "v_b2": _vimg(g["L0_ffn_b2"], NC),
        "v_cb": _vimg(g["L1_attn_conv_b"], NDI), "v_bdt": _vimg(g["L1_attn_b_dt"], NDI),
        "v_dskip": _vimg(g["L1_attn_D_skip"], NDI),
        "v_lng": _vimg(g["L1_attn_ln_g"], NC), "v_lnb": _vimg(g["L1_attn_ln_b"], NC),
        "v_alog": np.ascontiguousarray(
            g["L1_attn_A_log"].reshape(NDI, 128, NST).transpose(1, 0, 2).reshape(128, NDI * NST)),
        "v_cw": np.ascontiguousarray(
            g["L1_attn_conv_w"][:, 0, :].reshape(NDI, 128, 4).transpose(1, 0, 2).reshape(128, NDI * 4)),
        "v_f1s": _vimg(g["L1_ffn_w1_s"], NH), "v_f2s": _vimg(g["L1_ffn_w2_s"], NH),
        "v_h1s": _vimg(g["L1_ffn_h1_s"], NH), "v_h2s": _vimg(g["L1_ffn_h2_s"], NH),
        "v_w3s": _vimg(g["L1_ffn_w3_s"], NC),
        "v_cvec": np.ascontiguousarray(
            np.broadcast_to(np.asarray(CVALS, f32), (128, len(CVALS)))),
        "ones_col": np.ones((128, 1), ml_dtypes.bfloat16),
        "ones_row": np.ones((1, 128), f32),
    }
    in_maps = []
    for i in range(8):
        d = dict(shared)
        d["xT"] = np.ascontiguousarray(x[i].T)
        in_maps.append(d)

    nc = _NC_CACHE.get("nc")
    if nc is None:
        nc = build()
        _NC_CACHE["nc"] = nc
    res = run_bass_kernel_spmd(nc, in_maps, core_ids=list(range(8)))
    out = np.stack([np.asarray(r["yT"], np.float32).T for r in res.results], axis=0)
    return np.ascontiguousarray(out)


# version 20
# speedup vs baseline: 1.0206x; 1.0206x over previous
import contextlib
import numpy as np
import ml_dtypes
import concourse.bass as bass
import concourse.mybir as mybir
from concourse.bass_utils import run_bass_kernel_spmd

F32 = mybir.dt.float32
F32R = mybir.dt.float32r
BF16 = mybir.dt.bfloat16
F16 = mybir.dt.float16
FP8 = mybir.dt.float8e4
AF = mybir.ActivationFunctionType
ALU = mybir.AluOpType

T = 2048
TC = 512
NT = T // TC
C = 1024
NC = C // 128
DI = 2048
NDI = DI // 128
H = 4096
NH = H // 128
NST = 16

CVALS = [1.0, -1.0, -0.5, 1e-5, 1e-6, 1e-8, 20.0, -20.0, 0.0]
CIX = {v: i for i, v in enumerate(CVALS)}

SEM_ENG = {"T": "tensor", "V": "vector", "S": "scalar", "G": "gpsimd"}
ENG_SEM = {"tensor": "T", "vector": "V", "scalar": "S", "gpsimd": "G"}


class Buf:
    __slots__ = ("h", "w", "r")

    def __init__(self, h):
        self.h = h
        self.w = None
        self.r = []


class Ctx:
    def __init__(self, nc):
        self.nc = nc
        self.es = contextlib.ExitStack()
        self.block = None
        self.sem = {}
        self.cnt = {}
        for s in ("T", "V", "S", "G", "MS", "dW", "dL", "dS"):
            self.sem[s] = self.es.enter_context(nc.semaphore(s))
            self.cnt[s] = 0
        self.waited = {e: {} for e in ("tensor", "vector", "scalar", "gpsimd", "sync")}
        self.pend_by_eng = {}

    def sb(self, name, shape, dt, scope=None):
        cm = self.nc.sbuf_tensor(name, shape, dt)
        h = scope.enter_context(cm) if scope is not None else self.es.enter_context(cm)
        return Buf(h)

    def _flush(self):
        for eng in ("sync", "tensor", "vector", "scalar", "gpsimd"):
            items = self.pend_by_eng.get(eng)
            if not items:
                continue
            self.pend_by_eng[eng] = []

            def body(p, items=items):
                for it in items:
                    it(p)
            getattr(self.block, eng)(body)

    def _q(self, eng, fn):
        self.pend_by_eng.setdefault(eng, []).append(fn)

    def _wait_emit(self, eng, p, toks, skip_same=True):
        best = {}
        for t in toks:
            if t is None:
                continue
            s, v, short = t
            if skip_same and not short and SEM_ENG.get(s) in (eng,):
                continue
            if v > best.get(s, 0):
                best[s] = v
        for s, v in best.items():
            if v > self.waited[eng].get(s, 0):
                p.wait_ge(self.sem[s], v)
                self.waited[eng][s] = v

    def op(self, eng, fn, reads=(), writes=(), extra=(), dma=None, memset=False,
           short=False):
        toks = []
        for b in reads:
            toks.append(b.w)
        for b in writes:
            toks.append(b.w)
            toks.extend(b.r)
        toks.extend(extra)
        toks = [(s, self.cnt[s] if s in ("MS", "dW", "dL", "dS") else v, sh)
                for (s, v, sh) in toks if toks is not None
                for (s, v, sh) in ((s, v, sh),)] if False else [
            ((t[0], self.cnt[t[0]], t[2]) if t is not None and t[0] in ("MS", "dW", "dL", "dS")
             else t) for t in toks]
        sem = dma if dma else ("MS" if memset else ENG_SEM[eng])
        self.cnt[sem] += 16 if dma else 1
        tok = (sem, self.cnt[sem], bool(short))
        skip_same = not (dma or memset)

        def run(p, fn=fn, toks=tuple(toks), sem=sem, tok=tok, eng=eng, sk=skip_same):
            self._wait_emit(eng, p, toks, skip_same=sk)
            ins = fn(p)
            ins.then_inc(self.sem[sem], 16 if sem in ("dW", "dL", "dS") else 1)
        self._q(eng, run)
        for b in reads:
            b.r.append(tok)
        for b in writes:
            b.w = tok
            b.r = []
        return tok

    def raw(self, eng, fn):
        self._q(eng, fn)

    def tok(self, s):
        return (s, self.cnt[s], False)

    def barrier(self):
        self._flush()
        snap = tuple((s, self.cnt[s], False)
                     for s in ("T", "V", "S", "G", "MS", "dW", "dL", "dS"))
        for e in ("tensor", "vector", "scalar", "gpsimd"):
            def f(p, e=e):
                self._wait_emit(e, p, snap)
            getattr(self.block, e)(f)


class PsumPool:
    def __init__(self, ctx):
        self.banks = []
        for i in range(8):
            h = ctx.es.enter_context(ctx.nc.psum_tensor(f"pb{i}", [128, TC], F32))
            self.banks.append(Buf(h))
        self.idx = 0

    def get(self, n=1):
        out = []
        for _ in range(n):
            out.append(self.banks[self.idx])
            self.idx = (self.idx + 1) % 8
        return out if n > 1 else out[0]


class _StopBuild(Exception):
    pass


def build(debug=False, stop=None):
    nc = bass.Bass("TRN2", target_bir_lowering=False)
    X = Ctx(nc)
    pool = PsumPool(X)

    def dram_in(name, shape, dt=F32):
        return nc.dram_tensor(name, shape, dt, kind="ExternalInput").ap()

    def scratch(name, shape, dt):
        kind = "ExternalOutput" if debug else "Internal"
        return nc.dram_tensor(name, shape, dt, kind=kind).ap()

    xT = dram_in("xT", [C, T])
    Wk0 = dram_in("Wk0", [C, C]); Wv0 = dram_in("Wv0", [C, C]); Wr0 = dram_in("Wr0", [C, C])
    Wo0p = dram_in("Wo0p", [NC, 128, C])
    Wk2p = dram_in("Wk2p", [NH, 128, C])
    Wr2p = dram_in("Wr2p", [NC, 128, C])
    Wv2p = dram_in("Wv2p", [NC, 128, H])
    w1p = dram_in("w1p", [NH, 128, C])
    w2p = dram_in("w2p", [NC, 128, H])
    Winp = dram_in("Winp", [2 * NDI, 128, C])
    WxTi = dram_in("WxTi", [128, NDI * 32])
    Wdtp = dram_in("Wdtp", [NDI, 128, DI])
    Woutp = dram_in("Woutp", [NC, 128, DI])
    f1p = dram_in("f1p", [NH, 128, C])
    f2p = dram_in("f2p", [NH, 128, C])
    h1p = dram_in("h1p", [NH, 128, H])
    h2p = dram_in("h2p", [NH, 128, H])
    w3p = dram_in("w3p", [NC, 128, H])
    VSPEC = [("ln1w", NC), ("ln2w", NC), ("td", NC), ("tf", NC), ("tmk", NC),
             ("tmv", NC), ("tmr", NC), ("tmk2", NC), ("tmr2", NC), ("b1", NH),
             ("b2", NC), ("cb", NDI), ("bdt", NDI), ("dskip", NDI), ("lng", NC),
             ("lnb", NC), ("alog", NDI * NST), ("cw", NDI * 4), ("f1s", NH),
             ("f2s", NH), ("h1s", NH), ("h2s", NH), ("w3s", NC), ("cvec", len(CVALS))]
    vin = {nm: dram_in("v_" + nm, [128, w]) for nm, w in VSPEC}
    ones_col = dram_in("ones_col", [128, 1], BF16)
    ones_row = dram_in("ones_row", [1, 128])
    yT = nc.dram_tensor("yT", [C, T], F32, kind="ExternalOutput").ap()

    d_xn = scratch("d_xn", [C, T], F32)
    d_astr = scratch("d_astr", [C, T], F32)
    d_xnbf = scratch("d_xnbf", [C, T], BF16)
    d_x1 = scratch("d_x1", [C, T], F32)
    d_xn1 = scratch("d_xn1", [C, T], BF16)
    d_k2 = scratch("d_k2", [H, T], BF16)
    d_sres = scratch("d_sres", [DI, T], BF16)
    d_xi = scratch("d_xi", [DI, T], BF16)
    d_delta = scratch("d_delta", [DI, T], F32)
    d_ybf = scratch("d_ybf", [DI, T], BF16)
    d_mamba = scratch("d_mamba", [C, T], F32)
    d_bmat = scratch("d_bmat", [32, T], F32)
    d_hq8 = scratch("d_hq8", [H, T], FP8) if debug else None
    d_wq8 = scratch("d_wq8", [C, H], FP8) if debug else None
    d_ff16 = scratch("d_ff16", [C, T], F16) if debug else None
    d_cbq = scratch("d_cbq", [128, NC], F32) if debug else None
    d_cbq16 = scratch("d_cbq16", [128, NC], F16) if debug else None
    d_ffn0 = scratch("d_ffn0", [C, T], F32) if debug else None
    d_h = scratch("d_h", [H, T], BF16)
    d_g1 = scratch("d_g1", [H, T], BF16)
    d_g2 = scratch("d_g2", [H, T], BF16)

    cons = {nm: X.sb("c_" + nm, [128, w], F32) for nm, w in VSPEC}
    c_ones = X.sb("c_ones", [128, 1], BF16)
    c_onesrow = X.sb("c_onesrow", [1, 128], F32)
    c_onesrow_r = X.sb("c_onesrow_r", [1, 128], F32R)
    c_ew = X.sb("c_ew", [128, NC], F32)
    c_A = X.sb("c_A", [128, NDI * NST], F32)
    c_bq = X.sb("c_bq", [128, NC], F32)
    c_bq16 = X.sb("c_bq16", [128, NC], F16)
    ring = [X.sb(f"ring{i}", [128, 2048], F32) for i in range(2)]
    ring_ix = [0]
    cast_rr = [0]
    st_s1 = X.sb("st_s1", [1, TC], F32)
    st_s2 = X.sb("st_s2", [1, TC], F32)
    st_m = X.sb("st_m", [1, TC], F32)
    st_inv = X.sb("st_inv", [1, TC], F32)
    st_r1 = X.sb("st_r1", [1, TC], F32R)
    st_r2 = X.sb("st_r2", [1, TC], F32R)
    bc_m = X.sb("bc_m", [128, TC], F32)
    bc_inv = X.sb("bc_inv", [128, TC], F32)
    c_Ast = X.sb("c_Ast", [128, NC], F32)
    c_Bst = X.sb("c_Bst", [128, NC], F32)

    def cv(val, np_=128):
        return cons["cvec"].h[0:np_, CIX[val]:CIX[val] + 1]

    RW = 2048

    CAST_ENGS = ("gpsimd",)

    def load_parcel(src_ap, dst_buf, K, cast_eng=None):
        for c0 in range(0, K, RW):
            c1 = min(c0 + RW, K)
            slot = ring[ring_ix[0]]
            ring_ix[0] = (ring_ix[0] + 1) % len(ring)
            X.op("sync", lambda p, s=slot, a=src_ap, c0=c0, c1=c1: p.dma_start(
                 out=s.h[:, 0:c1 - c0], in_=a[:, c0:c1]),
                 reads=(), writes=(slot,), dma="dW")
            eng = cast_eng or CAST_ENGS[cast_rr[0] % len(CAST_ENGS)]
            cast_rr[0] += 1
            if eng == "scalar":
                X.op(eng, lambda p, s=slot, d=dst_buf, c0=c0, c1=c1: p.copy(
                     out=d.h[:, c0:c1], in_=s.h[:, 0:c1 - c0]),
                     reads=(slot,), writes=(dst_buf,))
            else:
                X.op(eng, lambda p, s=slot, d=dst_buf, c0=c0, c1=c1: p.tensor_copy(
                     out=d.h[:, c0:c1], in_=s.h[:, 0:c1 - c0]),
                     reads=(slot,), writes=(dst_buf,))

    def stats_tc(sq_tiles, val_tiles, nchunks, eps, want_mean, tcol):
        n = 128 * nchunks
        for which, tiles, dst in ((0, val_tiles, st_s1), (1, sq_tiles, st_s2)):
            if which == 0 and not want_mean:
                continue
            bank = pool.get()
            def mm(p, tiles=tiles, bank=bank):
                for k in range(nchunks):
                    ins = p.matmul(out=bank.h[0:1, :], lhsT=c_ones.h[:], rhs=tiles[k].h[:],
                                   start=(k == 0), stop=(k == nchunks - 1))
                return ins
            X.op("tensor", mm, reads=[c_ones] + list(tiles), writes=(bank,))
            X.op("scalar", lambda p, bank=bank, dst=dst: p.copy(out=dst.h[0:1, :],
                                                                in_=bank.h[0:1, :]),
                 reads=(bank,), writes=(dst,), short=True)
        if want_mean:
            X.op("vector", lambda p: p.tensor_scalar(out=st_m.h[0:1, :], in0=st_s1.h[0:1, :],
                                                     scalar1=1.0 / n, scalar2=None,
                                                     op0=ALU.mult),
                 reads=(st_s1,), writes=(st_m,), short=True)
            X.op("vector", lambda p: p.tensor_tensor(out=st_inv.h[0:1, :], in0=st_m.h[0:1, :],
                                                     in1=st_m.h[0:1, :], op=ALU.mult),
                 reads=(st_m,), writes=(st_inv,), short=True)
            X.op("vector", lambda p: p.tensor_scalar(out=st_s2.h[0:1, :], in0=st_s2.h[0:1, :],
                                                     scalar1=1.0 / n, scalar2=None,
                                                     op0=ALU.mult),
                 reads=(st_s2,), writes=(st_s2,), short=True)
            X.op("vector", lambda p: p.tensor_tensor(out=st_s2.h[0:1, :], in0=st_s2.h[0:1, :],
                                                     in1=st_inv.h[0:1, :], op=ALU.subtract),
                 reads=(st_s2, st_inv), writes=(st_s2,), short=True)
        else:
            X.op("vector", lambda p: p.tensor_scalar(out=st_s2.h[0:1, :], in0=st_s2.h[0:1, :],
                                                     scalar1=1.0 / n, scalar2=None,
                                                     op0=ALU.mult),
                 reads=(st_s2,), writes=(st_s2,), short=True)
        X.op("vector", lambda p, eps=eps: p.tensor_scalar(out=st_s2.h[0:1, :],
                                                          in0=st_s2.h[0:1, :], scalar1=eps,
                                                          scalar2=None, op0=ALU.add),
             reads=(st_s2,), writes=(st_s2,), short=True)
        X.op("scalar", lambda p: p.activation(out=st_inv.h[0:1, :], in_=st_s2.h[0:1, :],
                                              func=AF.Ln),
             reads=(st_s2,), writes=(st_inv,), short=True)
        X.op("scalar", lambda p: p.activation(out=st_inv.h[0:1, :], in_=st_inv.h[0:1, :],
                                              func=AF.Exp, scale=cv(-0.5, 1)),
             reads=(st_inv, cons["cvec"]), writes=(st_inv,), short=True)
        X.op("vector", lambda p: p.tensor_copy(out=st_r2.h[0:1, :], in_=st_inv.h[0:1, :]),
             reads=(st_inv,), writes=(st_r2,), short=True)
        if want_mean:
            X.op("vector", lambda p: p.tensor_copy(out=st_r1.h[0:1, :], in_=st_m.h[0:1, :]),
                 reads=(st_m,), writes=(st_r1,), short=True)
        for row, dstbc, en in ((st_r2, bc_inv, True), (st_r1, bc_m, want_mean)):
            if not en:
                continue
            bank = pool.get()
            X.op("tensor", lambda p, row=row, bank=bank: p.matmul(out=bank.h[:],
                 lhsT=c_onesrow_r.h[:], rhs=row.h[0:1, :], start=True, stop=True),
                 reads=(row, c_onesrow_r), writes=(bank,))
            X.op("scalar", lambda p, bank=bank, d=dstbc: p.copy(out=d.h[:], in_=bank.h[:]),
                 reads=(bank,), writes=(dstbc,))

    with X.es, nc.Block() as block:
        X.block = block
        def ckpt(n):
            if stop is not None and n >= stop:
                raise _StopBuild()
        try:
            for nm, w in VSPEC:
                X.op("sync", lambda p, nm=nm, w=w: p.dma_start(out=cons[nm].h[:, 0:w],
                                                               in_=vin[nm]),
                     writes=(cons[nm],), dma="dW")
            X.op("sync", lambda p: p.dma_start(out=c_ones.h[:], in_=ones_col),
                 writes=(c_ones,), dma="dW")
            X.op("sync", lambda p: p.dma_start(out=c_onesrow.h[:], in_=ones_row),
                 writes=(c_onesrow,), dma="dW")
            X.op("vector", lambda p: p.tensor_copy(out=c_onesrow_r.h[:], in_=c_onesrow.h[:]),
                 reads=(c_onesrow,), writes=(c_onesrow_r,), short=True)
            X.op("vector", lambda p: p.tensor_copy(out=c_bq16.h[:], in_=cons["b2"].h[:, 0:NC]),
                 reads=(cons["b2"],), writes=(c_bq16,), short=True)
            X.op("vector", lambda p: p.tensor_copy(out=c_bq.h[:], in_=c_bq16.h[:]),
                 reads=(c_bq16,), writes=(c_bq,), short=True)
            X.op("vector", lambda p: p.memset(c_Ast.h[:], 0.0), writes=(c_Ast,), memset=True)
            X.op("vector", lambda p: p.memset(c_Bst.h[:], 0.0), writes=(c_Bst,), memset=True)
            if debug:
                X.op("scalar", lambda p: p.dma_start(out=d_cbq, in_=c_bq.h[:]),
                     reads=(c_bq,), dma="dS")
                X.op("scalar", lambda p: p.dma_start(out=d_cbq16, in_=c_bq16.h[:]),
                     reads=(c_bq16,), dma="dS")
            X.op("scalar", lambda p: p.activation(out=c_ew.h[:], in_=cons["td"].h[:, 0:NC],
                                                  func=AF.Exp),
                 reads=(cons["td"],), writes=(c_ew,), short=True)
            X.op("scalar", lambda p: p.activation(out=c_A.h[:], in_=cons["alog"].h[:],
                                                  func=AF.Exp),
                 reads=(cons["alog"],), writes=(c_A,), short=True)
            X.op("vector", lambda p: p.tensor_scalar(out=c_ew.h[:], in0=c_ew.h[:], scalar1=-1.0,
                                                     scalar2=-20.0, op0=ALU.mult, op1=ALU.max),
                 reads=(c_ew,), writes=(c_ew,), short=True)
            X.op("vector", lambda p: p.tensor_scalar(out=c_A.h[:], in0=c_A.h[:], scalar1=-1.0,
                                                     scalar2=None, op0=ALU.mult),
                 reads=(c_A,), writes=(c_A,), short=True)
            X.op("scalar", lambda p: p.activation(out=c_ew.h[:], in_=c_ew.h[:], func=AF.Exp),
                 reads=(c_ew,), writes=(c_ew,), short=True)
            X.barrier()

            ckpt(0)
            ph0 = contextlib.ExitStack()
            with ph0:
                x0t = [X.sb(f"p0x{k}", [128, T], F32, ph0) for k in range(NC)]
                vbf = [X.sb(f"p0vb{k}", [128, TC], BF16, ph0) for k in range(NC)]
                sbf = [X.sb(f"p0sb{k}", [128, TC], BF16, ph0) for k in range(NC)]
                xnc = [X.sb(f"p0c{i}", [128, TC], F32, ph0) for i in range(2)]
                xnb = [X.sb(f"p0b{i}", [128, TC], BF16, ph0) for i in range(2)]
                for k in range(NC):
                    X.op("sync", lambda p, k=k: p.dma_start(out=x0t[k].h[:],
                                                            in_=xT[k * 128:(k + 1) * 128, :]),
                         writes=(x0t[k],), dma="dW")
                for tau in range(NT):
                    t0 = tau * TC
                    for k in range(NC):
                        X.op("vector", lambda p, k=k, t0=t0: p.tensor_copy(out=vbf[k].h[:],
                             in_=x0t[k].h[:, t0:t0 + TC]), reads=(x0t[k],), writes=(vbf[k],))
                        X.op("scalar", lambda p, k=k, t0=t0: p.activation(out=sbf[k].h[:],
                             in_=x0t[k].h[:, t0:t0 + TC], func=AF.Square),
                             reads=(x0t[k],), writes=(sbf[k],))
                    stats_tc(sbf, vbf, NC, 1e-5, True, t0)
                    for k in range(NC):
                        stg = xnc[k % 2]; stgb = xnb[k % 2]
                        X.op("vector", lambda p, k=k, t0=t0, stg=stg: p.tensor_tensor(
                             out=stg.h[:], in0=x0t[k].h[:, t0:t0 + TC],
                             in1=bc_m.h[:], op=ALU.subtract),
                             reads=(x0t[k], bc_m), writes=(stg,))
                        X.op("vector", lambda p, stg=stg: p.tensor_tensor(
                             out=stg.h[:], in0=stg.h[:], in1=bc_inv.h[:], op=ALU.mult),
                             reads=(stg, bc_inv), writes=(stg,))
                        X.op("vector", lambda p, stg=stg, stgb=stgb: p.tensor_copy(
                             out=stgb.h[:], in_=stg.h[:]), reads=(stg,), writes=(stgb,))
                        X.op("scalar", lambda p, k=k, t0=t0, stg=stg: p.dma_start(
                             out=d_xn[k * 128:(k + 1) * 128, t0:t0 + TC], in_=stg.h[:]),
                             reads=(stg,), dma="dS")
                        X.op("scalar", lambda p, k=k, t0=t0, stgb=stgb: p.dma_start(
                             out=d_xnbf[k * 128:(k + 1) * 128, t0:t0 + TC], in_=stgb.h[:]),
                             reads=(stgb,), dma="dS")
            X.barrier()

            ckpt(1)
            ph1 = contextlib.ExitStack()
            with ph1:
                wk = [X.sb(f"p1wk{k}", [128, C], BF16, ph1) for k in range(NC)]
                wv = [X.sb(f"p1wv{k}", [128, C], BF16, ph1) for k in range(NC)]
                wr = [X.sb(f"p1wr{k}", [128, C], BF16, ph1) for k in range(NC)]
                wo_t = [X.sb(f"p1wo{i}", [128, C], BF16, ph1) for i in range(2)]
                xk = [X.sb(f"p1xk{k}", [128, TC], BF16, ph1) for k in range(NC)]
                xv = [X.sb(f"p1xv{k}", [128, TC], BF16, ph1) for k in range(NC)]
                xr = [X.sb(f"p1xr{k}", [128, TC], BF16, ph1) for k in range(NC)]
                tdif = X.sb("p1td", [128, TC], F32, ph1)
                kc = X.sb("p1kc", [128, TC], F32, ph1)
                vc = X.sb("p1vc", [128, TC], F32, ph1)
                tek = X.sb("p1tek", [128, TC], F32, ph1)
                tekt = X.sb("p1tekt", [128, TC], F32, ph1)
                ekv = X.sb("p1ekv", [128, TC], F32, ph1)
                Ab = X.sb("p1Ab", [128, TC + 1], F32, ph1)
                Bb = X.sb("p1Bb", [128, TC + 1], F32, ph1)
                tnum = X.sb("p1num", [128, TC], F32, ph1)
                tden = X.sb("p1den", [128, TC], F32, ph1)
                rc_t = X.sb("p1rc", [128, TC], F32, ph1)
                rw = [X.sb(f"p1rw{k}", [128, TC], BF16, ph1) for k in range(NC)]
                a1st = [X.sb(f"p1a1{i}", [128, TC], F32, ph1) for i in range(2)]
                xnw = [X.sb(f"p1xw{k}", [128, TC + 1], F32, ph1) for k in range(NC)]

                for src, dsts in ((Wk0, wk), (Wv0, wv), (Wr0, wr)):
                    for k in range(NC):
                        load_parcel(src[k * 128:(k + 1) * 128, :], dsts[k], C)

                for tau in range(NT):
                    t0 = tau * TC
                    for k in range(NC):
                        if tau == 0:
                            X.op("vector", lambda p, k=k: p.memset(xnw[k].h[:, 0:1], 0.0),
                                 writes=(xnw[k],), memset=True)
                            X.op("scalar", lambda p, k=k: p.dma_start(out=xnw[k].h[:, 1:TC + 1],
                                 in_=d_xn[k * 128:(k + 1) * 128, 0:TC]),
                                 writes=(xnw[k],), dma="dL")
                        else:
                            X.op("scalar", lambda p, k=k, t0=t0: p.dma_start(out=xnw[k].h[:],
                                 in_=d_xn[k * 128:(k + 1) * 128, t0 - 1:t0 + TC]),
                                 writes=(xnw[k],), dma="dL")
                    for k in range(NC):
                        X.op("vector", lambda p, k=k: p.tensor_tensor(out=tdif.h[:],
                             in0=xnw[k].h[:, 1:TC + 1],
                             in1=xnw[k].h[:, 0:TC], op=ALU.subtract),
                             reads=(xnw[k],), writes=(tdif,))
                        for tmv_, dst in (("tmk", xk), ("tmv", xv), ("tmr", xr)):
                            X.op("vector", lambda p, k=k, tmv_=tmv_, dst=dst:
                                 p.scalar_tensor_tensor(out=dst[k].h[:], in0=tdif.h[:],
                                                        scalar=cons[tmv_].h[:, k:k + 1],
                                                        in1=xnw[k].h[:, 0:TC],
                                                        op0=ALU.mult, op1=ALU.add),
                                 reads=(tdif, cons[tmv_], xnw[k]), writes=(dst[k],))
                    for m in range(NC):
                        bks = {}
                        for nm, wt, xt in (("k", wk, xk), ("v", wv, xv), ("r", wr, xr)):
                            bank = pool.get()
                            def mmf(p, wt=wt, xt=xt, bank=bank, m=m):
                                for kk in range(NC):
                                    ins = p.matmul(out=bank.h[:],
                                                   lhsT=wt[kk].h[:, m * 128:(m + 1) * 128],
                                                   rhs=xt[kk].h[:],
                                                   start=(kk == 0), stop=(kk == NC - 1))
                                return ins
                            X.op("tensor", mmf, reads=list(wt) + list(xt), writes=(bank,))
                            bks[nm] = bank
                        X.op("scalar", lambda p, b=bks["k"]: p.copy(out=kc.h[:], in_=b.h[:]),
                             reads=(bks["k"],), writes=(kc,))
                        X.op("scalar", lambda p, b=bks["v"]: p.copy(out=vc.h[:], in_=b.h[:]),
                             reads=(bks["v"],), writes=(vc,))
                        X.op("vector", lambda p: p.tensor_scalar(out=tek.h[:], in0=kc.h[:],
                             scalar1=-20.0, scalar2=20.0, op0=ALU.max, op1=ALU.min),
                             reads=(kc,), writes=(tek,))
                        X.op("vector", lambda p, m=m: p.tensor_scalar(out=tekt.h[:], in0=kc.h[:],
                             scalar1=cons["tf"].h[:, m:m + 1], scalar2=-20.0,
                             op0=ALU.add, op1=ALU.max),
                             reads=(kc, cons["tf"]), writes=(tekt,))
                        X.op("vector", lambda p: p.tensor_scalar(out=tekt.h[:], in0=tekt.h[:],
                             scalar1=20.0, scalar2=None, op0=ALU.min),
                             reads=(tekt,), writes=(tekt,))
                        X.op("scalar", lambda p: p.activation(out=tek.h[:], in_=tek.h[:],
                             func=AF.Exp), reads=(tek,), writes=(tek,))
                        X.op("scalar", lambda p: p.activation(out=tekt.h[:], in_=tekt.h[:],
                             func=AF.Exp), reads=(tekt,), writes=(tekt,))
                        X.op("vector", lambda p: p.tensor_tensor(out=ekv.h[:], in0=tek.h[:],
                             in1=vc.h[:], op=ALU.mult), reads=(tek, vc), writes=(ekv,))
                        X.op("vector", lambda p, m=m: p.tensor_copy(out=Ab.h[:, 0:1],
                             in_=c_Ast.h[:, m:m + 1]), reads=(c_Ast,), writes=(Ab,), short=True)
                        X.op("vector", lambda p, m=m: p.tensor_copy(out=Bb.h[:, 0:1],
                             in_=c_Bst.h[:, m:m + 1]), reads=(c_Bst,), writes=(Bb,), short=True)
                        X.op("vector", lambda p, m=m: p.tensor_tensor_scan(
                             out=Ab.h[:, 1:TC + 1],
                             data0=c_ew.h[:, m:m + 1].broadcast_to((128, TC)),
                             data1=ekv.h[:], initial=c_Ast.h[:, m:m + 1],
                             op0=ALU.mult, op1=ALU.add),
                             reads=(c_ew, ekv, c_Ast), writes=(Ab,))
                        X.op("vector", lambda p, m=m: p.tensor_tensor_scan(
                             out=Bb.h[:, 1:TC + 1],
                             data0=c_ew.h[:, m:m + 1].broadcast_to((128, TC)),
                             data1=tek.h[:], initial=c_Bst.h[:, m:m + 1],
                             op0=ALU.mult, op1=ALU.add),
                             reads=(c_ew, tek, c_Bst), writes=(Bb,))
                        X.op("vector", lambda p, m=m: p.tensor_copy(out=c_Ast.h[:, m:m + 1],
                             in_=Ab.h[:, TC:TC + 1]), reads=(Ab,), writes=(c_Ast,), short=True)
                        X.op("vector", lambda p, m=m: p.tensor_copy(out=c_Bst.h[:, m:m + 1],
                             in_=Bb.h[:, TC:TC + 1]), reads=(Bb,), writes=(c_Bst,), short=True)
                        X.op("vector", lambda p: p.tensor_tensor(out=tnum.h[:], in0=tekt.h[:],
                             in1=vc.h[:], op=ALU.mult), reads=(tekt, vc), writes=(tnum,))
                        X.op("vector", lambda p: p.tensor_tensor(out=tnum.h[:], in0=tnum.h[:],
                             in1=Ab.h[:, 0:TC], op=ALU.add), reads=(tnum, Ab), writes=(tnum,))
                        X.op("vector", lambda p: p.tensor_tensor(out=tden.h[:], in0=tekt.h[:],
                             in1=Bb.h[:, 0:TC], op=ALU.add), reads=(tekt, Bb), writes=(tden,))
                        X.op("vector", lambda p: p.tensor_scalar(out=tden.h[:], in0=tden.h[:],
                             scalar1=1e-8, scalar2=None, op0=ALU.add),
                             reads=(tden,), writes=(tden,))
                        X.op("vector", lambda p: p.reciprocal(out=tden.h[:], in_=tden.h[:]),
                             reads=(tden,), writes=(tden,))
                        X.op("vector", lambda p: p.tensor_tensor(out=tnum.h[:], in0=tnum.h[:],
                             in1=tden.h[:], op=ALU.mult), reads=(tnum, tden), writes=(tnum,))
                        X.op("scalar", lambda p, b=bks["r"]: p.activation(out=rc_t.h[:],
                             in_=b.h[:], func=AF.Exp, scale=cv(-1.0)),
                             reads=(bks["r"], cons["cvec"]), writes=(rc_t,))
                        X.op("vector", lambda p: p.tensor_scalar(out=rc_t.h[:], in0=rc_t.h[:],
                             scalar1=1.0, scalar2=None, op0=ALU.add),
                             reads=(rc_t,), writes=(rc_t,))
                        X.op("vector", lambda p: p.reciprocal(out=rc_t.h[:], in_=rc_t.h[:]),
                             reads=(rc_t,), writes=(rc_t,))
                        X.op("vector", lambda p, m=m: p.tensor_tensor(out=rw[m].h[:],
                             in0=rc_t.h[:], in1=tnum.h[:], op=ALU.mult),
                             reads=(rc_t, tnum), writes=(rw[m],))
                    for m in range(NC):
                        wob = wo_t[m % 2]
                        load_parcel(Wo0p[m], wob, C)
                        bank = pool.get()
                        def mmo(p, wob=wob, bank=bank):
                            for kk in range(NC):
                                ins = p.matmul(out=bank.h[:],
                                               lhsT=wob.h[:, kk * 128:(kk + 1) * 128],
                                               rhs=rw[kk].h[:], start=(kk == 0), stop=(kk == NC - 1))
                            return ins
                        X.op("tensor", mmo, reads=[wob] + list(rw), writes=(bank,))
                        stg = a1st[m % 2]
                        X.op("vector", lambda p, bank=bank, m=m, stg=stg: p.tensor_tensor(
                             out=stg.h[:], in0=bank.h[:],
                             in1=xnw[m].h[:, 1:TC + 1], op=ALU.add),
                             reads=(bank, xnw[m]), writes=(stg,))
                        X.op("scalar", lambda p, m=m, t0=t0, stg=stg: p.dma_start(
                             out=d_astr[m * 128:(m + 1) * 128, t0:t0 + TC], in_=stg.h[:]),
                             reads=(stg,), dma="dS")
            X.barrier()

            ckpt(2)
            ph2 = contextlib.ExitStack()
            with ph2:
                a1t = [X.sb(f"p2a1{k}", [128, TC], F32, ph2) for k in range(NC)]
                a1sq = [X.sb(f"p2sq{k}", [128, TC], BF16, ph2) for k in range(NC)]
                tdif2 = X.sb("p2td", [128, TC], F32, ph2)
                r2f = [X.sb(f"p2r2{k}", [128, T], F16, ph2) for k in range(NC)]
                tmpf = X.sb("p2tmp", [128, TC], F32, ph2)
                k2st = X.sb("p2k2st", [128, T], BF16, ph2)
                ph2m = contextlib.ExitStack()
                xk2 = [X.sb(f"p2xk{k}", [128, T], BF16, ph2m) for k in range(NC)]
                xr2 = [X.sb(f"p2xr{k}", [128, T], BF16, ph2m) for k in range(NC)]
                h2b = [X.sb(f"p2h2{k}", [128, TC + 1], F32, ph2m) for k in range(NC)]
                for k in range(NC):
                    X.op("vector", lambda p, k=k: p.memset(h2b[k].h[:, 0:1], 0.0),
                         writes=(h2b[k],), memset=True)
                for tau in range(NT):
                    t0 = tau * TC
                    for k in range(NC):
                        X.op("scalar", lambda p, k=k, t0=t0: p.dma_start(out=a1t[k].h[:],
                             in_=d_astr[k * 128:(k + 1) * 128, t0:t0 + TC]),
                             writes=(a1t[k],), dma="dL")
                    for k in range(NC):
                        X.op("scalar", lambda p, k=k: p.activation(out=a1sq[k].h[:],
                             in_=a1t[k].h[:], func=AF.Square), reads=(a1t[k],), writes=(a1sq[k],))
                    stats_tc(a1sq, None, NC, 1e-6, False, t0)
                    for k in range(NC):
                        X.op("vector", lambda p, k=k: p.scalar_tensor_tensor(
                             out=h2b[k].h[:, 1:TC + 1], in0=a1t[k].h[:],
                             scalar=cons["ln2w"].h[:, k:k + 1], in1=bc_inv.h[:],
                             op0=ALU.mult, op1=ALU.mult),
                             reads=(a1t[k], cons["ln2w"], bc_inv), writes=(h2b[k],))
                    for k in range(NC):
                        X.op("vector", lambda p, k=k: p.tensor_tensor(out=tdif2.h[:],
                             in0=h2b[k].h[:, 1:TC + 1], in1=h2b[k].h[:, 0:TC], op=ALU.subtract),
                             reads=(h2b[k],), writes=(tdif2,))
                        for tm, dst in (("tmk2", xk2), ("tmr2", xr2)):
                            X.op("vector", lambda p, k=k, t0=t0, tm=tm, dst=dst:
                                 p.scalar_tensor_tensor(out=dst[k].h[:, t0:t0 + TC],
                                                        in0=tdif2.h[:],
                                                        scalar=cons[tm].h[:, k:k + 1],
                                                        in1=h2b[k].h[:, 0:TC],
                                                        op0=ALU.mult, op1=ALU.add),
                                 reads=(tdif2, cons[tm], h2b[k]), writes=(dst[k],))
                        X.op("vector", lambda p, k=k: p.tensor_copy(out=h2b[k].h[:, 0:1],
                             in_=h2b[k].h[:, TC:TC + 1]), reads=(h2b[k],), writes=(h2b[k],), short=True)
                wr2t = [X.sb(f"p2wr2{i}", [128, C], BF16, ph2m) for i in range(2)]
                for m in range(NC):
                    wrt = wr2t[m % 2]
                    load_parcel(Wr2p[m], wrt, C)
                    for tau in range(NT):
                        t0 = tau * TC
                        bank = pool.get()
                        def mm(p, m=m, t0=t0, bank=bank, wrt=wrt):
                            for kk in range(NC):
                                ins = p.matmul(out=bank.h[:],
                                               lhsT=wrt.h[:, kk * 128:(kk + 1) * 128],
                                               rhs=xr2[kk].h[:, t0:t0 + TC],
                                               start=(kk == 0), stop=(kk == NC - 1))
                            return ins
                        X.op("tensor", mm, reads=[wrt] + list(xr2), writes=(bank,))
                        X.op("scalar", lambda p, bank=bank: p.activation(out=tmpf.h[:],
                             in_=bank.h[:], func=AF.Exp, scale=cv(-1.0)),
                             reads=(bank, cons["cvec"]), writes=(tmpf,))
                        X.op("vector", lambda p: p.tensor_scalar(out=tmpf.h[:], in0=tmpf.h[:],
                             scalar1=1.0, scalar2=None, op0=ALU.add),
                             reads=(tmpf,), writes=(tmpf,))
                        X.op("vector", lambda p: p.reciprocal(out=tmpf.h[:], in_=tmpf.h[:]),
                             reads=(tmpf,), writes=(tmpf,))
                        X.op("vector", lambda p, m=m, t0=t0: p.tensor_copy(
                             out=r2f[m].h[:, t0:t0 + TC], in_=tmpf.h[:]),
                             reads=(tmpf,), writes=(r2f[m],))
                wk2t = [X.sb(f"p2wk2{i}", [128, C], BF16, ph2m) for i in range(2)]
                for m in range(NH):
                    wt = wk2t[m % 2]
                    load_parcel(Wk2p[m], wt, C)
                    for tau in range(NT):
                        t0 = tau * TC
                        bank = pool.get()
                        def mm(p, wt=wt, t0=t0, bank=bank):
                            for kk in range(NC):
                                ins = p.matmul(out=bank.h[:],
                                               lhsT=wt.h[:, kk * 128:(kk + 1) * 128],
                                               rhs=xk2[kk].h[:, t0:t0 + TC],
                                               start=(kk == 0), stop=(kk == NC - 1))
                            return ins
                        X.op("tensor", mm, reads=[wt] + list(xk2), writes=(bank,))
                        X.op("scalar", lambda p, bank=bank: p.copy(out=tmpf.h[:], in_=bank.h[:]),
                             reads=(bank,), writes=(tmpf,))
                        X.op("vector", lambda p, t0=t0: p.scalar_tensor_tensor(
                             out=k2st.h[:, t0:t0 + TC], in0=tmpf.h[:], scalar=0.0,
                             in1=tmpf.h[:], op0=ALU.max, op1=ALU.mult),
                             reads=(tmpf,), writes=(k2st,))
                    X.op("scalar", lambda p, m=m: p.dma_start(out=d_k2[m * 128:(m + 1) * 128, :],
                         in_=k2st.h[:]), reads=(k2st,), dma="dS")
                X.barrier()
                ph2m.close()
                wv2t = [X.sb(f"p2wv2{k}", [128, H], BF16, ph2) for k in range(NC)]
                k2c = [X.sb(f"p2k2c{i}", [128, TC], BF16, ph2) for i in range(2)]
                outst = [X.sb(f"p2o{i}", [128, TC], F32, ph2) for i in range(2)]
                for m in range(NC):
                    load_parcel(Wv2p[m], wv2t[m], H)
                for tau in range(NT):
                    t0 = tau * TC
                    banks = pool.get(8)
                    for kk in range(NH):
                        kb = k2c[kk % 2]
                        X.op("scalar", lambda p, kk=kk, t0=t0, kb=kb: p.dma_start(out=kb.h[:],
                             in_=d_k2[kk * 128:(kk + 1) * 128, t0:t0 + TC]),
                             writes=(kb,), dma="dL")
                        def mm(p, kk=kk, kb=kb, banks=banks):
                            for m in range(NC):
                                ins = p.matmul(out=banks[m].h[:],
                                               lhsT=wv2t[m].h[:, kk * 128:(kk + 1) * 128],
                                               rhs=kb.h[:], start=(kk == 0), stop=(kk == NH - 1))
                            return ins
                        X.op("tensor", mm, reads=[kb] + wv2t, writes=tuple(banks))
                    for m in range(NC):
                        X.op("scalar", lambda p, m=m, t0=t0: p.dma_start(out=a1t[m].h[:],
                             in_=d_astr[m * 128:(m + 1) * 128, t0:t0 + TC]),
                             writes=(a1t[m],), dma="dL")
                        stg = outst[m % 2]
                        X.op("vector", lambda p, m=m, t0=t0, banks=banks, stg=stg:
                             p.tensor_tensor(out=stg.h[:], in0=banks[m].h[:],
                                             in1=r2f[m].h[:, t0:t0 + TC], op=ALU.mult),
                             reads=(banks[m], r2f[m]), writes=(stg,))
                        X.op("vector", lambda p, m=m, stg=stg: p.tensor_tensor(out=stg.h[:],
                             in0=stg.h[:], in1=a1t[m].h[:], op=ALU.add),
                             reads=(stg, a1t[m]), writes=(stg,))
                        X.op("scalar", lambda p, m=m, t0=t0, stg=stg: p.dma_start(
                             out=d_astr[m * 128:(m + 1) * 128, t0:t0 + TC], in_=stg.h[:]),
                             reads=(stg,), dma="dS")
            X.barrier()

            ckpt(3)
            ph3 = contextlib.ExitStack()
            with ph3:
                hq8 = [X.sb(f"p3h8{m}", [128, T], FP8, ph3) for m in range(NH)]
                ffst = X.sb("p3ff", [128, TC], F16, ph3)
                ffs32 = X.sb("p3ff32", [128, TC], F32, ph3)
                x1t = [X.sb(f"p3x1{k}", [128, TC], F32, ph3) for k in range(NC)]
                x1sq = [X.sb(f"p3sq{k}", [128, TC], BF16, ph3) for k in range(NC)]
                x1v = [X.sb(f"p3vv{k}", [128, TC], BF16, ph3) for k in range(NC)]
                a1c = X.sb("p3a1", [128, TC], F32, ph3)
                x0c = X.sb("p3x0", [128, TC], F32, ph3)
                xn1st = [X.sb(f"p3xo{i}", [128, TC], BF16, ph3) for i in range(2)]
                dbgf = X.sb("p3dbg", [128, TC], F32, ph3)
                ph3a = contextlib.ExitStack()
                xnbf = [X.sb(f"p3xn{k}", [128, T], BF16, ph3a) for k in range(NC)]
                w1t = [X.sb(f"p3w1{i}", [128, C], BF16, ph3a) for i in range(2)]
                for k in range(NC):
                    X.op("scalar", lambda p, k=k: p.dma_start(out=xnbf[k].h[:],
                         in_=d_xnbf[k * 128:(k + 1) * 128, :]), writes=(xnbf[k],), dma="dL")
                for m in range(NH):
                    wt = w1t[m % 2]
                    load_parcel(w1p[m], wt, C)
                    for tau in range(NT):
                        t0 = tau * TC
                        bank = pool.get()
                        def mm(p, wt=wt, t0=t0, bank=bank):
                            for kk in range(NC):
                                ins = p.matmul(out=bank.h[:],
                                               lhsT=wt.h[:, kk * 128:(kk + 1) * 128],
                                               rhs=xnbf[kk].h[:, t0:t0 + TC],
                                               start=(kk == 0), stop=(kk == NC - 1))
                            return ins
                        X.op("tensor", mm, reads=[wt] + xnbf, writes=(bank,))
                        X.op("scalar", lambda p, m=m, t0=t0, bank=bank: p.activation(
                             out=hq8[m].h[:, t0:t0 + TC], in_=bank.h[:], func=AF.Relu,
                             bias=cons["b1"].h[:, m:m + 1]),
                             reads=(bank, cons["b1"]), writes=(hq8[m],))
                    if debug:
                        X.op("scalar", lambda p, m=m: p.dma_start(
                             out=d_hq8[m * 128:(m + 1) * 128, :], in_=hq8[m].h[:]),
                             reads=(hq8[m],), dma="dS")
                X.barrier()
                ph3a.close()
                wq8 = [X.sb(f"p3w2{m}", [128, H], FP8, ph3) for m in range(NC)]
                for m in range(NC):
                    load_parcel(w2p[m], wq8[m], H)
                    if debug:
                        X.op("scalar", lambda p, m=m: p.dma_start(
                             out=d_wq8[m * 128:(m + 1) * 128, :], in_=wq8[m].h[:]),
                             reads=(wq8[m],), dma="dS")
                for tau in range(NT):
                    t0 = tau * TC
                    banks = pool.get(8)
                    for kk in range(NH):
                        def mm(p, kk=kk, t0=t0, banks=banks):
                            for m in range(NC):
                                ins = p.matmul(out=banks[m].h[:],
                                               lhsT=wq8[m].h[:, kk * 128:(kk + 1) * 128],
                                               rhs=hq8[kk].h[:, t0:t0 + TC],
                                               start=(kk == 0), stop=(kk == NH - 1))
                            return ins
                        X.op("tensor", mm, reads=[hq8[kk]] + wq8, writes=tuple(banks))
                    for m in range(NC):
                        X.op("vector", lambda p, m=m, banks=banks: p.tensor_scalar(
                             out=ffs32.h[:], in0=banks[m].h[:], scalar1=c_bq.h[:, m:m + 1],
                             scalar2=None, op0=ALU.add),
                             reads=(banks[m], c_bq), writes=(ffs32,))
                        X.op("vector", lambda p: p.tensor_copy(out=ffst.h[:], in_=ffs32.h[:]),
                             reads=(ffs32,), writes=(ffst,))
                        if debug:
                            X.op("scalar", lambda p, m=m, t0=t0: p.dma_start(
                                 out=d_ff16[m * 128:(m + 1) * 128, t0:t0 + TC], in_=ffst.h[:]),
                                 reads=(ffst,), dma="dS")
                        if debug:
                            X.op("scalar", lambda p, m=m, t0=t0, banks=banks: p.copy(
                                 out=dbgf.h[:], in_=banks[m].h[:]),
                                 reads=(banks[m],), writes=(dbgf,))
                            X.op("scalar", lambda p, m=m, t0=t0: p.dma_start(
                                 out=d_ffn0[m * 128:(m + 1) * 128, t0:t0 + TC], in_=dbgf.h[:]),
                                 reads=(dbgf,), dma="dS")
                        X.op("scalar", lambda p, m=m, t0=t0: p.dma_start(out=a1c.h[:],
                             in_=d_astr[m * 128:(m + 1) * 128, t0:t0 + TC]),
                             writes=(a1c,), dma="dL")
                        X.op("scalar", lambda p, m=m, t0=t0: p.dma_start(out=x0c.h[:],
                             in_=xT[m * 128:(m + 1) * 128, t0:t0 + TC]),
                             writes=(x0c,), dma="dL")
                        X.op("vector", lambda p, m=m: p.tensor_tensor(out=x1t[m].h[:],
                             in0=a1c.h[:], in1=x0c.h[:], op=ALU.add),
                             reads=(a1c, x0c), writes=(x1t[m],))
                        X.op("vector", lambda p, m=m: p.tensor_tensor(out=x1t[m].h[:],
                             in0=x1t[m].h[:], in1=ffst.h[:], op=ALU.add),
                             reads=(x1t[m], ffst), writes=(x1t[m],))
                        X.op("scalar", lambda p, m=m, t0=t0: p.dma_start(
                             out=d_x1[m * 128:(m + 1) * 128, t0:t0 + TC], in_=x1t[m].h[:]),
                             reads=(x1t[m],), dma="dS")
                    for k in range(NC):
                        X.op("vector", lambda p, k=k: p.tensor_copy(out=x1v[k].h[:],
                             in_=x1t[k].h[:]), reads=(x1t[k],), writes=(x1v[k],))
                        X.op("scalar", lambda p, k=k: p.activation(out=x1sq[k].h[:],
                             in_=x1t[k].h[:], func=AF.Square), reads=(x1t[k],), writes=(x1sq[k],))
                    stats_tc(x1sq, x1v, NC, 1e-5, True, t0)
                    for k in range(NC):
                        stg = xn1st[k % 2]
                        X.op("vector", lambda p, k=k: p.tensor_tensor(out=x1t[k].h[:],
                             in0=x1t[k].h[:], in1=bc_m.h[:], op=ALU.subtract),
                             reads=(x1t[k], bc_m), writes=(x1t[k],))
                        X.op("vector", lambda p, k=k, stg=stg: p.tensor_tensor(out=stg.h[:],
                             in0=x1t[k].h[:], in1=bc_inv.h[:], op=ALU.mult),
                             reads=(x1t[k], bc_inv), writes=(stg,))
                        X.op("scalar", lambda p, k=k, t0=t0, stg=stg: p.dma_start(
                             out=d_xn1[k * 128:(k + 1) * 128, t0:t0 + TC], in_=stg.h[:]),
                             reads=(stg,), dma="dS")
            X.barrier()

            ckpt(4)
            ph45 = contextlib.ExitStack()
            if True:
                xn1 = [X.sb(f"p4xn{k}", [128, T], BF16, ph45) for k in range(NC)]
                xi_bf = [X.sb(f"p4xi{m}", [128, T], BF16, ph45) for m in range(NDI)]
                xipre = X.sb("p4xp", [128, T + 3], F32, ph45)
                xc_b = X.sb("p4xc", [128, T], F32, ph45)
                wt2 = [X.sb(f"p4w{i}", [128, C], BF16, ph45) for i in range(2)]
                srst = X.sb("p4sr", [128, T], BF16, ph45)
                wxt = X.sb("p4wx", [128, NDI * 32], BF16, ph45)
                bmat = X.sb("p4bm", [32, T], F32, ph45)
                for k in range(NC):
                    X.op("scalar", lambda p, k=k: p.dma_start(out=xn1[k].h[:],
                         in_=d_xn1[k * 128:(k + 1) * 128, :]), writes=(xn1[k],), dma="dL")
                X.op("vector", lambda p: p.memset(xipre.h[:, 0:3], 0.0), writes=(xipre,),
                     memset=True)
                for m in range(2 * NDI):
                    wt = wt2[m % 2]
                    load_parcel(Winp[m], wt, C)
                    bks = []
                    for tau in range(NT):
                        t0 = tau * TC
                        bank = pool.get()
                        def mm(p, wt=wt, t0=t0, bank=bank):
                            for kk in range(NC):
                                ins = p.matmul(out=bank.h[:],
                                               lhsT=wt.h[:, kk * 128:(kk + 1) * 128],
                                               rhs=xn1[kk].h[:, t0:t0 + TC],
                                               start=(kk == 0), stop=(kk == NC - 1))
                            return ins
                        X.op("tensor", mm, reads=[wt] + xn1, writes=(bank,))
                        bks.append(bank)
                    if m < NDI:
                        for tau in range(NT):
                            X.op("scalar", lambda p, tau=tau, bks=bks: p.copy(
                                 out=xipre.h[:, 3 + tau * TC:3 + (tau + 1) * TC],
                                 in_=bks[tau].h[:]), reads=(bks[tau],), writes=(xipre,))
                        X.op("vector", lambda p, m=m: p.tensor_scalar(out=xc_b.h[:],
                             in0=xipre.h[:, 3:3 + T], scalar1=cons["cw"].h[:, 4 * m + 3:4 * m + 4],
                             scalar2=None, op0=ALU.mult),
                             reads=(xipre, cons["cw"]), writes=(xc_b,))
                        for j in range(3):
                            X.op("vector", lambda p, m=m, j=j: p.scalar_tensor_tensor(
                                 out=xc_b.h[:], in0=xipre.h[:, j:j + T],
                                 scalar=cons["cw"].h[:, 4 * m + j:4 * m + j + 1],
                                 in1=xc_b.h[:], op0=ALU.mult, op1=ALU.add),
                                 reads=(xipre, cons["cw"], xc_b), writes=(xc_b,))
                        X.op("scalar", lambda p, m=m: p.activation(out=xi_bf[m].h[:],
                             in_=xc_b.h[:], func=AF.Silu, bias=cons["cb"].h[:, m:m + 1]),
                             reads=(xc_b, cons["cb"]), writes=(xi_bf[m],))
                    else:
                        mm_ = m - NDI
                        for tau in range(NT):
                            X.op("scalar", lambda p, tau=tau, bks=bks: p.activation(
                                 out=srst.h[:, tau * TC:(tau + 1) * TC], in_=bks[tau].h[:],
                                 func=AF.Silu), reads=(bks[tau],), writes=(srst,))
                        X.op("scalar", lambda p, mm_=mm_: p.dma_start(
                             out=d_sres[mm_ * 128:(mm_ + 1) * 128, :], in_=srst.h[:]),
                             reads=(srst,), dma="dS")
                load_parcel(WxTi, wxt, NDI * 32)
                for tau in range(NT):
                    t0 = tau * TC
                    bank = pool.get()
                    def mmx(p, t0=t0, bank=bank):
                        for kk in range(NDI):
                            ins = p.matmul(out=bank.h[0:32, :],
                                           lhsT=wxt.h[:, kk * 32:(kk + 1) * 32],
                                           rhs=xi_bf[kk].h[:, t0:t0 + TC],
                                           start=(kk == 0), stop=(kk == NDI - 1))
                        return ins
                    X.op("tensor", mmx, reads=[wxt] + xi_bf, writes=(bank,))
                    X.op("scalar", lambda p, t0=t0, bank=bank: p.copy(
                         out=bmat.h[0:32, t0:t0 + TC], in_=bank.h[0:32, :]),
                         reads=(bank,), writes=(bmat,))
                X.op("scalar", lambda p: p.dma_start(out=d_bmat, in_=bmat.h[0:32, :]),
                     reads=(bmat,), dma="dS")
            X.barrier()

            ckpt(5)
            if True:
                wdt = [X.sb(f"p5w{i}", [128, DI], BF16, ph45) for i in range(2)]
                dst_ = X.sb("p5d", [128, T], F32, ph45)
                for m in range(NDI):
                    wt = wdt[m % 2]
                    load_parcel(Wdtp[m], wt, DI)
                    for tau in range(NT):
                        t0 = tau * TC
                        bank = pool.get()
                        def mm(p, wt=wt, t0=t0, bank=bank):
                            for kk in range(NDI):
                                ins = p.matmul(out=bank.h[:],
                                               lhsT=wt.h[:, kk * 128:(kk + 1) * 128],
                                               rhs=xi_bf[kk].h[:, t0:t0 + TC],
                                               start=(kk == 0), stop=(kk == NDI - 1))
                            return ins
                        X.op("tensor", mm, reads=[wt] + xi_bf, writes=(bank,))
                        X.op("scalar", lambda p, m=m, t0=t0, bank=bank: p.activation(
                             out=dst_.h[:, t0:t0 + TC], in_=bank.h[:], func=AF.Exp,
                             bias=cons["bdt"].h[:, m:m + 1]),
                             reads=(bank, cons["bdt"]), writes=(dst_,))
                        X.op("scalar", lambda p, t0=t0: p.activation(
                             out=dst_.h[:, t0:t0 + TC], in_=dst_.h[:, t0:t0 + TC],
                             func=AF.Ln, bias=1.0), reads=(dst_,), writes=(dst_,))
                    X.op("scalar", lambda p, m=m: p.dma_start(
                         out=d_delta[m * 128:(m + 1) * 128, :], in_=dst_.h[:]),
                         reads=(dst_,), dma="dS")
                for m in range(NDI):
                    X.op("scalar", lambda p, m=m: p.dma_start(
                         out=d_xi[m * 128:(m + 1) * 128, :], in_=xi_bf[m].h[:]),
                         reads=(xi_bf[m],), dma="dS")
            X.barrier()
            ph45.close()

            ckpt(6)
            ph6 = contextlib.ExitStack()
            with ph6:
                brep = [X.sb(f"p6b{n}", [128, T], BF16, ph6) for n in range(NST)]
                dl = [X.sb("p6d0", [128, T], F32, ph6)] * 2
                xic = [X.sb(f"p6x{i}", [128, T], BF16, ph6) for i in range(2)]
                src = [X.sb(f"p6s{i}", [128, T], BF16, ph6) for i in range(2)]
                ub = X.sb("p6u", [128, T], BF16, ph6)
                ab = [X.sb(f"p6a{i}", [128, T], F32, ph6) for i in range(2)]
                bb = [X.sb("p6bb0", [128, T], F32, ph6)] * 2
                hb = X.sb("p6h", [128, T], F32, ph6)
                hc = X.sb("p6hc", [128, T], BF16, ph6)
                tb = X.sb("p6t", [128, T], BF16, ph6)
                yb = X.sb("p6y", [128, T], BF16, ph6)
                ybfst = ub
                fxn = [X.sb(f"p8xn{k}", [128, TC], BF16, ph6) for k in range(NC)]
                fw1 = [X.sb(f"p8w1{i}", [128, C], BF16, ph6) for i in range(2)]
                fw2 = [X.sb(f"p8w2{i}", [128, C], BF16, ph6) for i in range(2)]
                s1t = X.sb("p8s1", [128, TC], F32, ph6)
                t2t = X.sb("p8t2", [128, TC], F32, ph6)
                hst = [X.sb(f"p8h{i}", [128, TC], BF16, ph6) for i in range(2)]
                ffn1_state = {"u": 0}

                def emit_ffn1_units(nu):
                    for _ in range(nu):
                        u_ = ffn1_state["u"]
                        if u_ >= NT * NH:
                            return
                        ffn1_state["u"] += 1
                        tau, m = divmod(u_, NH)
                        t0 = tau * TC
                        if m == 0:
                            for k in range(NC):
                                X.op("scalar", lambda p, k=k, t0=t0: p.dma_start(
                                     out=fxn[k].h[:],
                                     in_=d_xn1[k * 128:(k + 1) * 128, t0:t0 + TC]),
                                     writes=(fxn[k],), dma="dL")
                        w1b = fw1[m % 2]; w2b = fw2[m % 2]
                        load_parcel(f1p[m], w1b, C)
                        load_parcel(f2p[m], w2b, C)
                        hs = hst[m % 2]
                        bank1 = pool.get(); bank2 = pool.get()
                        def mm1(p, w1b=w1b, bank1=bank1):
                            for kk in range(NC):
                                ins = p.matmul(out=bank1.h[:],
                                               lhsT=w1b.h[:, kk * 128:(kk + 1) * 128],
                                               rhs=fxn[kk].h[:],
                                               start=(kk == 0), stop=(kk == NC - 1))
                            return ins
                        X.op("tensor", mm1, reads=[w1b] + fxn, writes=(bank1,))
                        def mm2(p, w2b=w2b, bank2=bank2):
                            for kk in range(NC):
                                ins = p.matmul(out=bank2.h[:],
                                               lhsT=w2b.h[:, kk * 128:(kk + 1) * 128],
                                               rhs=fxn[kk].h[:],
                                               start=(kk == 0), stop=(kk == NC - 1))
                            return ins
                        X.op("tensor", mm2, reads=[w2b] + fxn, writes=(bank2,))
                        X.op("scalar", lambda p, m=m, bank1=bank1: p.activation(
                             out=s1t.h[:], in_=bank1.h[:], func=AF.Silu,
                             scale=cons["f1s"].h[:, m:m + 1]),
                             reads=(bank1, cons["f1s"]), writes=(s1t,))
                        X.op("scalar", lambda p, m=m, bank2=bank2: p.mul(
                             out=t2t.h[:], in_=bank2.h[:],
                             mul=cons["f2s"].h[:, m:m + 1]),
                             reads=(bank2, cons["f2s"]), writes=(t2t,))
                        X.op("vector", lambda p, hs=hs: p.tensor_tensor(
                             out=hs.h[:], in0=s1t.h[:], in1=t2t.h[:], op=ALU.mult),
                             reads=(s1t, t2t), writes=(hs,))
                        X.op("scalar", lambda p, m=m, t0=t0, hs=hs: p.dma_start(
                             out=d_h[m * 128:(m + 1) * 128, t0:t0 + TC], in_=hs.h[:]),
                             reads=(hs,), dma="dS")
                for n in range(NST):
                    for tau in range(NT):
                        t0 = tau * TC
                        X.op("scalar", lambda p, n=n, t0=t0: p.dma_start(
                             out=st_s1.h[0:1, :],
                             in_=d_bmat[16 + n:17 + n, t0:t0 + TC]),
                             writes=(st_s1,), dma="dL")
                        X.op("vector", lambda p: p.tensor_copy(out=st_r1.h[0:1, :],
                             in_=st_s1.h[0:1, :]), reads=(st_s1,), writes=(st_r1,),
                             short=True)
                        bank = pool.get()
                        X.op("tensor", lambda p, t0=t0, bank=bank: p.matmul(
                             out=bank.h[:], lhsT=c_onesrow_r.h[:],
                             rhs=st_r1.h[0:1, :], start=True, stop=True),
                             reads=(st_r1, c_onesrow_r), writes=(bank,))
                        X.op("scalar", lambda p, n=n, t0=t0, bank=bank: p.copy(
                             out=brep[n].h[:, t0:t0 + TC], in_=bank.h[:]),
                             reads=(bank,), writes=(brep[n],))
                for c in range(NDI):
                    dlc = dl[c % 2]; xcc = xic[c % 2]; srr = src[c % 2]
                    X.op("scalar", lambda p, c=c, dlc=dlc: p.dma_start(out=dlc.h[:],
                         in_=d_delta[c * 128:(c + 1) * 128, :]), writes=(dlc,), dma="dL")
                    X.op("scalar", lambda p, c=c, xcc=xcc: p.dma_start(out=xcc.h[:],
                         in_=d_xi[c * 128:(c + 1) * 128, :]), writes=(xcc,), dma="dL")
                    X.op("scalar", lambda p, c=c, s=srr: p.dma_start(out=s.h[:],
                         in_=d_sres[c * 128:(c + 1) * 128, :]), writes=(srr,), dma="dL")
                    X.op("vector", lambda p, dlc=dlc, xcc=xcc: p.tensor_tensor(out=ub.h[:],
                         in0=dlc.h[:], in1=xcc.h[:], op=ALU.mult),
                         reads=(dlc, xcc), writes=(ub,))
                    for n in range(NST):
                        av = ab[n % 2]
                        bv = bb[n % 2]
                        X.op("scalar", lambda p, c=c, n=n, dlc=dlc, av=av: p.activation(
                             out=av.h[:], in_=dlc.h[:], func=AF.Exp,
                             scale=c_A.h[:, c * NST + n:c * NST + n + 1]),
                             reads=(dlc, c_A), writes=(av,))
                        X.op("vector", lambda p, n=n, bv=bv: p.tensor_tensor(out=bv.h[:],
                             in0=ub.h[:], in1=brep[n].h[:], op=ALU.mult),
                             reads=(ub, brep[n]), writes=(bv,))
                        X.op("vector", lambda p, av=av, bv=bv: p.tensor_tensor_scan(
                             out=hb.h[:], data0=av.h[:], data1=bv.h[:], initial=0.0,
                             op0=ALU.mult, op1=ALU.add), reads=(av, bv), writes=(hb,))
                        X.op("vector", lambda p: p.tensor_copy(out=hc.h[:], in_=hb.h[:]),
                             reads=(hb,), writes=(hc,))
                        X.op("vector", lambda p, n=n: p.tensor_tensor(out=tb.h[:],
                             in0=hc.h[:], in1=brep[n].h[:], op=ALU.mult),
                             reads=(hc, brep[n]), writes=(tb,))
                        if n == 0:
                            X.op("vector", lambda p: p.tensor_copy(out=yb.h[:],
                                 in_=tb.h[:]), reads=(tb,), writes=(yb,))
                        else:
                            X.op("vector", lambda p: p.tensor_tensor(out=yb.h[:],
                                 in0=yb.h[:], in1=tb.h[:], op=ALU.add),
                                 reads=(yb, tb), writes=(yb,))
                    X.op("vector", lambda p, c=c, xcc=xcc: p.scalar_tensor_tensor(out=tb.h[:],
                         in0=xcc.h[:], scalar=cons["dskip"].h[:, c:c + 1], in1=yb.h[:],
                         op0=ALU.mult, op1=ALU.add),
                         reads=(xcc, cons["dskip"], yb), writes=(tb,))
                    X.op("vector", lambda p, s=srr: p.tensor_tensor(out=ybfst.h[:],
                         in0=tb.h[:], in1=s.h[:], op=ALU.mult),
                         reads=(tb, srr), writes=(ybfst,))
                    X.op("scalar", lambda p, c=c: p.dma_start(
                         out=d_ybf[c * 128:(c + 1) * 128, :], in_=ybfst.h[:]),
                         reads=(ybfst,), dma="dS")
                    emit_ffn1_units(8)
                emit_ffn1_units(NT * NH)
            X.barrier()

            ckpt(7)
            ph7 = contextlib.ExitStack()
            with ph7:
                wot = [X.sb(f"p7w{m}", [128, DI], BF16, ph7) for m in range(NC)]
                ybc = [X.sb(f"p7y{i}", [128, TC], BF16, ph7) for i in range(2)]
                zt = [X.sb(f"p7z{k}", [128, TC], F32, ph7) for k in range(NC)]
                zsq = [X.sb(f"p7sq{k}", [128, TC], BF16, ph7) for k in range(NC)]
                zv = [X.sb(f"p7vv{k}", [128, TC], BF16, ph7) for k in range(NC)]
                x1c = X.sb("p7x1", [128, TC], F32, ph7)
                mst = [X.sb(f"p7m{i}", [128, TC], F32, ph7) for i in range(2)]
                for m in range(NC):
                    load_parcel(Woutp[m], wot[m], DI)
                for tau in range(NT):
                    t0 = tau * TC
                    banks = pool.get(8)
                    for kk in range(NDI):
                        kb = ybc[kk % 2]
                        X.op("scalar", lambda p, kk=kk, t0=t0, kb=kb: p.dma_start(out=kb.h[:],
                             in_=d_ybf[kk * 128:(kk + 1) * 128, t0:t0 + TC]),
                             writes=(kb,), dma="dL")
                        def mm(p, kk=kk, kb=kb, banks=banks):
                            for m in range(NC):
                                ins = p.matmul(out=banks[m].h[:],
                                               lhsT=wot[m].h[:, kk * 128:(kk + 1) * 128],
                                               rhs=kb.h[:], start=(kk == 0), stop=(kk == NDI - 1))
                            return ins
                        X.op("tensor", mm, reads=[kb] + wot, writes=tuple(banks))
                    for m in range(NC):
                        X.op("scalar", lambda p, m=m, t0=t0: p.dma_start(out=x1c.h[:],
                             in_=d_x1[m * 128:(m + 1) * 128, t0:t0 + TC]),
                             writes=(x1c,), dma="dL")
                        X.op("vector", lambda p, m=m, banks=banks: p.tensor_tensor(
                             out=zt[m].h[:], in0=banks[m].h[:], in1=x1c.h[:], op=ALU.add),
                             reads=(banks[m], x1c), writes=(zt[m],))
                    for k in range(NC):
                        X.op("vector", lambda p, k=k: p.tensor_copy(out=zv[k].h[:],
                             in_=zt[k].h[:]), reads=(zt[k],), writes=(zv[k],))
                        X.op("scalar", lambda p, k=k: p.activation(out=zsq[k].h[:],
                             in_=zt[k].h[:], func=AF.Square), reads=(zt[k],), writes=(zsq[k],))
                    stats_tc(zsq, zv, NC, 1e-5, True, t0)
                    for k in range(NC):
                        stg = mst[k % 2]
                        X.op("vector", lambda p, k=k: p.tensor_tensor(out=zt[k].h[:],
                             in0=zt[k].h[:], in1=bc_m.h[:], op=ALU.subtract),
                             reads=(zt[k], bc_m), writes=(zt[k],))
                        X.op("vector", lambda p, k=k: p.tensor_tensor(out=zt[k].h[:],
                             in0=zt[k].h[:], in1=bc_inv.h[:], op=ALU.mult),
                             reads=(zt[k], bc_inv), writes=(zt[k],))
                        X.op("vector", lambda p, k=k, stg=stg: p.tensor_scalar(out=stg.h[:],
                             in0=zt[k].h[:], scalar1=cons["lng"].h[:, k:k + 1],
                             scalar2=cons["lnb"].h[:, k:k + 1], op0=ALU.mult, op1=ALU.add),
                             reads=(zt[k], cons["lng"], cons["lnb"]), writes=(stg,))
                        X.op("scalar", lambda p, k=k, t0=t0, stg=stg: p.dma_start(
                             out=d_mamba[k * 128:(k + 1) * 128, t0:t0 + TC], in_=stg.h[:]),
                             reads=(stg,), dma="dS")
            X.barrier()

            ckpt(8)
            ckpt(9)
            def gelu_stage(ph_idx, src_d, parcels, scale_nm, dst_d):
                ph = contextlib.ExitStack()
                with ph:
                    int_ = [X.sb(f"p{ph_idx}i{k}", [128, T], BF16, ph) for k in range(NH)]
                    wt2_ = [X.sb(f"p{ph_idx}w{i}", [128, H], BF16, ph) for i in range(2)]
                    gst = [X.sb(f"p{ph_idx}g{i}", [128, T], BF16, ph) for i in range(2)]
                    for k in range(NH):
                        X.op("scalar", lambda p, k=k: p.dma_start(out=int_[k].h[:],
                             in_=src_d[k * 128:(k + 1) * 128, :]), writes=(int_[k],), dma="dL")
                    for m in range(NH):
                        wt = wt2_[m % 2]
                        load_parcel(parcels[m], wt, H)
                        gs = gst[m % 2]
                        for tau in range(NT):
                            t0 = tau * TC
                            bank = pool.get()
                            def mm(p, wt=wt, t0=t0, bank=bank):
                                for kk in range(NH):
                                    ins = p.matmul(out=bank.h[:],
                                                   lhsT=wt.h[:, kk * 128:(kk + 1) * 128],
                                                   rhs=int_[kk].h[:, t0:t0 + TC],
                                                   start=(kk == 0), stop=(kk == NH - 1))
                                return ins
                            X.op("tensor", mm, reads=[wt] + int_, writes=(bank,))
                            X.op("scalar", lambda p, m=m, t0=t0, bank=bank, gs=gs: p.activation(
                                 out=gs.h[:, t0:t0 + TC], in_=bank.h[:], func=AF.Gelu,
                                 scale=cons[scale_nm].h[:, m:m + 1]),
                                 reads=(bank, cons[scale_nm]), writes=(gs,))
                        X.op("scalar", lambda p, m=m, gs=gs: p.dma_start(
                             out=dst_d[m * 128:(m + 1) * 128, :], in_=gs.h[:]),
                             reads=(gs,), dma="dS")
                X.barrier()
            gelu_stage(9, d_h, h1p, "h1s", d_g1)
            gelu_stage(10, d_g1, h2p, "h2s", d_g2)

            ckpt(11)
            ph11 = contextlib.ExitStack()
            with ph11:
                g2t = [X.sb(f"p11g{k}", [128, T], BF16, ph11) for k in range(NH)]
                w3t = [X.sb(f"p11w{i}", [128, H], BF16, ph11) for i in range(2)]
                ffc = X.sb("p11f", [128, TC], F32, ph11)
                x1c2 = X.sb("p11x", [128, TC], F32, ph11)
                mbc = X.sb("p11m", [128, TC], F32, ph11)
                ost = [X.sb(f"p11o{i}", [128, TC], F32, ph11) for i in range(2)]
                for k in range(NH):
                    X.op("scalar", lambda p, k=k: p.dma_start(out=g2t[k].h[:],
                         in_=d_g2[k * 128:(k + 1) * 128, :]), writes=(g2t[k],), dma="dL")
                for m in range(NC):
                    wt = w3t[m % 2]
                    load_parcel(w3p[m], wt, H)
                    for tau in range(NT):
                        t0 = tau * TC
                        bank = pool.get()
                        def mm(p, wt=wt, t0=t0, bank=bank):
                            for kk in range(NH):
                                ins = p.matmul(out=bank.h[:],
                                               lhsT=wt.h[:, kk * 128:(kk + 1) * 128],
                                               rhs=g2t[kk].h[:, t0:t0 + TC],
                                               start=(kk == 0), stop=(kk == NH - 1))
                            return ins
                        X.op("tensor", mm, reads=[wt] + g2t, writes=(bank,))
                        X.op("vector", lambda p, m=m, bank=bank: p.tensor_scalar(out=ffc.h[:],
                             in0=bank.h[:], scalar1=cons["w3s"].h[:, m:m + 1], scalar2=None,
                             op0=ALU.mult), reads=(bank, cons["w3s"]), writes=(ffc,))
                        X.op("scalar", lambda p, m=m, t0=t0: p.dma_start(out=x1c2.h[:],
                             in_=d_x1[m * 128:(m + 1) * 128, t0:t0 + TC]),
                             writes=(x1c2,), dma="dL")
                        X.op("scalar", lambda p, m=m, t0=t0: p.dma_start(out=mbc.h[:],
                             in_=d_mamba[m * 128:(m + 1) * 128, t0:t0 + TC]),
                             writes=(mbc,), dma="dL")
                        stg = ost[tau % 2]
                        X.op("vector", lambda p, stg=stg: p.tensor_tensor(out=stg.h[:],
                             in0=ffc.h[:], in1=x1c2.h[:], op=ALU.add),
                             reads=(ffc, x1c2), writes=(stg,))
                        X.op("vector", lambda p, stg=stg: p.tensor_tensor(out=stg.h[:],
                             in0=stg.h[:], in1=mbc.h[:], op=ALU.add),
                             reads=(stg, mbc), writes=(stg,))
                        X.op("scalar", lambda p, m=m, t0=t0, stg=stg: p.dma_start(
                             out=yT[m * 128:(m + 1) * 128, t0:t0 + TC], in_=stg.h[:]),
                             reads=(stg,), dma="dS")

        except _StopBuild:
            pass
        X._flush()
        fin = tuple((s, X.cnt[s], False) for s in ("dS", "dL", "dW"))
        def fin_f(p):
            X._wait_emit("scalar", p, fin)
        block.scalar(fin_f)
        def fin_g(p):
            X._wait_emit("gpsimd", p, fin)
        block.gpsimd(fin_g)

    return nc


def _parcel(WT, n_m):
    K, M = WT.shape
    assert M == n_m * 128
    out = np.empty((n_m, 128, K), np.float32)
    for m in range(n_m):
        blk = WT[:, m * 128:(m + 1) * 128]
        out[m] = blk.reshape(K // 128, 128, 128).transpose(1, 0, 2).reshape(128, K)
    return np.ascontiguousarray(out)


def _vimg(v, nch):
    return np.ascontiguousarray(np.asarray(v, np.float32).reshape(nch, 128).T)


_NC_CACHE = {}


def kernel(**inputs):
    f32 = np.float32
    g = {k: np.asarray(v, f32) for k, v in inputs.items()}
    x = g["L0_attn_x"]

    shared = {
        "Wk0": g["L0_attn_W_key"], "Wv0": g["L0_attn_W_value"], "Wr0": g["L0_attn_W_recept"],
        "Wo0p": _parcel(g["L0_attn_W_out"], NC),
        "Wk2p": _parcel(g["L0_attn_W_key2"], NH),
        "Wr2p": _parcel(g["L0_attn_W_recept2"], NC),
        "Wv2p": _parcel(g["L0_attn_W_value2"], NC),
        "w1p": _parcel(np.ascontiguousarray(g["L0_ffn_w1"].T), NH),
        "w2p": _parcel(np.ascontiguousarray(g["L0_ffn_w2"].T), NC),
        "Winp": _parcel(np.ascontiguousarray(g["L1_attn_W_in"].T), 2 * NDI),
        "WxTi": np.ascontiguousarray(
            g["L1_attn_W_x"].T.reshape(NDI, 128, 32).transpose(1, 0, 2).reshape(128, NDI * 32)),
        "Wdtp": _parcel(np.ascontiguousarray(g["L1_attn_W_dt"].T), NDI),
        "Woutp": _parcel(np.ascontiguousarray(g["L1_attn_W_out"].T), NC),
        "f1p": _parcel(np.ascontiguousarray(g["L1_ffn_w1_w"].T), NH),
        "f2p": _parcel(np.ascontiguousarray(g["L1_ffn_w2_w"].T), NH),
        "h1p": _parcel(np.ascontiguousarray(g["L1_ffn_h1_w"].T), NH),
        "h2p": _parcel(np.ascontiguousarray(g["L1_ffn_h2_w"].T), NH),
        "w3p": _parcel(np.ascontiguousarray(g["L1_ffn_w3_w"].T), NC),
        "v_ln1w": _vimg(g["L0_attn_ln1_w"], NC), "v_ln2w": _vimg(g["L0_attn_ln2_w"], NC),
        "v_td": _vimg(g["L0_attn_time_decay"], NC), "v_tf": _vimg(g["L0_attn_time_first"], NC),
        "v_tmk": _vimg(g["L0_attn_time_mix_k"], NC), "v_tmv": _vimg(g["L0_attn_time_mix_v"], NC),
        "v_tmr": _vimg(g["L0_attn_time_mix_r"], NC),
        "v_tmk2": _vimg(g["L0_attn_time_mix_k2"], NC),
        "v_tmr2": _vimg(g["L0_attn_time_mix_r2"], NC),
        "v_b1": _vimg(g["L0_ffn_b1"], NH), "v_b2": _vimg(g["L0_ffn_b2"], NC),
        "v_cb": _vimg(g["L1_attn_conv_b"], NDI), "v_bdt": _vimg(g["L1_attn_b_dt"], NDI),
        "v_dskip": _vimg(g["L1_attn_D_skip"], NDI),
        "v_lng": _vimg(g["L1_attn_ln_g"], NC), "v_lnb": _vimg(g["L1_attn_ln_b"], NC),
        "v_alog": np.ascontiguousarray(
            g["L1_attn_A_log"].reshape(NDI, 128, NST).transpose(1, 0, 2).reshape(128, NDI * NST)),
        "v_cw": np.ascontiguousarray(
            g["L1_attn_conv_w"][:, 0, :].reshape(NDI, 128, 4).transpose(1, 0, 2).reshape(128, NDI * 4)),
        "v_f1s": _vimg(g["L1_ffn_w1_s"], NH), "v_f2s": _vimg(g["L1_ffn_w2_s"], NH),
        "v_h1s": _vimg(g["L1_ffn_h1_s"], NH), "v_h2s": _vimg(g["L1_ffn_h2_s"], NH),
        "v_w3s": _vimg(g["L1_ffn_w3_s"], NC),
        "v_cvec": np.ascontiguousarray(
            np.broadcast_to(np.asarray(CVALS, f32), (128, len(CVALS)))),
        "ones_col": np.ones((128, 1), ml_dtypes.bfloat16),
        "ones_row": np.ones((1, 128), f32),
    }
    in_maps = []
    for i in range(8):
        d = dict(shared)
        d["xT"] = np.ascontiguousarray(x[i].T)
        in_maps.append(d)

    nc = _NC_CACHE.get("nc")
    if nc is None:
        nc = build()
        _NC_CACHE["nc"] = nc
    res = run_bass_kernel_spmd(nc, in_maps, core_ids=list(range(8)))
    out = np.stack([np.asarray(r["yT"], np.float32).T for r in res.results], axis=0)
    return np.ascontiguousarray(out)


# version 21 (final)
# speedup vs baseline: 1.1116x; 1.0891x over previous
import contextlib
import numpy as np
import ml_dtypes
import concourse.bass as bass
import concourse.mybir as mybir
from concourse.bass_utils import run_bass_kernel_spmd

F32 = mybir.dt.float32
F32R = mybir.dt.float32r
BF16 = mybir.dt.bfloat16
F16 = mybir.dt.float16
FP8 = mybir.dt.float8e4
AF = mybir.ActivationFunctionType
ALU = mybir.AluOpType

T = 2048
TC = 512
NT = T // TC
C = 1024
NC = C // 128
DI = 2048
NDI = DI // 128
H = 4096
NH = H // 128
NST = 16

CVALS = [1.0, -1.0, -0.5, 1e-5, 1e-6, 1e-8, 20.0, -20.0, 0.0]
CIX = {v: i for i, v in enumerate(CVALS)}

SEM_ENG = {"T": "tensor", "V": "vector", "S": "scalar", "G": "gpsimd"}
ENG_SEM = {"tensor": "T", "vector": "V", "scalar": "S", "gpsimd": "G"}


class Buf:
    __slots__ = ("h", "w", "r")

    def __init__(self, h):
        self.h = h
        self.w = None
        self.r = []


class Ctx:
    def __init__(self, nc):
        self.nc = nc
        self.es = contextlib.ExitStack()
        self.block = None
        self.sem = {}
        self.cnt = {}
        for s in ("T", "V", "S", "G", "MS", "dW", "dL", "dS"):
            self.sem[s] = self.es.enter_context(nc.semaphore(s))
            self.cnt[s] = 0
        self.waited = {e: {} for e in ("tensor", "vector", "scalar", "gpsimd", "sync")}
        self.pend_by_eng = {}

    def sb(self, name, shape, dt, scope=None):
        cm = self.nc.sbuf_tensor(name, shape, dt)
        h = scope.enter_context(cm) if scope is not None else self.es.enter_context(cm)
        return Buf(h)

    def _flush(self):
        for eng in ("sync", "tensor", "vector", "scalar", "gpsimd"):
            items = self.pend_by_eng.get(eng)
            if not items:
                continue
            self.pend_by_eng[eng] = []

            def body(p, items=items):
                for it in items:
                    it(p)
            getattr(self.block, eng)(body)

    def _q(self, eng, fn):
        self.pend_by_eng.setdefault(eng, []).append(fn)

    def _wait_emit(self, eng, p, toks, skip_same=True):
        best = {}
        for t in toks:
            if t is None:
                continue
            s, v, short = t
            if skip_same and not short and SEM_ENG.get(s) in (eng,):
                continue
            if v > best.get(s, 0):
                best[s] = v
        for s, v in best.items():
            if v > self.waited[eng].get(s, 0):
                p.wait_ge(self.sem[s], v)
                self.waited[eng][s] = v

    def op(self, eng, fn, reads=(), writes=(), extra=(), dma=None, memset=False,
           short=False):
        toks = []
        for b in reads:
            toks.append(b.w)
        for b in writes:
            toks.append(b.w)
            toks.extend(b.r)
        toks.extend(extra)
        toks = [(s, self.cnt[s] if s in ("MS", "dW", "dL", "dS") else v, sh)
                for (s, v, sh) in toks if toks is not None
                for (s, v, sh) in ((s, v, sh),)] if False else [
            ((t[0], self.cnt[t[0]], t[2]) if t is not None and t[0] in ("MS", "dW", "dL", "dS")
             else t) for t in toks]
        sem = dma if dma else ("MS" if memset else ENG_SEM[eng])
        self.cnt[sem] += 16 if dma else 1
        tok = (sem, self.cnt[sem], bool(short))
        skip_same = not (dma or memset)

        def run(p, fn=fn, toks=tuple(toks), sem=sem, tok=tok, eng=eng, sk=skip_same):
            self._wait_emit(eng, p, toks, skip_same=sk)
            ins = fn(p)
            ins.then_inc(self.sem[sem], 16 if sem in ("dW", "dL", "dS") else 1)
        self._q(eng, run)
        for b in reads:
            b.r.append(tok)
        for b in writes:
            b.w = tok
            b.r = []
        return tok

    def raw(self, eng, fn):
        self._q(eng, fn)

    def tok(self, s):
        return (s, self.cnt[s], False)

    def barrier(self):
        self._flush()
        snap = tuple((s, self.cnt[s], False)
                     for s in ("T", "V", "S", "G", "MS", "dW", "dL", "dS"))
        for e in ("tensor", "vector", "scalar", "gpsimd"):
            def f(p, e=e):
                self._wait_emit(e, p, snap)
            getattr(self.block, e)(f)


class PsumPool:
    def __init__(self, ctx):
        self.banks = []
        for i in range(8):
            h = ctx.es.enter_context(ctx.nc.psum_tensor(f"pb{i}", [128, TC], F32))
            self.banks.append(Buf(h))
        self.idx = 0

    def get(self, n=1):
        out = []
        for _ in range(n):
            out.append(self.banks[self.idx])
            self.idx = (self.idx + 1) % 8
        return out if n > 1 else out[0]


class _StopBuild(Exception):
    pass


def build(debug=False, stop=None):
    nc = bass.Bass("TRN2", target_bir_lowering=False)
    X = Ctx(nc)
    pool = PsumPool(X)

    def dram_in(name, shape, dt=F32):
        return nc.dram_tensor(name, shape, dt, kind="ExternalInput").ap()

    def scratch(name, shape, dt):
        kind = "ExternalOutput" if debug else "Internal"
        return nc.dram_tensor(name, shape, dt, kind=kind).ap()

    xT = dram_in("xT", [C, T])
    Wk0 = dram_in("Wk0", [C, C]); Wv0 = dram_in("Wv0", [C, C]); Wr0 = dram_in("Wr0", [C, C])
    Wo0p = dram_in("Wo0p", [NC, 128, C])
    Wk2p = dram_in("Wk2p", [NH, 128, C])
    Wr2p = dram_in("Wr2p", [NC, 128, C])
    Wv2p = dram_in("Wv2p", [NC, 128, H])
    w1p = dram_in("w1p", [NH, 128, C])
    w2p = dram_in("w2p", [NC, 128, H])
    Winp = dram_in("Winp", [2 * NDI, 128, C])
    WxTi = dram_in("WxTi", [128, NDI * 32])
    Wdtp = dram_in("Wdtp", [NDI, 128, DI])
    Woutp = dram_in("Woutp", [NC, 128, DI])
    f1p = dram_in("f1p", [NH, 128, C])
    f2p = dram_in("f2p", [NH, 128, C])
    h1p = dram_in("h1p", [NH, 128, H])
    h2p = dram_in("h2p", [NH, 128, H])
    w3p = dram_in("w3p", [NC, 128, H])
    VSPEC = [("ln1w", NC), ("ln2w", NC), ("td", NC), ("tf", NC), ("tmk", NC),
             ("tmv", NC), ("tmr", NC), ("tmk2", NC), ("tmr2", NC), ("b1", NH),
             ("b2", NC), ("cb", NDI), ("bdt", NDI), ("dskip", NDI), ("lng", NC),
             ("lnb", NC), ("alog", NDI * NST), ("cw", NDI * 4), ("f1s", NH),
             ("f2s", NH), ("h1s", NH), ("h2s", NH), ("w3s", NC), ("cvec", len(CVALS))]
    vin = {nm: dram_in("v_" + nm, [128, w]) for nm, w in VSPEC}
    ones_col = dram_in("ones_col", [128, 1], BF16)
    ones_row = dram_in("ones_row", [1, 128])
    yT = nc.dram_tensor("yT", [C, T], F32, kind="ExternalOutput").ap()

    d_xn = scratch("d_xn", [C, T], F32)
    d_astr = scratch("d_astr", [C, T], F32)
    d_xnbf = scratch("d_xnbf", [C, T], BF16)
    d_x1 = scratch("d_x1", [C, T], F32)
    d_xn1 = scratch("d_xn1", [C, T], BF16)
    d_k2 = scratch("d_k2", [H, T], BF16)
    d_sres = scratch("d_sres", [DI, T], BF16)
    d_xi = scratch("d_xi", [DI, T], BF16)
    d_delta = scratch("d_delta", [DI, T], F32)
    d_ybf = scratch("d_ybf", [DI, T], BF16)
    d_mamba = scratch("d_mamba", [C, T], F32)
    d_bmat = scratch("d_bmat", [32, T], F32)
    d_hq8 = scratch("d_hq8", [H, T], FP8) if debug else None
    d_wq8 = scratch("d_wq8", [C, H], FP8) if debug else None
    d_ff16 = scratch("d_ff16", [C, T], F16) if debug else None
    d_cbq = scratch("d_cbq", [128, NC], F32) if debug else None
    d_cbq16 = scratch("d_cbq16", [128, NC], F16) if debug else None
    d_ffn0 = scratch("d_ffn0", [C, T], F32) if debug else None
    d_h = scratch("d_h", [H, T], BF16)
    d_g1 = scratch("d_g1", [H, T], BF16)
    d_g2 = scratch("d_g2", [H, T], BF16)

    cons = {nm: X.sb("c_" + nm, [128, w], F32) for nm, w in VSPEC}
    c_ones = X.sb("c_ones", [128, 1], BF16)
    c_onesrow = X.sb("c_onesrow", [1, 128], F32)
    c_onesrow_r = X.sb("c_onesrow_r", [1, 128], F32R)
    c_ew = X.sb("c_ew", [128, NC], F32)
    c_A = X.sb("c_A", [128, NDI * NST], F32)
    c_bq = X.sb("c_bq", [128, NC], F32)
    c_bq16 = X.sb("c_bq16", [128, NC], F16)
    ring = [X.sb(f"ring{i}", [128, 2048], F32) for i in range(2)]
    ring_ix = [0]
    st_s1 = X.sb("st_s1", [1, TC], F32)
    st_s2 = X.sb("st_s2", [1, TC], F32)
    st_m = X.sb("st_m", [1, TC], F32)
    st_inv = X.sb("st_inv", [1, TC], F32)
    st_r1 = X.sb("st_r1", [1, TC], F32R)
    st_r2 = X.sb("st_r2", [1, TC], F32R)
    bc_m = X.sb("bc_m", [128, TC], F32)
    bc_inv = X.sb("bc_inv", [128, TC], F32)
    c_Ast = X.sb("c_Ast", [128, NC], F32)
    c_Bst = X.sb("c_Bst", [128, NC], F32)

    def cv(val, np_=128):
        return cons["cvec"].h[0:np_, CIX[val]:CIX[val] + 1]

    RW = 2048

    def load_parcel(src_ap, dst_buf, K):
        for c0 in range(0, K, RW):
            c1 = min(c0 + RW, K)
            slot = ring[ring_ix[0]]
            ring_ix[0] = (ring_ix[0] + 1) % len(ring)
            X.op("sync", lambda p, s=slot, a=src_ap, c0=c0, c1=c1: p.dma_start(
                 out=s.h[:, 0:c1 - c0], in_=a[:, c0:c1]),
                 reads=(), writes=(slot,), dma="dW")
            X.op("gpsimd", lambda p, s=slot, d=dst_buf, c0=c0, c1=c1: p.tensor_copy(
                 out=d.h[:, c0:c1], in_=s.h[:, 0:c1 - c0]),
                 reads=(slot,), writes=(dst_buf,))

    def stats_tc(sq_tiles, val_tiles, nchunks, eps, want_mean, tcol):
        n = 128 * nchunks
        for which, tiles, dst in ((0, val_tiles, st_s1), (1, sq_tiles, st_s2)):
            if which == 0 and not want_mean:
                continue
            bank = pool.get()
            def mm(p, tiles=tiles, bank=bank):
                for k in range(nchunks):
                    ins = p.matmul(out=bank.h[0:1, :], lhsT=c_ones.h[:], rhs=tiles[k].h[:],
                                   start=(k == 0), stop=(k == nchunks - 1))
                return ins
            X.op("tensor", mm, reads=[c_ones] + list(tiles), writes=(bank,))
            X.op("scalar", lambda p, bank=bank, dst=dst: p.copy(out=dst.h[0:1, :],
                                                                in_=bank.h[0:1, :]),
                 reads=(bank,), writes=(dst,), short=True)
        if want_mean:
            X.op("vector", lambda p: p.tensor_scalar(out=st_m.h[0:1, :], in0=st_s1.h[0:1, :],
                                                     scalar1=1.0 / n, scalar2=None,
                                                     op0=ALU.mult),
                 reads=(st_s1,), writes=(st_m,), short=True)
            X.op("vector", lambda p: p.tensor_tensor(out=st_inv.h[0:1, :], in0=st_m.h[0:1, :],
                                                     in1=st_m.h[0:1, :], op=ALU.mult),
                 reads=(st_m,), writes=(st_inv,), short=True)
            X.op("vector", lambda p: p.tensor_scalar(out=st_s2.h[0:1, :], in0=st_s2.h[0:1, :],
                                                     scalar1=1.0 / n, scalar2=None,
                                                     op0=ALU.mult),
                 reads=(st_s2,), writes=(st_s2,), short=True)
            X.op("vector", lambda p: p.tensor_tensor(out=st_s2.h[0:1, :], in0=st_s2.h[0:1, :],
                                                     in1=st_inv.h[0:1, :], op=ALU.subtract),
                 reads=(st_s2, st_inv), writes=(st_s2,), short=True)
        else:
            X.op("vector", lambda p: p.tensor_scalar(out=st_s2.h[0:1, :], in0=st_s2.h[0:1, :],
                                                     scalar1=1.0 / n, scalar2=None,
                                                     op0=ALU.mult),
                 reads=(st_s2,), writes=(st_s2,), short=True)
        X.op("vector", lambda p, eps=eps: p.tensor_scalar(out=st_s2.h[0:1, :],
                                                          in0=st_s2.h[0:1, :], scalar1=eps,
                                                          scalar2=None, op0=ALU.add),
             reads=(st_s2,), writes=(st_s2,), short=True)
        X.op("scalar", lambda p: p.activation(out=st_inv.h[0:1, :], in_=st_s2.h[0:1, :],
                                              func=AF.Ln),
             reads=(st_s2,), writes=(st_inv,), short=True)
        X.op("scalar", lambda p: p.activation(out=st_inv.h[0:1, :], in_=st_inv.h[0:1, :],
                                              func=AF.Exp, scale=cv(-0.5, 1)),
             reads=(st_inv, cons["cvec"]), writes=(st_inv,), short=True)
        X.op("vector", lambda p: p.tensor_copy(out=st_r2.h[0:1, :], in_=st_inv.h[0:1, :]),
             reads=(st_inv,), writes=(st_r2,), short=True)
        if want_mean:
            X.op("vector", lambda p: p.tensor_copy(out=st_r1.h[0:1, :], in_=st_m.h[0:1, :]),
                 reads=(st_m,), writes=(st_r1,), short=True)
        for row, dstbc, en in ((st_r2, bc_inv, True), (st_r1, bc_m, want_mean)):
            if not en:
                continue
            bank = pool.get()
            X.op("tensor", lambda p, row=row, bank=bank: p.matmul(out=bank.h[:],
                 lhsT=c_onesrow_r.h[:], rhs=row.h[0:1, :], start=True, stop=True),
                 reads=(row, c_onesrow_r), writes=(bank,))
            X.op("scalar", lambda p, bank=bank, d=dstbc: p.copy(out=d.h[:], in_=bank.h[:]),
                 reads=(bank,), writes=(dstbc,))

    with X.es, nc.Block() as block:
        X.block = block
        def ckpt(n):
            if stop is not None and n >= stop:
                raise _StopBuild()
        try:
            for nm, w in VSPEC:
                X.op("sync", lambda p, nm=nm, w=w: p.dma_start(out=cons[nm].h[:, 0:w],
                                                               in_=vin[nm]),
                     writes=(cons[nm],), dma="dW")
            X.op("sync", lambda p: p.dma_start(out=c_ones.h[:], in_=ones_col),
                 writes=(c_ones,), dma="dW")
            X.op("sync", lambda p: p.dma_start(out=c_onesrow.h[:], in_=ones_row),
                 writes=(c_onesrow,), dma="dW")
            X.op("vector", lambda p: p.tensor_copy(out=c_onesrow_r.h[:], in_=c_onesrow.h[:]),
                 reads=(c_onesrow,), writes=(c_onesrow_r,), short=True)
            X.op("vector", lambda p: p.tensor_copy(out=c_bq16.h[:], in_=cons["b2"].h[:, 0:NC]),
                 reads=(cons["b2"],), writes=(c_bq16,), short=True)
            X.op("vector", lambda p: p.tensor_copy(out=c_bq.h[:], in_=c_bq16.h[:]),
                 reads=(c_bq16,), writes=(c_bq,), short=True)
            X.op("vector", lambda p: p.memset(c_Ast.h[:], 0.0), writes=(c_Ast,), memset=True)
            X.op("vector", lambda p: p.memset(c_Bst.h[:], 0.0), writes=(c_Bst,), memset=True)
            if debug:
                X.op("scalar", lambda p: p.dma_start(out=d_cbq, in_=c_bq.h[:]),
                     reads=(c_bq,), dma="dS")
                X.op("scalar", lambda p: p.dma_start(out=d_cbq16, in_=c_bq16.h[:]),
                     reads=(c_bq16,), dma="dS")
            X.op("scalar", lambda p: p.activation(out=c_ew.h[:], in_=cons["td"].h[:, 0:NC],
                                                  func=AF.Exp),
                 reads=(cons["td"],), writes=(c_ew,), short=True)
            X.op("scalar", lambda p: p.activation(out=c_A.h[:], in_=cons["alog"].h[:],
                                                  func=AF.Exp),
                 reads=(cons["alog"],), writes=(c_A,), short=True)
            X.op("vector", lambda p: p.tensor_scalar(out=c_ew.h[:], in0=c_ew.h[:], scalar1=-1.0,
                                                     scalar2=-20.0, op0=ALU.mult, op1=ALU.max),
                 reads=(c_ew,), writes=(c_ew,), short=True)
            X.op("vector", lambda p: p.tensor_scalar(out=c_A.h[:], in0=c_A.h[:], scalar1=-1.0,
                                                     scalar2=None, op0=ALU.mult),
                 reads=(c_A,), writes=(c_A,), short=True)
            X.op("scalar", lambda p: p.activation(out=c_ew.h[:], in_=c_ew.h[:], func=AF.Exp),
                 reads=(c_ew,), writes=(c_ew,), short=True)
            X.barrier()

            ckpt(0)
            ph0 = contextlib.ExitStack()
            with ph0:
                x0t = [X.sb(f"p0x{k}", [128, T], F32, ph0) for k in range(NC)]
                vbf = [X.sb(f"p0vb{k}", [128, TC], BF16, ph0) for k in range(NC)]
                sbf = [X.sb(f"p0sb{k}", [128, TC], BF16, ph0) for k in range(NC)]
                xnc = [X.sb(f"p0c{i}", [128, TC], F32, ph0) for i in range(2)]
                xnb = [X.sb(f"p0b{i}", [128, TC], BF16, ph0) for i in range(2)]
                for k in range(NC):
                    X.op("sync", lambda p, k=k: p.dma_start(out=x0t[k].h[:],
                                                            in_=xT[k * 128:(k + 1) * 128, :]),
                         writes=(x0t[k],), dma="dW")
                for tau in range(NT):
                    t0 = tau * TC
                    for k in range(NC):
                        X.op("vector", lambda p, k=k, t0=t0: p.tensor_copy(out=vbf[k].h[:],
                             in_=x0t[k].h[:, t0:t0 + TC]), reads=(x0t[k],), writes=(vbf[k],))
                        X.op("scalar", lambda p, k=k, t0=t0: p.activation(out=sbf[k].h[:],
                             in_=x0t[k].h[:, t0:t0 + TC], func=AF.Square),
                             reads=(x0t[k],), writes=(sbf[k],))
                    stats_tc(sbf, vbf, NC, 1e-5, True, t0)
                    for k in range(NC):
                        stg = xnc[k % 2]; stgb = xnb[k % 2]
                        X.op("vector", lambda p, k=k, t0=t0, stg=stg: p.tensor_tensor(
                             out=stg.h[:], in0=x0t[k].h[:, t0:t0 + TC],
                             in1=bc_m.h[:], op=ALU.subtract),
                             reads=(x0t[k], bc_m), writes=(stg,))
                        X.op("vector", lambda p, stg=stg: p.tensor_tensor(
                             out=stg.h[:], in0=stg.h[:], in1=bc_inv.h[:], op=ALU.mult),
                             reads=(stg, bc_inv), writes=(stg,))
                        X.op("vector", lambda p, stg=stg, stgb=stgb: p.tensor_copy(
                             out=stgb.h[:], in_=stg.h[:]), reads=(stg,), writes=(stgb,))
                        X.op("scalar", lambda p, k=k, t0=t0, stg=stg: p.dma_start(
                             out=d_xn[k * 128:(k + 1) * 128, t0:t0 + TC], in_=stg.h[:]),
                             reads=(stg,), dma="dS")
                        X.op("scalar", lambda p, k=k, t0=t0, stgb=stgb: p.dma_start(
                             out=d_xnbf[k * 128:(k + 1) * 128, t0:t0 + TC], in_=stgb.h[:]),
                             reads=(stgb,), dma="dS")
            X.barrier()

            ckpt(1)
            ph1 = contextlib.ExitStack()
            with ph1:
                wk = [X.sb(f"p1wk{k}", [128, C], BF16, ph1) for k in range(NC)]
                wv = [X.sb(f"p1wv{k}", [128, C], BF16, ph1) for k in range(NC)]
                wr = [X.sb(f"p1wr{k}", [128, C], BF16, ph1) for k in range(NC)]
                wo_t = [X.sb(f"p1wo{i}", [128, C], BF16, ph1) for i in range(2)]
                xk = [X.sb(f"p1xk{k}", [128, TC], BF16, ph1) for k in range(NC)]
                xv = [X.sb(f"p1xv{k}", [128, TC], BF16, ph1) for k in range(NC)]
                xr = [X.sb(f"p1xr{k}", [128, TC], BF16, ph1) for k in range(NC)]
                tdif = X.sb("p1td", [128, TC], F32, ph1)
                kc = X.sb("p1kc", [128, TC], F32, ph1)
                vc = X.sb("p1vc", [128, TC], F32, ph1)
                tek = X.sb("p1tek", [128, TC], F32, ph1)
                tekt = X.sb("p1tekt", [128, TC], F32, ph1)
                ekv = X.sb("p1ekv", [128, TC], F32, ph1)
                Ab = X.sb("p1Ab", [128, TC + 1], F32, ph1)
                Bb = X.sb("p1Bb", [128, TC + 1], F32, ph1)
                tnum = X.sb("p1num", [128, TC], F32, ph1)
                tden = X.sb("p1den", [128, TC], F32, ph1)
                rc_t = X.sb("p1rc", [128, TC], F32, ph1)
                rw = [X.sb(f"p1rw{k}", [128, TC], BF16, ph1) for k in range(NC)]
                a1st = [X.sb(f"p1a1{i}", [128, TC], F32, ph1) for i in range(2)]
                xnw = [X.sb(f"p1xw{k}", [128, TC + 1], F32, ph1) for k in range(NC)]

                for src, dsts in ((Wk0, wk), (Wv0, wv), (Wr0, wr)):
                    for k in range(NC):
                        load_parcel(src[k * 128:(k + 1) * 128, :], dsts[k], C)

                for tau in range(NT):
                    t0 = tau * TC
                    for k in range(NC):
                        if tau == 0:
                            X.op("vector", lambda p, k=k: p.memset(xnw[k].h[:, 0:1], 0.0),
                                 writes=(xnw[k],), memset=True)
                            X.op("scalar", lambda p, k=k: p.dma_start(out=xnw[k].h[:, 1:TC + 1],
                                 in_=d_xn[k * 128:(k + 1) * 128, 0:TC]),
                                 writes=(xnw[k],), dma="dL")
                        else:
                            X.op("scalar", lambda p, k=k, t0=t0: p.dma_start(out=xnw[k].h[:],
                                 in_=d_xn[k * 128:(k + 1) * 128, t0 - 1:t0 + TC]),
                                 writes=(xnw[k],), dma="dL")
                    for k in range(NC):
                        X.op("vector", lambda p, k=k: p.tensor_tensor(out=tdif.h[:],
                             in0=xnw[k].h[:, 1:TC + 1],
                             in1=xnw[k].h[:, 0:TC], op=ALU.subtract),
                             reads=(xnw[k],), writes=(tdif,))
                        for tmv_, dst in (("tmk", xk), ("tmv", xv), ("tmr", xr)):
                            X.op("vector", lambda p, k=k, tmv_=tmv_, dst=dst:
                                 p.scalar_tensor_tensor(out=dst[k].h[:], in0=tdif.h[:],
                                                        scalar=cons[tmv_].h[:, k:k + 1],
                                                        in1=xnw[k].h[:, 0:TC],
                                                        op0=ALU.mult, op1=ALU.add),
                                 reads=(tdif, cons[tmv_], xnw[k]), writes=(dst[k],))
                    for m in range(NC):
                        bks = {}
                        for nm, wt, xt in (("k", wk, xk), ("v", wv, xv), ("r", wr, xr)):
                            bank = pool.get()
                            def mmf(p, wt=wt, xt=xt, bank=bank, m=m):
                                for kk in range(NC):
                                    ins = p.matmul(out=bank.h[:],
                                                   lhsT=wt[kk].h[:, m * 128:(m + 1) * 128],
                                                   rhs=xt[kk].h[:],
                                                   start=(kk == 0), stop=(kk == NC - 1))
                                return ins
                            X.op("tensor", mmf, reads=list(wt) + list(xt), writes=(bank,))
                            bks[nm] = bank
                        X.op("scalar", lambda p, b=bks["k"]: p.copy(out=kc.h[:], in_=b.h[:]),
                             reads=(bks["k"],), writes=(kc,))
                        X.op("scalar", lambda p, b=bks["v"]: p.copy(out=vc.h[:], in_=b.h[:]),
                             reads=(bks["v"],), writes=(vc,))
                        X.op("vector", lambda p: p.tensor_scalar(out=tek.h[:], in0=kc.h[:],
                             scalar1=-20.0, scalar2=20.0, op0=ALU.max, op1=ALU.min),
                             reads=(kc,), writes=(tek,))
                        X.op("vector", lambda p, m=m: p.tensor_scalar(out=tekt.h[:], in0=kc.h[:],
                             scalar1=cons["tf"].h[:, m:m + 1], scalar2=-20.0,
                             op0=ALU.add, op1=ALU.max),
                             reads=(kc, cons["tf"]), writes=(tekt,))
                        X.op("vector", lambda p: p.tensor_scalar(out=tekt.h[:], in0=tekt.h[:],
                             scalar1=20.0, scalar2=None, op0=ALU.min),
                             reads=(tekt,), writes=(tekt,))
                        X.op("scalar", lambda p: p.activation(out=tek.h[:], in_=tek.h[:],
                             func=AF.Exp), reads=(tek,), writes=(tek,))
                        X.op("scalar", lambda p: p.activation(out=tekt.h[:], in_=tekt.h[:],
                             func=AF.Exp), reads=(tekt,), writes=(tekt,))
                        X.op("vector", lambda p: p.tensor_tensor(out=ekv.h[:], in0=tek.h[:],
                             in1=vc.h[:], op=ALU.mult), reads=(tek, vc), writes=(ekv,))
                        X.op("vector", lambda p, m=m: p.tensor_copy(out=Ab.h[:, 0:1],
                             in_=c_Ast.h[:, m:m + 1]), reads=(c_Ast,), writes=(Ab,), short=True)
                        X.op("vector", lambda p, m=m: p.tensor_copy(out=Bb.h[:, 0:1],
                             in_=c_Bst.h[:, m:m + 1]), reads=(c_Bst,), writes=(Bb,), short=True)
                        X.op("vector", lambda p, m=m: p.tensor_tensor_scan(
                             out=Ab.h[:, 1:TC + 1],
                             data0=c_ew.h[:, m:m + 1].broadcast_to((128, TC)),
                             data1=ekv.h[:], initial=c_Ast.h[:, m:m + 1],
                             op0=ALU.mult, op1=ALU.add),
                             reads=(c_ew, ekv, c_Ast), writes=(Ab,))
                        X.op("vector", lambda p, m=m: p.tensor_tensor_scan(
                             out=Bb.h[:, 1:TC + 1],
                             data0=c_ew.h[:, m:m + 1].broadcast_to((128, TC)),
                             data1=tek.h[:], initial=c_Bst.h[:, m:m + 1],
                             op0=ALU.mult, op1=ALU.add),
                             reads=(c_ew, tek, c_Bst), writes=(Bb,))
                        X.op("vector", lambda p, m=m: p.tensor_copy(out=c_Ast.h[:, m:m + 1],
                             in_=Ab.h[:, TC:TC + 1]), reads=(Ab,), writes=(c_Ast,), short=True)
                        X.op("vector", lambda p, m=m: p.tensor_copy(out=c_Bst.h[:, m:m + 1],
                             in_=Bb.h[:, TC:TC + 1]), reads=(Bb,), writes=(c_Bst,), short=True)
                        X.op("vector", lambda p: p.tensor_tensor(out=tnum.h[:], in0=tekt.h[:],
                             in1=vc.h[:], op=ALU.mult), reads=(tekt, vc), writes=(tnum,))
                        X.op("vector", lambda p: p.tensor_tensor(out=tnum.h[:], in0=tnum.h[:],
                             in1=Ab.h[:, 0:TC], op=ALU.add), reads=(tnum, Ab), writes=(tnum,))
                        X.op("vector", lambda p: p.tensor_tensor(out=tden.h[:], in0=tekt.h[:],
                             in1=Bb.h[:, 0:TC], op=ALU.add), reads=(tekt, Bb), writes=(tden,))
                        X.op("vector", lambda p: p.tensor_scalar(out=tden.h[:], in0=tden.h[:],
                             scalar1=1e-8, scalar2=None, op0=ALU.add),
                             reads=(tden,), writes=(tden,))
                        X.op("vector", lambda p: p.reciprocal(out=tden.h[:], in_=tden.h[:]),
                             reads=(tden,), writes=(tden,))
                        X.op("vector", lambda p: p.tensor_tensor(out=tnum.h[:], in0=tnum.h[:],
                             in1=tden.h[:], op=ALU.mult), reads=(tnum, tden), writes=(tnum,))
                        X.op("scalar", lambda p, b=bks["r"]: p.activation(out=rc_t.h[:],
                             in_=b.h[:], func=AF.Exp, scale=cv(-1.0)),
                             reads=(bks["r"], cons["cvec"]), writes=(rc_t,))
                        X.op("vector", lambda p: p.tensor_scalar(out=rc_t.h[:], in0=rc_t.h[:],
                             scalar1=1.0, scalar2=None, op0=ALU.add),
                             reads=(rc_t,), writes=(rc_t,))
                        X.op("vector", lambda p: p.reciprocal(out=rc_t.h[:], in_=rc_t.h[:]),
                             reads=(rc_t,), writes=(rc_t,))
                        X.op("vector", lambda p, m=m: p.tensor_tensor(out=rw[m].h[:],
                             in0=rc_t.h[:], in1=tnum.h[:], op=ALU.mult),
                             reads=(rc_t, tnum), writes=(rw[m],))
                    for m in range(NC):
                        wob = wo_t[m % 2]
                        load_parcel(Wo0p[m], wob, C)
                        bank = pool.get()
                        def mmo(p, wob=wob, bank=bank):
                            for kk in range(NC):
                                ins = p.matmul(out=bank.h[:],
                                               lhsT=wob.h[:, kk * 128:(kk + 1) * 128],
                                               rhs=rw[kk].h[:], start=(kk == 0), stop=(kk == NC - 1))
                            return ins
                        X.op("tensor", mmo, reads=[wob] + list(rw), writes=(bank,))
                        stg = a1st[m % 2]
                        X.op("vector", lambda p, bank=bank, m=m, stg=stg: p.tensor_tensor(
                             out=stg.h[:], in0=bank.h[:],
                             in1=xnw[m].h[:, 1:TC + 1], op=ALU.add),
                             reads=(bank, xnw[m]), writes=(stg,))
                        X.op("scalar", lambda p, m=m, t0=t0, stg=stg: p.dma_start(
                             out=d_astr[m * 128:(m + 1) * 128, t0:t0 + TC], in_=stg.h[:]),
                             reads=(stg,), dma="dS")
            X.barrier()

            ckpt(2)
            ph2 = contextlib.ExitStack()
            with ph2:
                a1t = [X.sb(f"p2a1{k}", [128, TC], F32, ph2) for k in range(NC)]
                a1sq = [X.sb(f"p2sq{k}", [128, TC], BF16, ph2) for k in range(NC)]
                tdif2 = X.sb("p2td", [128, TC], F32, ph2)
                r2f = [X.sb(f"p2r2{k}", [128, T], F16, ph2) for k in range(NC)]
                tmpf = X.sb("p2tmp", [128, TC], F32, ph2)
                k2st = X.sb("p2k2st", [128, T], BF16, ph2)
                ph2m = contextlib.ExitStack()
                xk2 = [X.sb(f"p2xk{k}", [128, T], BF16, ph2m) for k in range(NC)]
                xr2 = [X.sb(f"p2xr{k}", [128, T], BF16, ph2m) for k in range(NC)]
                h2b = [X.sb(f"p2h2{k}", [128, TC + 1], F32, ph2m) for k in range(NC)]
                for k in range(NC):
                    X.op("vector", lambda p, k=k: p.memset(h2b[k].h[:, 0:1], 0.0),
                         writes=(h2b[k],), memset=True)
                for tau in range(NT):
                    t0 = tau * TC
                    for k in range(NC):
                        X.op("scalar", lambda p, k=k, t0=t0: p.dma_start(out=a1t[k].h[:],
                             in_=d_astr[k * 128:(k + 1) * 128, t0:t0 + TC]),
                             writes=(a1t[k],), dma="dL")
                    for k in range(NC):
                        X.op("scalar", lambda p, k=k: p.activation(out=a1sq[k].h[:],
                             in_=a1t[k].h[:], func=AF.Square), reads=(a1t[k],), writes=(a1sq[k],))
                    stats_tc(a1sq, None, NC, 1e-6, False, t0)
                    for k in range(NC):
                        X.op("vector", lambda p, k=k: p.scalar_tensor_tensor(
                             out=h2b[k].h[:, 1:TC + 1], in0=a1t[k].h[:],
                             scalar=cons["ln2w"].h[:, k:k + 1], in1=bc_inv.h[:],
                             op0=ALU.mult, op1=ALU.mult),
                             reads=(a1t[k], cons["ln2w"], bc_inv), writes=(h2b[k],))
                    for k in range(NC):
                        X.op("vector", lambda p, k=k: p.tensor_tensor(out=tdif2.h[:],
                             in0=h2b[k].h[:, 1:TC + 1], in1=h2b[k].h[:, 0:TC], op=ALU.subtract),
                             reads=(h2b[k],), writes=(tdif2,))
                        for tm, dst in (("tmk2", xk2), ("tmr2", xr2)):
                            X.op("vector", lambda p, k=k, t0=t0, tm=tm, dst=dst:
                                 p.scalar_tensor_tensor(out=dst[k].h[:, t0:t0 + TC],
                                                        in0=tdif2.h[:],
                                                        scalar=cons[tm].h[:, k:k + 1],
                                                        in1=h2b[k].h[:, 0:TC],
                                                        op0=ALU.mult, op1=ALU.add),
                                 reads=(tdif2, cons[tm], h2b[k]), writes=(dst[k],))
                        X.op("vector", lambda p, k=k: p.tensor_copy(out=h2b[k].h[:, 0:1],
                             in_=h2b[k].h[:, TC:TC + 1]), reads=(h2b[k],), writes=(h2b[k],), short=True)
                wr2t = [X.sb(f"p2wr2{i}", [128, C], BF16, ph2m) for i in range(2)]
                for m in range(NC):
                    wrt = wr2t[m % 2]
                    load_parcel(Wr2p[m], wrt, C)
                    for tau in range(NT):
                        t0 = tau * TC
                        bank = pool.get()
                        def mm(p, m=m, t0=t0, bank=bank, wrt=wrt):
                            for kk in range(NC):
                                ins = p.matmul(out=bank.h[:],
                                               lhsT=wrt.h[:, kk * 128:(kk + 1) * 128],
                                               rhs=xr2[kk].h[:, t0:t0 + TC],
                                               start=(kk == 0), stop=(kk == NC - 1))
                            return ins
                        X.op("tensor", mm, reads=[wrt] + list(xr2), writes=(bank,))
                        X.op("scalar", lambda p, bank=bank: p.activation(out=tmpf.h[:],
                             in_=bank.h[:], func=AF.Exp, scale=cv(-1.0)),
                             reads=(bank, cons["cvec"]), writes=(tmpf,))
                        X.op("vector", lambda p: p.tensor_scalar(out=tmpf.h[:], in0=tmpf.h[:],
                             scalar1=1.0, scalar2=None, op0=ALU.add),
                             reads=(tmpf,), writes=(tmpf,))
                        X.op("vector", lambda p: p.reciprocal(out=tmpf.h[:], in_=tmpf.h[:]),
                             reads=(tmpf,), writes=(tmpf,))
                        X.op("vector", lambda p, m=m, t0=t0: p.tensor_copy(
                             out=r2f[m].h[:, t0:t0 + TC], in_=tmpf.h[:]),
                             reads=(tmpf,), writes=(r2f[m],))
                wk2t = [X.sb(f"p2wk2{i}", [128, C], BF16, ph2m) for i in range(2)]
                for m in range(NH):
                    wt = wk2t[m % 2]
                    load_parcel(Wk2p[m], wt, C)
                    for tau in range(NT):
                        t0 = tau * TC
                        bank = pool.get()
                        def mm(p, wt=wt, t0=t0, bank=bank):
                            for kk in range(NC):
                                ins = p.matmul(out=bank.h[:],
                                               lhsT=wt.h[:, kk * 128:(kk + 1) * 128],
                                               rhs=xk2[kk].h[:, t0:t0 + TC],
                                               start=(kk == 0), stop=(kk == NC - 1))
                            return ins
                        X.op("tensor", mm, reads=[wt] + list(xk2), writes=(bank,))
                        X.op("scalar", lambda p, bank=bank: p.copy(out=tmpf.h[:], in_=bank.h[:]),
                             reads=(bank,), writes=(tmpf,))
                        X.op("vector", lambda p, t0=t0: p.scalar_tensor_tensor(
                             out=k2st.h[:, t0:t0 + TC], in0=tmpf.h[:], scalar=0.0,
                             in1=tmpf.h[:], op0=ALU.max, op1=ALU.mult),
                             reads=(tmpf,), writes=(k2st,))
                    X.op("scalar", lambda p, m=m: p.dma_start(out=d_k2[m * 128:(m + 1) * 128, :],
                         in_=k2st.h[:]), reads=(k2st,), dma="dS")
                X.barrier()
                ph2m.close()
                wv2t = [X.sb(f"p2wv2{k}", [128, H], BF16, ph2) for k in range(NC)]
                k2c = [X.sb(f"p2k2c{i}", [128, TC], BF16, ph2) for i in range(2)]
                outst = [X.sb(f"p2o{i}", [128, TC], F32, ph2) for i in range(2)]
                for m in range(NC):
                    load_parcel(Wv2p[m], wv2t[m], H)
                for tau in range(NT):
                    t0 = tau * TC
                    banks = pool.get(8)
                    for kk in range(NH):
                        kb = k2c[kk % 2]
                        X.op("scalar", lambda p, kk=kk, t0=t0, kb=kb: p.dma_start(out=kb.h[:],
                             in_=d_k2[kk * 128:(kk + 1) * 128, t0:t0 + TC]),
                             writes=(kb,), dma="dL")
                        def mm(p, kk=kk, kb=kb, banks=banks):
                            for m in range(NC):
                                ins = p.matmul(out=banks[m].h[:],
                                               lhsT=wv2t[m].h[:, kk * 128:(kk + 1) * 128],
                                               rhs=kb.h[:], start=(kk == 0), stop=(kk == NH - 1))
                            return ins
                        X.op("tensor", mm, reads=[kb] + wv2t, writes=tuple(banks))
                    for m in range(NC):
                        X.op("scalar", lambda p, m=m, t0=t0: p.dma_start(out=a1t[m].h[:],
                             in_=d_astr[m * 128:(m + 1) * 128, t0:t0 + TC]),
                             writes=(a1t[m],), dma="dL")
                        stg = outst[m % 2]
                        X.op("vector", lambda p, m=m, t0=t0, banks=banks, stg=stg:
                             p.tensor_tensor(out=stg.h[:], in0=banks[m].h[:],
                                             in1=r2f[m].h[:, t0:t0 + TC], op=ALU.mult),
                             reads=(banks[m], r2f[m]), writes=(stg,))
                        X.op("vector", lambda p, m=m, stg=stg: p.tensor_tensor(out=stg.h[:],
                             in0=stg.h[:], in1=a1t[m].h[:], op=ALU.add),
                             reads=(stg, a1t[m]), writes=(stg,))
                        X.op("scalar", lambda p, m=m, t0=t0, stg=stg: p.dma_start(
                             out=d_astr[m * 128:(m + 1) * 128, t0:t0 + TC], in_=stg.h[:]),
                             reads=(stg,), dma="dS")
            X.barrier()

            ckpt(3)
            ph3 = contextlib.ExitStack()
            with ph3:
                hq8 = [X.sb(f"p3h8{m}", [128, T], FP8, ph3) for m in range(NH)]
                ffst = X.sb("p3ff", [128, TC], F16, ph3)
                ffs32 = X.sb("p3ff32", [128, TC], F32, ph3)
                x1t = [X.sb(f"p3x1{k}", [128, TC], F32, ph3) for k in range(NC)]
                x1sq = [X.sb(f"p3sq{k}", [128, TC], BF16, ph3) for k in range(NC)]
                x1v = [X.sb(f"p3vv{k}", [128, TC], BF16, ph3) for k in range(NC)]
                a1c = X.sb("p3a1", [128, TC], F32, ph3)
                x0c = X.sb("p3x0", [128, TC], F32, ph3)
                xn1st = [X.sb(f"p3xo{i}", [128, TC], BF16, ph3) for i in range(2)]
                dbgf = X.sb("p3dbg", [128, TC], F32, ph3)
                ph3a = contextlib.ExitStack()
                xnbf = [X.sb(f"p3xn{k}", [128, T], BF16, ph3a) for k in range(NC)]
                w1t = [X.sb(f"p3w1{i}", [128, C], BF16, ph3a) for i in range(2)]
                for k in range(NC):
                    X.op("scalar", lambda p, k=k: p.dma_start(out=xnbf[k].h[:],
                         in_=d_xnbf[k * 128:(k + 1) * 128, :]), writes=(xnbf[k],), dma="dL")
                for m in range(NH):
                    wt = w1t[m % 2]
                    load_parcel(w1p[m], wt, C)
                    banks3 = pool.get(4)
                    def mm(p, wt=wt, banks3=banks3):
                        for kk in range(NC):
                            for tau in range(NT):
                                ins = p.matmul(out=banks3[tau].h[:],
                                               lhsT=wt.h[:, kk * 128:(kk + 1) * 128],
                                               rhs=xnbf[kk].h[:, tau * TC:(tau + 1) * TC],
                                               start=(kk == 0), stop=(kk == NC - 1))
                        return ins
                    X.op("tensor", mm, reads=[wt] + xnbf, writes=tuple(banks3))
                    for tau in range(NT):
                        t0 = tau * TC
                        X.op("scalar", lambda p, m=m, t0=t0, bank=banks3[tau]: p.activation(
                             out=hq8[m].h[:, t0:t0 + TC], in_=bank.h[:], func=AF.Relu,
                             bias=cons["b1"].h[:, m:m + 1]),
                             reads=(banks3[tau], cons["b1"]), writes=(hq8[m],))
                    if debug:
                        X.op("scalar", lambda p, m=m: p.dma_start(
                             out=d_hq8[m * 128:(m + 1) * 128, :], in_=hq8[m].h[:]),
                             reads=(hq8[m],), dma="dS")
                X.barrier()
                ph3a.close()
                wq8 = [X.sb(f"p3w2{m}", [128, H], FP8, ph3) for m in range(NC)]
                for m in range(NC):
                    load_parcel(w2p[m], wq8[m], H)
                    if debug:
                        X.op("scalar", lambda p, m=m: p.dma_start(
                             out=d_wq8[m * 128:(m + 1) * 128, :], in_=wq8[m].h[:]),
                             reads=(wq8[m],), dma="dS")
                for tau in range(NT):
                    t0 = tau * TC
                    banks = pool.get(8)
                    for kk in range(NH):
                        def mm(p, kk=kk, t0=t0, banks=banks):
                            for m in range(NC):
                                ins = p.matmul(out=banks[m].h[:],
                                               lhsT=wq8[m].h[:, kk * 128:(kk + 1) * 128],
                                               rhs=hq8[kk].h[:, t0:t0 + TC],
                                               start=(kk == 0), stop=(kk == NH - 1))
                            return ins
                        X.op("tensor", mm, reads=[hq8[kk]] + wq8, writes=tuple(banks))
                    for m in range(NC):
                        X.op("vector", lambda p, m=m, banks=banks: p.tensor_scalar(
                             out=ffs32.h[:], in0=banks[m].h[:], scalar1=c_bq.h[:, m:m + 1],
                             scalar2=None, op0=ALU.add),
                             reads=(banks[m], c_bq), writes=(ffs32,))
                        X.op("vector", lambda p: p.tensor_copy(out=ffst.h[:], in_=ffs32.h[:]),
                             reads=(ffs32,), writes=(ffst,))
                        if debug:
                            X.op("scalar", lambda p, m=m, t0=t0: p.dma_start(
                                 out=d_ff16[m * 128:(m + 1) * 128, t0:t0 + TC], in_=ffst.h[:]),
                                 reads=(ffst,), dma="dS")
                        if debug:
                            X.op("scalar", lambda p, m=m, t0=t0, banks=banks: p.copy(
                                 out=dbgf.h[:], in_=banks[m].h[:]),
                                 reads=(banks[m],), writes=(dbgf,))
                            X.op("scalar", lambda p, m=m, t0=t0: p.dma_start(
                                 out=d_ffn0[m * 128:(m + 1) * 128, t0:t0 + TC], in_=dbgf.h[:]),
                                 reads=(dbgf,), dma="dS")
                        X.op("scalar", lambda p, m=m, t0=t0: p.dma_start(out=a1c.h[:],
                             in_=d_astr[m * 128:(m + 1) * 128, t0:t0 + TC]),
                             writes=(a1c,), dma="dL")
                        X.op("scalar", lambda p, m=m, t0=t0: p.dma_start(out=x0c.h[:],
                             in_=xT[m * 128:(m + 1) * 128, t0:t0 + TC]),
                             writes=(x0c,), dma="dL")
                        X.op("vector", lambda p, m=m: p.tensor_tensor(out=x1t[m].h[:],
                             in0=a1c.h[:], in1=x0c.h[:], op=ALU.add),
                             reads=(a1c, x0c), writes=(x1t[m],))
                        X.op("vector", lambda p, m=m: p.tensor_tensor(out=x1t[m].h[:],
                             in0=x1t[m].h[:], in1=ffst.h[:], op=ALU.add),
                             reads=(x1t[m], ffst), writes=(x1t[m],))
                        X.op("scalar", lambda p, m=m, t0=t0: p.dma_start(
                             out=d_x1[m * 128:(m + 1) * 128, t0:t0 + TC], in_=x1t[m].h[:]),
                             reads=(x1t[m],), dma="dS")
                    for k in range(NC):
                        X.op("vector", lambda p, k=k: p.tensor_copy(out=x1v[k].h[:],
                             in_=x1t[k].h[:]), reads=(x1t[k],), writes=(x1v[k],))
                        X.op("scalar", lambda p, k=k: p.activation(out=x1sq[k].h[:],
                             in_=x1t[k].h[:], func=AF.Square), reads=(x1t[k],), writes=(x1sq[k],))
                    stats_tc(x1sq, x1v, NC, 1e-5, True, t0)
                    for k in range(NC):
                        stg = xn1st[k % 2]
                        X.op("vector", lambda p, k=k: p.tensor_tensor(out=x1t[k].h[:],
                             in0=x1t[k].h[:], in1=bc_m.h[:], op=ALU.subtract),
                             reads=(x1t[k], bc_m), writes=(x1t[k],))
                        X.op("vector", lambda p, k=k, stg=stg: p.tensor_tensor(out=stg.h[:],
                             in0=x1t[k].h[:], in1=bc_inv.h[:], op=ALU.mult),
                             reads=(x1t[k], bc_inv), writes=(stg,))
                        X.op("scalar", lambda p, k=k, t0=t0, stg=stg: p.dma_start(
                             out=d_xn1[k * 128:(k + 1) * 128, t0:t0 + TC], in_=stg.h[:]),
                             reads=(stg,), dma="dS")
            X.barrier()

            ckpt(4)
            ph45 = contextlib.ExitStack()
            if True:
                xn1 = [X.sb(f"p4xn{k}", [128, T], BF16, ph45) for k in range(NC)]
                xi_bf = [X.sb(f"p4xi{m}", [128, T], BF16, ph45) for m in range(NDI)]
                xipre = X.sb("p4xp", [128, T + 3], F32, ph45)
                xc_b = X.sb("p4xc", [128, T], F32, ph45)
                wt2 = [X.sb(f"p4w{i}", [128, C], BF16, ph45) for i in range(2)]
                srst = X.sb("p4sr", [128, T], BF16, ph45)
                wxt = X.sb("p4wx", [128, NDI * 32], BF16, ph45)
                bmat = X.sb("p4bm", [32, T], F32, ph45)
                for k in range(NC):
                    X.op("scalar", lambda p, k=k: p.dma_start(out=xn1[k].h[:],
                         in_=d_xn1[k * 128:(k + 1) * 128, :]), writes=(xn1[k],), dma="dL")
                X.op("vector", lambda p: p.memset(xipre.h[:, 0:3], 0.0), writes=(xipre,),
                     memset=True)
                for m in range(2 * NDI):
                    wt = wt2[m % 2]
                    load_parcel(Winp[m], wt, C)
                    bks = pool.get(4)
                    def mm(p, wt=wt, bks=bks):
                        for kk in range(NC):
                            for tau in range(NT):
                                ins = p.matmul(out=bks[tau].h[:],
                                               lhsT=wt.h[:, kk * 128:(kk + 1) * 128],
                                               rhs=xn1[kk].h[:, tau * TC:(tau + 1) * TC],
                                               start=(kk == 0), stop=(kk == NC - 1))
                        return ins
                    X.op("tensor", mm, reads=[wt] + xn1, writes=tuple(bks))
                    if m < NDI:
                        for tau in range(NT):
                            X.op("scalar", lambda p, tau=tau, bks=bks: p.copy(
                                 out=xipre.h[:, 3 + tau * TC:3 + (tau + 1) * TC],
                                 in_=bks[tau].h[:]), reads=(bks[tau],), writes=(xipre,))
                        X.op("vector", lambda p, m=m: p.tensor_scalar(out=xc_b.h[:],
                             in0=xipre.h[:, 3:3 + T], scalar1=cons["cw"].h[:, 4 * m + 3:4 * m + 4],
                             scalar2=None, op0=ALU.mult),
                             reads=(xipre, cons["cw"]), writes=(xc_b,))
                        for j in range(3):
                            X.op("vector", lambda p, m=m, j=j: p.scalar_tensor_tensor(
                                 out=xc_b.h[:], in0=xipre.h[:, j:j + T],
                                 scalar=cons["cw"].h[:, 4 * m + j:4 * m + j + 1],
                                 in1=xc_b.h[:], op0=ALU.mult, op1=ALU.add),
                                 reads=(xipre, cons["cw"], xc_b), writes=(xc_b,))
                        X.op("scalar", lambda p, m=m: p.activation(out=xi_bf[m].h[:],
                             in_=xc_b.h[:], func=AF.Silu, bias=cons["cb"].h[:, m:m + 1]),
                             reads=(xc_b, cons["cb"]), writes=(xi_bf[m],))
                    else:
                        mm_ = m - NDI
                        for tau in range(NT):
                            X.op("scalar", lambda p, tau=tau, bks=bks: p.activation(
                                 out=srst.h[:, tau * TC:(tau + 1) * TC], in_=bks[tau].h[:],
                                 func=AF.Silu), reads=(bks[tau],), writes=(srst,))
                        X.op("scalar", lambda p, mm_=mm_: p.dma_start(
                             out=d_sres[mm_ * 128:(mm_ + 1) * 128, :], in_=srst.h[:]),
                             reads=(srst,), dma="dS")
                load_parcel(WxTi, wxt, NDI * 32)
                for tau in range(NT):
                    t0 = tau * TC
                    bank = pool.get()
                    def mmx(p, t0=t0, bank=bank):
                        for kk in range(NDI):
                            ins = p.matmul(out=bank.h[0:32, :],
                                           lhsT=wxt.h[:, kk * 32:(kk + 1) * 32],
                                           rhs=xi_bf[kk].h[:, t0:t0 + TC],
                                           start=(kk == 0), stop=(kk == NDI - 1))
                        return ins
                    X.op("tensor", mmx, reads=[wxt] + xi_bf, writes=(bank,))
                    X.op("scalar", lambda p, t0=t0, bank=bank: p.copy(
                         out=bmat.h[0:32, t0:t0 + TC], in_=bank.h[0:32, :]),
                         reads=(bank,), writes=(bmat,))
                X.op("scalar", lambda p: p.dma_start(out=d_bmat, in_=bmat.h[0:32, :]),
                     reads=(bmat,), dma="dS")
            X.barrier()

            ckpt(5)
            if True:
                wdt = [X.sb(f"p5w{i}", [128, DI], BF16, ph45) for i in range(2)]
                dst_ = X.sb("p5d", [128, T], F32, ph45)
                for m in range(NDI):
                    wt = wdt[m % 2]
                    load_parcel(Wdtp[m], wt, DI)
                    banks5 = pool.get(4)
                    def mm(p, wt=wt, banks5=banks5):
                        for kk in range(NDI):
                            for tau in range(NT):
                                ins = p.matmul(out=banks5[tau].h[:],
                                               lhsT=wt.h[:, kk * 128:(kk + 1) * 128],
                                               rhs=xi_bf[kk].h[:, tau * TC:(tau + 1) * TC],
                                               start=(kk == 0), stop=(kk == NDI - 1))
                        return ins
                    X.op("tensor", mm, reads=[wt] + xi_bf, writes=tuple(banks5))
                    for tau in range(NT):
                        t0 = tau * TC
                        X.op("scalar", lambda p, m=m, t0=t0, bank=banks5[tau]: p.activation(
                             out=dst_.h[:, t0:t0 + TC], in_=bank.h[:], func=AF.Exp,
                             bias=cons["bdt"].h[:, m:m + 1]),
                             reads=(banks5[tau], cons["bdt"]), writes=(dst_,))
                        X.op("scalar", lambda p, t0=t0: p.activation(
                             out=dst_.h[:, t0:t0 + TC], in_=dst_.h[:, t0:t0 + TC],
                             func=AF.Ln, bias=1.0), reads=(dst_,), writes=(dst_,))
                    X.op("scalar", lambda p, m=m: p.dma_start(
                         out=d_delta[m * 128:(m + 1) * 128, :], in_=dst_.h[:]),
                         reads=(dst_,), dma="dS")
                for m in range(NDI):
                    X.op("scalar", lambda p, m=m: p.dma_start(
                         out=d_xi[m * 128:(m + 1) * 128, :], in_=xi_bf[m].h[:]),
                         reads=(xi_bf[m],), dma="dS")
            X.barrier()
            ph45.close()

            ckpt(6)
            ph6 = contextlib.ExitStack()
            with ph6:
                brep = [X.sb(f"p6b{n}", [128, T], BF16, ph6) for n in range(NST)]
                brow = X.sb("p6br", [1, T], F32, ph6)
                brow_r = X.sb("p6brr", [1, T], F32R, ph6)
                dl = [X.sb(f"p6d{i}", [128, T], F32, ph6) for i in range(2)]
                xic = [X.sb(f"p6x{i}", [128, T], BF16, ph6) for i in range(2)]
                src = [X.sb(f"p6s{i}", [128, T], BF16, ph6) for i in range(2)]
                ub = X.sb("p6u", [128, T], BF16, ph6)
                ab = [X.sb(f"p6a{i}", [128, T], F32, ph6) for i in range(2)]
                bb = [X.sb(f"p6bb{i}", [128, T], F32, ph6) for i in range(2)]
                hb = X.sb("p6h", [128, T], F32, ph6)
                hc = X.sb("p6hc", [128, T], BF16, ph6)
                tb = X.sb("p6t", [128, T], BF16, ph6)
                yb = X.sb("p6y", [128, T], BF16, ph6)
                ybfst = ub
                for n in range(NST):
                    X.op("scalar", lambda p, n=n: p.dma_start(out=brow.h[0:1, :],
                         in_=d_bmat[16 + n:17 + n, :]), writes=(brow,), dma="dL")
                    X.op("vector", lambda p: p.tensor_copy(out=brow_r.h[0:1, :],
                         in_=brow.h[0:1, :]), reads=(brow,), writes=(brow_r,), short=True)
                    for tau in range(NT):
                        t0 = tau * TC
                        bank = pool.get()
                        X.op("tensor", lambda p, n=n, t0=t0, bank=bank: p.matmul(
                             out=bank.h[:], lhsT=c_onesrow_r.h[:],
                             rhs=brow_r.h[0:1, t0:t0 + TC], start=True, stop=True),
                             reads=(brow_r, c_onesrow_r), writes=(bank,))
                        X.op("scalar", lambda p, n=n, t0=t0, bank=bank: p.copy(
                             out=brep[n].h[:, t0:t0 + TC], in_=bank.h[:]),
                             reads=(bank,), writes=(brep[n],))
                for c in range(NDI):
                    dlc = dl[c % 2]; xcc = xic[c % 2]; srr = src[c % 2]
                    X.op("scalar", lambda p, c=c, dlc=dlc: p.dma_start(out=dlc.h[:],
                         in_=d_delta[c * 128:(c + 1) * 128, :]), writes=(dlc,), dma="dL")
                    X.op("scalar", lambda p, c=c, xcc=xcc: p.dma_start(out=xcc.h[:],
                         in_=d_xi[c * 128:(c + 1) * 128, :]), writes=(xcc,), dma="dL")
                    X.op("scalar", lambda p, c=c, s=srr: p.dma_start(out=s.h[:],
                         in_=d_sres[c * 128:(c + 1) * 128, :]), writes=(srr,), dma="dL")
                    X.op("vector", lambda p, dlc=dlc, xcc=xcc: p.tensor_tensor(out=ub.h[:],
                         in0=dlc.h[:], in1=xcc.h[:], op=ALU.mult),
                         reads=(dlc, xcc), writes=(ub,))
                    for n in range(NST):
                        av = ab[n % 2]
                        bv = bb[n % 2]
                        X.op("scalar", lambda p, c=c, n=n, dlc=dlc, av=av: p.activation(
                             out=av.h[:], in_=dlc.h[:], func=AF.Exp,
                             scale=c_A.h[:, c * NST + n:c * NST + n + 1]),
                             reads=(dlc, c_A), writes=(av,))
                        X.op("vector", lambda p, n=n, bv=bv: p.tensor_tensor(out=bv.h[:],
                             in0=ub.h[:], in1=brep[n].h[:], op=ALU.mult),
                             reads=(ub, brep[n]), writes=(bv,))
                        X.op("vector", lambda p, av=av, bv=bv: p.tensor_tensor_scan(
                             out=hb.h[:], data0=av.h[:], data1=bv.h[:], initial=0.0,
                             op0=ALU.mult, op1=ALU.add), reads=(av, bv), writes=(hb,))
                        X.op("vector", lambda p: p.tensor_copy(out=hc.h[:], in_=hb.h[:]),
                             reads=(hb,), writes=(hc,))
                        X.op("vector", lambda p, n=n: p.tensor_tensor(out=tb.h[:],
                             in0=hc.h[:], in1=brep[n].h[:], op=ALU.mult),
                             reads=(hc, brep[n]), writes=(tb,))
                        if n == 0:
                            X.op("vector", lambda p: p.tensor_copy(out=yb.h[:],
                                 in_=tb.h[:]), reads=(tb,), writes=(yb,))
                        else:
                            X.op("vector", lambda p: p.tensor_tensor(out=yb.h[:],
                                 in0=yb.h[:], in1=tb.h[:], op=ALU.add),
                                 reads=(yb, tb), writes=(yb,))
                    X.op("vector", lambda p, c=c, xcc=xcc: p.scalar_tensor_tensor(out=tb.h[:],
                         in0=xcc.h[:], scalar=cons["dskip"].h[:, c:c + 1], in1=yb.h[:],
                         op0=ALU.mult, op1=ALU.add),
                         reads=(xcc, cons["dskip"], yb), writes=(tb,))
                    X.op("vector", lambda p, s=srr: p.tensor_tensor(out=ybfst.h[:],
                         in0=tb.h[:], in1=s.h[:], op=ALU.mult),
                         reads=(tb, srr), writes=(ybfst,))
                    X.op("scalar", lambda p, c=c: p.dma_start(
                         out=d_ybf[c * 128:(c + 1) * 128, :], in_=ybfst.h[:]),
                         reads=(ybfst,), dma="dS")
            X.barrier()

            ckpt(7)
            ph7 = contextlib.ExitStack()
            with ph7:
                wot = [X.sb(f"p7w{m}", [128, DI], BF16, ph7) for m in range(NC)]
                ybc = [X.sb(f"p7y{i}", [128, TC], BF16, ph7) for i in range(2)]
                zt = [X.sb(f"p7z{k}", [128, TC], F32, ph7) for k in range(NC)]
                zsq = [X.sb(f"p7sq{k}", [128, TC], BF16, ph7) for k in range(NC)]
                zv = [X.sb(f"p7vv{k}", [128, TC], BF16, ph7) for k in range(NC)]
                x1c = X.sb("p7x1", [128, TC], F32, ph7)
                mst = [X.sb(f"p7m{i}", [128, TC], F32, ph7) for i in range(2)]
                for m in range(NC):
                    load_parcel(Woutp[m], wot[m], DI)
                for tau in range(NT):
                    t0 = tau * TC
                    banks = pool.get(8)
                    for kk in range(NDI):
                        kb = ybc[kk % 2]
                        X.op("scalar", lambda p, kk=kk, t0=t0, kb=kb: p.dma_start(out=kb.h[:],
                             in_=d_ybf[kk * 128:(kk + 1) * 128, t0:t0 + TC]),
                             writes=(kb,), dma="dL")
                        def mm(p, kk=kk, kb=kb, banks=banks):
                            for m in range(NC):
                                ins = p.matmul(out=banks[m].h[:],
                                               lhsT=wot[m].h[:, kk * 128:(kk + 1) * 128],
                                               rhs=kb.h[:], start=(kk == 0), stop=(kk == NDI - 1))
                            return ins
                        X.op("tensor", mm, reads=[kb] + wot, writes=tuple(banks))
                    for m in range(NC):
                        X.op("scalar", lambda p, m=m, t0=t0: p.dma_start(out=x1c.h[:],
                             in_=d_x1[m * 128:(m + 1) * 128, t0:t0 + TC]),
                             writes=(x1c,), dma="dL")
                        X.op("vector", lambda p, m=m, banks=banks: p.tensor_tensor(
                             out=zt[m].h[:], in0=banks[m].h[:], in1=x1c.h[:], op=ALU.add),
                             reads=(banks[m], x1c), writes=(zt[m],))
                    for k in range(NC):
                        X.op("vector", lambda p, k=k: p.tensor_copy(out=zv[k].h[:],
                             in_=zt[k].h[:]), reads=(zt[k],), writes=(zv[k],))
                        X.op("scalar", lambda p, k=k: p.activation(out=zsq[k].h[:],
                             in_=zt[k].h[:], func=AF.Square), reads=(zt[k],), writes=(zsq[k],))
                    stats_tc(zsq, zv, NC, 1e-5, True, t0)
                    for k in range(NC):
                        stg = mst[k % 2]
                        X.op("vector", lambda p, k=k: p.tensor_tensor(out=zt[k].h[:],
                             in0=zt[k].h[:], in1=bc_m.h[:], op=ALU.subtract),
                             reads=(zt[k], bc_m), writes=(zt[k],))
                        X.op("vector", lambda p, k=k: p.tensor_tensor(out=zt[k].h[:],
                             in0=zt[k].h[:], in1=bc_inv.h[:], op=ALU.mult),
                             reads=(zt[k], bc_inv), writes=(zt[k],))
                        X.op("vector", lambda p, k=k, stg=stg: p.tensor_scalar(out=stg.h[:],
                             in0=zt[k].h[:], scalar1=cons["lng"].h[:, k:k + 1],
                             scalar2=cons["lnb"].h[:, k:k + 1], op0=ALU.mult, op1=ALU.add),
                             reads=(zt[k], cons["lng"], cons["lnb"]), writes=(stg,))
                        X.op("scalar", lambda p, k=k, t0=t0, stg=stg: p.dma_start(
                             out=d_mamba[k * 128:(k + 1) * 128, t0:t0 + TC], in_=stg.h[:]),
                             reads=(stg,), dma="dS")
            X.barrier()

            ckpt(8)
            ph8 = contextlib.ExitStack()
            with ph8:
                xn1b = [X.sb(f"p8xn{k}", [128, T], BF16, ph8) for k in range(NC)]
                fw1 = [X.sb(f"p8w1{i}", [128, C], BF16, ph8) for i in range(2)]
                fw2 = [X.sb(f"p8w2{i}", [128, C], BF16, ph8) for i in range(2)]
                s1t = X.sb("p8s1", [128, TC], F32, ph8)
                t2t = X.sb("p8t2", [128, TC], F32, ph8)
                hst = [X.sb(f"p8h{i}", [128, T], BF16, ph8) for i in range(2)]
                for k in range(NC):
                    X.op("scalar", lambda p, k=k: p.dma_start(out=xn1b[k].h[:],
                         in_=d_xn1[k * 128:(k + 1) * 128, :]), writes=(xn1b[k],), dma="dL")
                for m in range(NH):
                    w1b = fw1[m % 2]; w2b = fw2[m % 2]
                    load_parcel(f1p[m], w1b, C)
                    load_parcel(f2p[m], w2b, C)
                    hstg = hst[m % 2]
                    for tau in range(NT):
                        t0 = tau * TC
                        bank1 = pool.get(); bank2 = pool.get()
                        def mm1(p, w1b=w1b, t0=t0, bank1=bank1):
                            for kk in range(NC):
                                ins = p.matmul(out=bank1.h[:],
                                               lhsT=w1b.h[:, kk * 128:(kk + 1) * 128],
                                               rhs=xn1b[kk].h[:, t0:t0 + TC],
                                               start=(kk == 0), stop=(kk == NC - 1))
                            return ins
                        X.op("tensor", mm1, reads=[w1b] + xn1b, writes=(bank1,))
                        def mm2(p, w2b=w2b, t0=t0, bank2=bank2):
                            for kk in range(NC):
                                ins = p.matmul(out=bank2.h[:],
                                               lhsT=w2b.h[:, kk * 128:(kk + 1) * 128],
                                               rhs=xn1b[kk].h[:, t0:t0 + TC],
                                               start=(kk == 0), stop=(kk == NC - 1))
                            return ins
                        X.op("tensor", mm2, reads=[w2b] + xn1b, writes=(bank2,))
                        X.op("scalar", lambda p, m=m, bank1=bank1: p.activation(out=s1t.h[:],
                             in_=bank1.h[:], func=AF.Silu, scale=cons["f1s"].h[:, m:m + 1]),
                             reads=(bank1, cons["f1s"]), writes=(s1t,))
                        X.op("vector", lambda p, m=m, bank2=bank2: p.tensor_scalar(out=t2t.h[:],
                             in0=bank2.h[:], scalar1=cons["f2s"].h[:, m:m + 1], scalar2=None,
                             op0=ALU.mult), reads=(bank2, cons["f2s"]), writes=(t2t,))
                        X.op("vector", lambda p, t0=t0, hstg=hstg: p.tensor_tensor(
                             out=hstg.h[:, t0:t0 + TC], in0=s1t.h[:], in1=t2t.h[:],
                             op=ALU.mult), reads=(s1t, t2t), writes=(hstg,))
                    X.op("scalar", lambda p, m=m, hstg=hstg: p.dma_start(
                         out=d_h[m * 128:(m + 1) * 128, :], in_=hstg.h[:]),
                         reads=(hstg,), dma="dS")
            X.barrier()

            ckpt(9)
            def gelu_stage(ph_idx, src_d, parcels, scale_nm, dst_d):
                ph = contextlib.ExitStack()
                with ph:
                    int_ = [X.sb(f"p{ph_idx}i{k}", [128, T], BF16, ph) for k in range(NH)]
                    wt2_ = [X.sb(f"p{ph_idx}w{i}", [128, H], BF16, ph) for i in range(2)]
                    gst = [X.sb(f"p{ph_idx}g{i}", [128, T], BF16, ph) for i in range(2)]
                    for k in range(NH):
                        X.op("scalar", lambda p, k=k: p.dma_start(out=int_[k].h[:],
                             in_=src_d[k * 128:(k + 1) * 128, :]), writes=(int_[k],), dma="dL")
                    for m in range(NH):
                        wt = wt2_[m % 2]
                        load_parcel(parcels[m], wt, H)
                        gs = gst[m % 2]
                        for tau in range(NT):
                            t0 = tau * TC
                            bank = pool.get()
                            def mm(p, wt=wt, t0=t0, bank=bank):
                                for kk in range(NH):
                                    ins = p.matmul(out=bank.h[:],
                                                   lhsT=wt.h[:, kk * 128:(kk + 1) * 128],
                                                   rhs=int_[kk].h[:, t0:t0 + TC],
                                                   start=(kk == 0), stop=(kk == NH - 1))
                                return ins
                            X.op("tensor", mm, reads=[wt] + int_, writes=(bank,))
                            X.op("scalar", lambda p, m=m, t0=t0, bank=bank, gs=gs: p.activation(
                                 out=gs.h[:, t0:t0 + TC], in_=bank.h[:], func=AF.Gelu,
                                 scale=cons[scale_nm].h[:, m:m + 1]),
                                 reads=(bank, cons[scale_nm]), writes=(gs,))
                        X.op("scalar", lambda p, m=m, gs=gs: p.dma_start(
                             out=dst_d[m * 128:(m + 1) * 128, :], in_=gs.h[:]),
                             reads=(gs,), dma="dS")
                X.barrier()
            gelu_stage(9, d_h, h1p, "h1s", d_g1)
            gelu_stage(10, d_g1, h2p, "h2s", d_g2)

            ckpt(11)
            ph11 = contextlib.ExitStack()
            with ph11:
                g2t = [X.sb(f"p11g{k}", [128, T], BF16, ph11) for k in range(NH)]
                w3t = [X.sb(f"p11w{i}", [128, H], BF16, ph11) for i in range(2)]
                ffc = X.sb("p11f", [128, TC], F32, ph11)
                x1c2 = X.sb("p11x", [128, TC], F32, ph11)
                mbc = X.sb("p11m", [128, TC], F32, ph11)
                ost = [X.sb(f"p11o{i}", [128, TC], F32, ph11) for i in range(2)]
                for k in range(NH):
                    X.op("scalar", lambda p, k=k: p.dma_start(out=g2t[k].h[:],
                         in_=d_g2[k * 128:(k + 1) * 128, :]), writes=(g2t[k],), dma="dL")
                for m in range(NC):
                    wt = w3t[m % 2]
                    load_parcel(w3p[m], wt, H)
                    for tau in range(NT):
                        t0 = tau * TC
                        bank = pool.get()
                        def mm(p, wt=wt, t0=t0, bank=bank):
                            for kk in range(NH):
                                ins = p.matmul(out=bank.h[:],
                                               lhsT=wt.h[:, kk * 128:(kk + 1) * 128],
                                               rhs=g2t[kk].h[:, t0:t0 + TC],
                                               start=(kk == 0), stop=(kk == NH - 1))
                            return ins
                        X.op("tensor", mm, reads=[wt] + g2t, writes=(bank,))
                        X.op("vector", lambda p, m=m, bank=bank: p.tensor_scalar(out=ffc.h[:],
                             in0=bank.h[:], scalar1=cons["w3s"].h[:, m:m + 1], scalar2=None,
                             op0=ALU.mult), reads=(bank, cons["w3s"]), writes=(ffc,))
                        X.op("scalar", lambda p, m=m, t0=t0: p.dma_start(out=x1c2.h[:],
                             in_=d_x1[m * 128:(m + 1) * 128, t0:t0 + TC]),
                             writes=(x1c2,), dma="dL")
                        X.op("scalar", lambda p, m=m, t0=t0: p.dma_start(out=mbc.h[:],
                             in_=d_mamba[m * 128:(m + 1) * 128, t0:t0 + TC]),
                             writes=(mbc,), dma="dL")
                        stg = ost[tau % 2]
                        X.op("vector", lambda p, stg=stg: p.tensor_tensor(out=stg.h[:],
                             in0=ffc.h[:], in1=x1c2.h[:], op=ALU.add),
                             reads=(ffc, x1c2), writes=(stg,))
                        X.op("vector", lambda p, stg=stg: p.tensor_tensor(out=stg.h[:],
                             in0=stg.h[:], in1=mbc.h[:], op=ALU.add),
                             reads=(stg, mbc), writes=(stg,))
                        X.op("scalar", lambda p, m=m, t0=t0, stg=stg: p.dma_start(
                             out=yT[m * 128:(m + 1) * 128, t0:t0 + TC], in_=stg.h[:]),
                             reads=(stg,), dma="dS")

        except _StopBuild:
            pass
        X._flush()
        fin = tuple((s, X.cnt[s], False) for s in ("dS", "dL", "dW"))
        def fin_f(p):
            X._wait_emit("scalar", p, fin)
        block.scalar(fin_f)
        def fin_g(p):
            X._wait_emit("gpsimd", p, fin)
        block.gpsimd(fin_g)

    return nc


def _parcel(WT, n_m):
    K, M = WT.shape
    assert M == n_m * 128
    out = np.empty((n_m, 128, K), np.float32)
    for m in range(n_m):
        blk = WT[:, m * 128:(m + 1) * 128]
        out[m] = blk.reshape(K // 128, 128, 128).transpose(1, 0, 2).reshape(128, K)
    return np.ascontiguousarray(out)


def _vimg(v, nch):
    return np.ascontiguousarray(np.asarray(v, np.float32).reshape(nch, 128).T)


_NC_CACHE = {}


def kernel(**inputs):
    f32 = np.float32
    g = {k: np.asarray(v, f32) for k, v in inputs.items()}
    x = g["L0_attn_x"]

    shared = {
        "Wk0": g["L0_attn_W_key"], "Wv0": g["L0_attn_W_value"], "Wr0": g["L0_attn_W_recept"],
        "Wo0p": _parcel(g["L0_attn_W_out"], NC),
        "Wk2p": _parcel(g["L0_attn_W_key2"], NH),
        "Wr2p": _parcel(g["L0_attn_W_recept2"], NC),
        "Wv2p": _parcel(g["L0_attn_W_value2"], NC),
        "w1p": _parcel(np.ascontiguousarray(g["L0_ffn_w1"].T), NH),
        "w2p": _parcel(np.ascontiguousarray(g["L0_ffn_w2"].T), NC),
        "Winp": _parcel(np.ascontiguousarray(g["L1_attn_W_in"].T), 2 * NDI),
        "WxTi": np.ascontiguousarray(
            g["L1_attn_W_x"].T.reshape(NDI, 128, 32).transpose(1, 0, 2).reshape(128, NDI * 32)),
        "Wdtp": _parcel(np.ascontiguousarray(g["L1_attn_W_dt"].T), NDI),
        "Woutp": _parcel(np.ascontiguousarray(g["L1_attn_W_out"].T), NC),
        "f1p": _parcel(np.ascontiguousarray(g["L1_ffn_w1_w"].T), NH),
        "f2p": _parcel(np.ascontiguousarray(g["L1_ffn_w2_w"].T), NH),
        "h1p": _parcel(np.ascontiguousarray(g["L1_ffn_h1_w"].T), NH),
        "h2p": _parcel(np.ascontiguousarray(g["L1_ffn_h2_w"].T), NH),
        "w3p": _parcel(np.ascontiguousarray(g["L1_ffn_w3_w"].T), NC),
        "v_ln1w": _vimg(g["L0_attn_ln1_w"], NC), "v_ln2w": _vimg(g["L0_attn_ln2_w"], NC),
        "v_td": _vimg(g["L0_attn_time_decay"], NC), "v_tf": _vimg(g["L0_attn_time_first"], NC),
        "v_tmk": _vimg(g["L0_attn_time_mix_k"], NC), "v_tmv": _vimg(g["L0_attn_time_mix_v"], NC),
        "v_tmr": _vimg(g["L0_attn_time_mix_r"], NC),
        "v_tmk2": _vimg(g["L0_attn_time_mix_k2"], NC),
        "v_tmr2": _vimg(g["L0_attn_time_mix_r2"], NC),
        "v_b1": _vimg(g["L0_ffn_b1"], NH), "v_b2": _vimg(g["L0_ffn_b2"], NC),
        "v_cb": _vimg(g["L1_attn_conv_b"], NDI), "v_bdt": _vimg(g["L1_attn_b_dt"], NDI),
        "v_dskip": _vimg(g["L1_attn_D_skip"], NDI),
        "v_lng": _vimg(g["L1_attn_ln_g"], NC), "v_lnb": _vimg(g["L1_attn_ln_b"], NC),
        "v_alog": np.ascontiguousarray(
            g["L1_attn_A_log"].reshape(NDI, 128, NST).transpose(1, 0, 2).reshape(128, NDI * NST)),
        "v_cw": np.ascontiguousarray(
            g["L1_attn_conv_w"][:, 0, :].reshape(NDI, 128, 4).transpose(1, 0, 2).reshape(128, NDI * 4)),
        "v_f1s": _vimg(g["L1_ffn_w1_s"], NH), "v_f2s": _vimg(g["L1_ffn_w2_s"], NH),
        "v_h1s": _vimg(g["L1_ffn_h1_s"], NH), "v_h2s": _vimg(g["L1_ffn_h2_s"], NH),
        "v_w3s": _vimg(g["L1_ffn_w3_s"], NC),
        "v_cvec": np.ascontiguousarray(
            np.broadcast_to(np.asarray(CVALS, f32), (128, len(CVALS)))),
        "ones_col": np.ones((128, 1), ml_dtypes.bfloat16),
        "ones_row": np.ones((1, 128), f32),
    }
    in_maps = []
    for i in range(8):
        d = dict(shared)
        d["xT"] = np.ascontiguousarray(x[i].T)
        in_maps.append(d)

    nc = _NC_CACHE.get("nc")
    if nc is None:
        nc = build()
        _NC_CACHE["nc"] = nc
    res = run_bass_kernel_spmd(nc, in_maps, core_ids=list(range(8)))
    out = np.stack([np.asarray(r["yT"], np.float32).T for r in res.results], axis=0)
    return np.ascontiguousarray(out)
